# version 31
# baseline (speedup 1.0000x reference)
import os
import numpy as np
from contextlib import ExitStack

import concourse.bass as bass
import concourse.mybir as mybir
from concourse.bass_utils import run_bass_kernel_spmd

F32 = mybir.dt.float32
BF16 = mybir.dt.bfloat16
I16 = mybir.dt.int16
AF = mybir.ActivationFunctionType
ALU = mybir.AluOpType
AX = mybir.AxisListType

P = 128
D = 1024
DFF = 2816
KC = D // P
DC = DFF // P
HEADS = 16
HD = 64
NPAIR = 8
EPS = 1e-5
SEGS = [(0, 768), (768, 768), (1536, 768), (2304, 512)]

_SIM = os.environ.get("BASS_SIM_FP64") == "1"
CB = 0.5 if _SIM else 0.0
_TRACE = []
_LAST_EM = None


class Emit:

    DMA_SEMS = ("DW", "DX", "DQ", "DO", "DY")

    def __init__(self):
        self.ops = {e: [] for e in ("T", "V", "A", "G", "S")}
        self.cnt = {s: 0 for s in ("T", "V", "A", "G") + self.DMA_SEMS}
        self.vc = {e: {} for e in self.ops}
        self.buf = {}
        self.sched = None

    def op(self, eng, fn, reads=(), writes=(), dma_sem=None, cost=None):
        if self.sched is not None:
            self.sched.append((eng, fn, tuple(reads), tuple(writes), dma_sem, cost))
            return
        self._op(eng, fn, reads, writes, dma_sem)

    def begin_sched(self):
        assert self.sched is None
        if os.environ.get("KSCHED") == "0":
            return
        self.sched = []

    def end_sched(self):
        if self.sched is None:
            return
        ops, self.sched = self.sched, None
        for k in _list_schedule(ops):
            eng, fn, reads, writes, dma_sem, _ = ops[k]
            self._op(eng, fn, reads, writes, dma_sem)

    def _op(self, eng, fn, reads=(), writes=(), dma_sem=None):
        if os.environ.get("KTRACE") == "1":
            _TRACE.append((eng, dma_sem, tuple(reads), tuple(writes)))
        sem = dma_sem if dma_sem else eng
        inc = 16 if dma_sem else 1
        if dma_sem:
            reads = list(reads) + [f"_tok_{sem}"]
            writes = list(writes) + [f"_tok_{sem}"]
        needed = {}
        for b in reads:
            st = self.buf.get(b)
            if st and st["w"]:
                s, c = st["w"]
                needed[s] = max(needed.get(s, 0), c)
        for b in writes:
            st = self.buf.get(b)
            if st:
                if st["w"]:
                    s, c = st["w"]
                    needed[s] = max(needed.get(s, 0), c)
                for s, c in st["r"].items():
                    needed[s] = max(needed.get(s, 0), c)
        waits = []
        vc = self.vc[eng]
        for s, c in sorted(needed.items()):
            if eng == "T" and s == "T":
                continue
            if vc.get(s, 0) < c:
                waits.append((s, c))
                vc[s] = c
        self.cnt[sem] += inc
        cnt = self.cnt[sem]
        self.ops[eng].append((waits, fn, sem, inc))
        for b in reads:
            st = self.buf.setdefault(b, {"w": None, "r": {}})
            st["r"][sem] = max(st["r"].get(sem, 0), cnt)
        for b in writes:
            self.buf[b] = {"w": (sem, cnt), "r": {}}

    def barrier(self):
        assert self.sched is None
        snapshot = dict(self.cnt)
        for e in self.ops:
            vc = self.vc[e]
            waits = []
            for s, c in sorted(snapshot.items()):
                if e == "T" and s == "T":
                    continue
                if c > 0 and vc.get(s, 0) < c:
                    waits.append((s, c))
                    vc[s] = c
            if waits:
                self.ops[e].append((waits, None, None, 0))


_ENG_COST = {"T": 260, "V": 700, "A": 720, "G": 1900, "S": 150}


def _list_schedule(ops):
    n = len(ops)
    busy = [0.0] * n
    lat = [0.0] * n
    for i, (eng, fn, rd, wr, dsem, cost) in enumerate(ops):
        if dsem:
            busy[i] = 150.0
            lat[i] = float(cost) if cost is not None else 4000.0
        else:
            busy[i] = float(cost) if cost is not None else _ENG_COST[eng]
            lat[i] = busy[i] + 100.0
    last_w = {}
    readers = {}
    preds = [None] * n
    succ = [[] for _ in range(n)]
    indeg = [0] * n
    for i, (eng, fn, rd, wr, dsem, cost) in enumerate(ops):
        r = list(rd)
        w = list(wr)
        if dsem:
            tok = "_tok_" + dsem
            r.append(tok)
            w.append(tok)
        dset = set()
        for b in r:
            j = last_w.get(b)
            if j is not None:
                dset.add(j)
        for b in w:
            j = last_w.get(b)
            if j is not None:
                dset.add(j)
            dset.update(readers.get(b, ()))
        dset.discard(i)
        preds[i] = dset
        for j in dset:
            succ[j].append(i)
        indeg[i] = len(dset)
        for b in r:
            readers.setdefault(b, set()).add(i)
        for b in w:
            last_w[b] = i
            readers[b] = set()
    prio = [0.0] * n
    for i in range(n - 1, -1, -1):
        m = 0.0
        for j in succ[i]:
            if prio[j] > m:
                m = prio[j]
        prio[i] = m + lat[i]
    eng_t = {e: 0.0 for e in _ENG_COST}
    avail = [0.0] * n
    ready = [i for i in range(n) if indeg[i] == 0]
    order = []
    while ready:
        best = -1
        best_key = None
        best_est = 0.0
        m = None
        ests = []
        for i in ready:
            est = eng_t[ops[i][0]]
            for j in preds[i]:
                if avail[j] > est:
                    est = avail[j]
            ests.append(est)
            if m is None or est < m:
                m = est
        for i, est in zip(ready, ests):
            if est <= m + 400.0:
                key = (-prio[i], i)
                if best < 0 or key < best_key:
                    best, best_key, best_est = i, key, est
        i = best
        eng_t[ops[i][0]] = best_est + busy[i]
        avail[i] = best_est + lat[i]
        order.append(i)
        ready.remove(i)
        for j in succ[i]:
            indeg[j] -= 1
            if indeg[j] == 0:
                ready.append(j)
    assert len(order) == n
    return order


def build_program(seq_len=2048, n_seq=2, _phases=("prep", "p1", "p2"), _dbg=False):
    S = seq_len * n_seq
    NT = S // P
    ST = seq_len // P
    assert seq_len % P == 0 and ST >= 2
    _ik = "ExternalOutput" if _dbg else "Internal"

    nc = bass.Bass("TRN2", target_bir_lowering=False, debug=False)

    x_d = nc.dram_tensor("x", [S, D], F32, kind="ExternalInput").ap()
    wqT_d = nc.dram_tensor("wqT", [D, D], F32, kind="ExternalInput").ap()
    wkT_d = nc.dram_tensor("wkT", [D, D], F32, kind="ExternalInput").ap()
    wvT_d = nc.dram_tensor("wvT", [D, D], F32, kind="ExternalInput").ap()
    wpT_d = nc.dram_tensor("wpT", [D, D], F32, kind="ExternalInput").ap()
    bp_d = nc.dram_tensor("bp", [1, D], F32, kind="ExternalInput").ap()
    convw_d = nc.dram_tensor("convw", [D, 3], F32, kind="ExternalInput").ap()
    convb_d = nc.dram_tensor("convb", [P, KC], F32, kind="ExternalInput").ap()
    wg_d = nc.dram_tensor("wg", [DFF, D], F32, kind="ExternalInput").ap()
    wu_d = nc.dram_tensor("wu", [DFF, D], F32, kind="ExternalInput").ap()
    wd_d = nc.dram_tensor("wd", [D, DFF], F32, kind="ExternalInput").ap()
    out_d = nc.dram_tensor("out", [S, D], F32, kind="ExternalOutput").ap()
    kv_dram = nc.dram_tensor("kv_scratch", [n_seq, P, NPAIR * P], BF16, kind=_ik).ap()
    xnfm_d = nc.dram_tensor("xnfm_d", [NT, P, D], BF16, kind=_ik).ap()
    xqfm_d = nc.dram_tensor("xqfm_d", [NT, P, D], BF16, kind=_ik).ap()
    qfm_d = nc.dram_tensor("qfm_d", [NT, P, D], BF16, kind=_ik).ap()
    wgq_d = nc.dram_tensor("wgq_d", [P, KC * DFF], BF16, kind=_ik).ap()
    wuq_d = nc.dram_tensor("wuq_d", [P, KC * DFF], BF16, kind=_ik).ap()
    wdq_d = nc.dram_tensor("wdq_d", [P, DC * D], BF16, kind=_ik).ap()
    wgq_v = wgq_d.rearrange("p (k f) -> p k f", f=DFF)
    wuq_v = wuq_d.rearrange("p (k f) -> p k f", f=DFF)
    wdq_v = wdq_d.rearrange("p (k f) -> p k f", f=D)
    if _dbg:
        xn_dbg = nc.dram_tensor("xn_dbg", [NT, P, D], F32, kind="ExternalOutput").ap()
        u1_dbg = nc.dram_tensor("u1_dbg", [NT, P, D], BF16, kind="ExternalOutput").ap()
        x_dbg = nc.dram_tensor("x_dbg", [NT, P, D], F32, kind="ExternalOutput").ap()
        st_dbg = nc.dram_tensor("st_dbg", [P, 2 * NT], F32, kind="ExternalOutput").ap()

    x_t = x_d.rearrange("(t p) d -> t p d", p=P)
    out_t = out_d.rearrange("(t p) d -> t p d", p=P)

    em = Emit()
    global _LAST_EM
    _LAST_EM = em
    E = em.op

    def dma(dst, src, rb, wb, sem="DW", eng="S", cost=None):
        E(eng, lambda e, d_=dst, s_=src: e.dma_start(out=d_, in_=s_), reads=rb, writes=wb,
          dma_sem=sem, cost=cost)

    with ExitStack() as ctx:
        sb = lambda name, shape, dt=F32: ctx.enter_context(nc.sbuf_tensor(name, shape, dt))
        psa = lambda name, shape, dt=F32: ctx.enter_context(nc.psum_tensor(name, shape, dt))

        ident_b = sb("ident_b", [P, P], BF16)
        ones_col = sb("ones_col", [P, 1], BF16)
        ones_row = sb("ones_row", [1, P], BF16)
        bp_b = sb("bp_b", [1, D], BF16)
        convw_b = sb("convw_b", [P, KC, 3], BF16)
        convb_b = sb("convb_b", [P, KC], BF16)
        qv = sb("qv", [P, D], F32)
        u1 = sb("u1", [P, D], BF16)
        u2am = sb("u2am", [P, D], BF16)
        fA = sb("fA", [P, D], F32)
        fB = sb("fB", [P, D], F32)
        bscale = sb("bscale", [P, 112], F32)
        brs = sb("brs", [P, 128], F32)
        psT = psa("psT", [P, KC, P], BF16)
        rden = brs[:, 112:128]
        kvblk = sb("kvblk", [P, NPAIR, P], BF16)
        KSp = sb("KSp", [P, n_seq, NPAIR, 2], BF16)
        wB = sb("wB", [P, KC, D], BF16)

        def quant_ops(src_ap, src_buf, out_ap, out_buf, width, par, scratch=None,
                      sgn_pre=None, g_off=False, force_wide=False):
            nb = width // 16
            wide = force_wide or width > 512
            if scratch is None:
                scratch = ((qv, "qv0", "qv1"), (u1, "u10", "u11"), (u2am, "u20", "u21"))
            (qvt, qn0, qn1), (u1t, un0, un1), (u2t, mn0, mn1) = scratch
            if wide:
                assert width <= 1024 and par == 0
                o = 0
                ob = 0
                qb, ub1, ub2 = [qn0, qn1], [un0, un1], [mn0, mn1]
            else:
                o = par * 512
                ob = par * 32
                qb, ub1, ub2 = [qn0 if par == 0 else qn1], \
                               [un0 if par == 0 else un1], [mn0 if par == 0 else mn1]
            sv = src_ap.rearrange("p (b k) -> p b k", k=16)
            qvp = qvt[:, o:o + width]
            u1p = u1t[:, o:o + width]
            u1i = u1p.bitcast(I16)
            u2p = u2t[:, o:o + width]
            if wide:
                bsc = bscale[:, 64:64 + nb]
                brp = brs[:, 64:64 + nb]
            else:
                bsc = bscale[:, ob:ob + nb]
                brp = brs[:, ob:ob + nb]
            rs_b = brp.unsqueeze(2).broadcast_to([P, nb, 16])
            sc_b = bsc.unsqueeze(2).broadcast_to([P, nb, 16])
            bb, rb = ("bscW", "brsW") if wide else (f"bsc{par}", f"brs{par}")
            ob_ = list(out_buf) if isinstance(out_buf, (tuple, list)) else [out_buf]
            wv = 160 + int(1.12 * width)
            wa = 230 + width
            wg = (160 + int(2.2 * width)) if g_off else wv
            GV = "G" if g_off else "V"
            if _SIM:
                ops = [
                    ("V", lambda e: e.tensor_reduce(bsc, sv, AX.X, ALU.max,
                                                    apply_absolute_value=True),
                     [src_buf, bb], [bb]),
                    ("A", lambda e: e.activation(qvp, src_ap, AF.Abs), [src_buf] + qb, qb),
                    ("V", lambda e: e.tensor_scalar(bsc, bsc, 1.0 / 6.0, 1e-12,
                                                    ALU.mult, ALU.max), [bb], [bb]),
                    ("V", lambda e: e.reciprocal(brp, bsc), [bb, rb], [rb]),
                    (GV,
                     lambda e: e.tensor_tensor(qvp.rearrange("p (b k) -> p b k", k=16),
                                               qvp.rearrange("p (b k) -> p b k", k=16),
                                               rs_b, ALU.mult), qb + [rb], qb),
                    ("V", lambda e: e.tensor_scalar(out_ap, qvp, 5.0, -2.0,
                                                    ALU.is_le, ALU.mult), qb + ob_, ob_),
                    ("A", lambda e: e.activation(u1i, qvp, AF.Copy, bias=CB, scale=2.0),
                     qb + ub1, ub1),
                    ("V", lambda e: e.tensor_scalar(u2p, u1i, 4.0, 0.5, ALU.min, ALU.mult),
                     ub1, ub2),
                    ("A", lambda e: e.activation(u1i, qvp, AF.Copy, bias=CB, scale=1.0),
                     qb + ub1, ub1),
                    ("V", lambda e: e.tensor_scalar(u1p, u1i, 2.0, 4.0, ALU.max, ALU.min),
                     ub1, ub1),
                    (GV, lambda e: e.tensor_add(u2p, u2p, u1p), ub2 + ub1, ub2),
                    (GV, lambda e: e.tensor_add(u2p, u2p, out_ap), ub2 + ob_, ub2),
                ]
                cst = [wv, wa, 300, 400, wg, wv, wa, wv, wa, wv, wg, wg]
                if sgn_pre is None:
                    ops.append(("A", lambda e: e.activation(out_ap, src_ap, AF.Sign),
                                [src_buf] + ob_, ob_))
                    ops.append((GV, lambda e: e.tensor_tensor(u2p, u2p, out_ap, ALU.mult),
                                ub2 + ob_, ub2))
                    cst += [wa, wg]
                else:
                    sga, sgb = sgn_pre
                    ops.append(("V", lambda e: e.tensor_tensor(u2p, u2p, sga, ALU.mult),
                                ub2 + [sgb], ub2))
                    cst += [wv]
            else:
                C2 = float(1.5 * 2 ** 21)
                C3 = float(1.5 * 2 ** 22)
                ops = [
                    ("V", lambda e: e.tensor_reduce(bsc, sv, AX.X, ALU.max,
                                                    apply_absolute_value=True),
                     [src_buf, bb], [bb]),
                    ("V", lambda e: e.tensor_scalar(
                        qvp.bitcast(mybir.dt.int32), src_ap.bitcast(mybir.dt.int32),
                        0x7FFFFFFF, None, ALU.bitwise_and), [src_buf] + qb, qb),
                    ("V", lambda e: e.tensor_scalar(bsc, bsc, 1.0 / 3.0, 2e-12,
                                                    ALU.mult, ALU.max), [bb], [bb]),
                    ("V", lambda e: e.reciprocal(brp, bsc), [bb, rb], [rb]),
                    (GV,
                     lambda e: e.tensor_tensor(qvp.rearrange("p (b k) -> p b k", k=16),
                                               qvp.rearrange("p (b k) -> p b k", k=16),
                                               rs_b, ALU.mult), qb + [rb], qb),
                    ("V", lambda e: e.tensor_scalar(u1p, qvp, C2, C2,
                                                    ALU.add, ALU.subtract), qb + ub1, ub1),
                    ("V", lambda e: e.tensor_scalar(u2p, qvp, C3, C3 + 1.0,
                                                    ALU.add, ALU.subtract), qb + ub2, ub2),
                    ("V", lambda e: e.tensor_scalar(u2p, u2p, 1.0, 0.0,
                                                    ALU.min, ALU.max), ub2, ub2),
                    ("V", lambda e: e.scalar_tensor_tensor(u2p, u1p, 1.0, u2p,
                                                           ALU.min, ALU.add),
                     ub1 + ub2, ub2),
                    ("V", lambda e: e.scalar_tensor_tensor(u2p, qvp, 2.5, u2p,
                                                           ALU.is_gt, ALU.add),
                     qb + ub2, ub2),
                ]
                cst = [wv, wv, 300, 400, wg, wv, wv, wv, wv, wv]
                if sgn_pre is None:
                    ops.append(("A", lambda e: e.activation(out_ap, src_ap, AF.Sign),
                                [src_buf] + ob_, ob_))
                    ops.append((GV, lambda e: e.tensor_tensor(u2p, u2p, out_ap, ALU.mult),
                                ub2 + ob_, ub2))
                    cst += [wa, wg]
                else:
                    sga, sgb = sgn_pre
                    ops.append((GV, lambda e: e.tensor_tensor(u2p, u2p, sga, ALU.mult),
                                ub2 + [sgb], ub2))
                    cst += [wg]
            ops.append((GV, lambda e: e.tensor_tensor(
                out_ap.rearrange("p (b k) -> p b k", k=16),
                u2p.rearrange("p (b k) -> p b k", k=16), sc_b, ALU.mult),
                ub2 + [bb], ob_))
            cst += [wg]
            return [t + (c,) for t, c in zip(ops, cst)]

        def emit_list(ops):
            for t in ops:
                E(t[0], t[1], reads=t[2], writes=t[3],
                  cost=(t[4] if len(t) > 4 else None))

        def emit_zip(la, lb):
            i = j = 0
            while i < len(la) or j < len(lb):
                if i < len(la):
                    t = la[i]
                    E(t[0], t[1], reads=t[2], writes=t[3],
                      cost=(t[4] if len(t) > 4 else None))
                    i += 1
                if j < len(lb):
                    t = lb[j]
                    E(t[0], t[1], reads=t[2], writes=t[3],
                      cost=(t[4] if len(t) > 4 else None))
                    j += 1

        def quant_wide(src_ap, src_buf, out_ap, out_names, width, scratch=None,
                       g_off=False):
            nA, nB = out_names
            if width <= 512:
                emit_list(quant_ops(src_ap, src_buf, out_ap, nA, width, 0, scratch,
                                    g_off=g_off))
                return
            w0 = 512
            w1 = width - 512
            emit_zip(quant_ops(src_ap[:, 0:w0], src_buf, out_ap[:, 0:w0], nA, w0, 0,
                               scratch, g_off=g_off),
                     quant_ops(src_ap[:, w0:width], src_buf, out_ap[:, w0:width],
                               nB, w1, 1, scratch, g_off=g_off))

        def transpose_group(src_ap_fn, nchunks, dst_t, dst_buf, loc_fuse=None,
                            evac="V", dst_off=0, src_bufs=()):
            rsrc = list(src_bufs)
            step = 4 if os.environ.get("KTS", "0") == "1" else KC
            for j in range(0, nchunks, step):
                m = min(step, nchunks - j)
                hf = (j // 4) % 2 if step == 4 else 0
                pb = f"psT{hf}"
                for c in range(m):
                    E("T", lambda e, jj=j + c, cc=4 * hf + c: e.transpose(
                        psT[:, cc, :], src_ap_fn(jj), ident_b[:]),
                      reads=rsrc + ["ident", pb], writes=[pb])
                if loc_fuse is None:
                    if evac == "A":
                        E("A", lambda e, jj=j + dst_off, mm=m, h_=hf: e.activation(
                            dst_t[:, jj:jj + mm, :].rearrange("p c q -> p (c q)"),
                            psT[:, 4 * h_:4 * h_ + mm, :].rearrange("p c q -> p (c q)"),
                            AF.Copy),
                          reads=[pb, dst_buf], writes=[dst_buf], cost=230 + m * P)
                    else:
                        E("V", lambda e, jj=j + dst_off, mm=m, h_=hf: e.tensor_copy(
                            dst_t[:, jj:jj + mm, :].rearrange("p c q -> p (c q)"),
                            psT[:, 4 * h_:4 * h_ + mm, :].rearrange("p c q -> p (c q)")),
                          reads=[pb], writes=[dst_buf], cost=160 + m * P)
                else:
                    loc_t, loc_buf = loc_fuse
                    E("V", lambda e, jj=j, mm=m, h_=hf: e.scalar_tensor_tensor(
                        dst_t[:, jj:jj + mm, :].rearrange("p c q -> p (c q)"),
                        psT[:, 4 * h_:4 * h_ + mm, :].rearrange("p c q -> p (c q)"), 1.0,
                        loc_t[:, jj:jj + mm, :].rearrange("p c q -> p (c q)"),
                        ALU.mult, ALU.add),
                      reads=[pb, loc_buf], writes=[dst_buf], cost=160 + m * P)

        E("G", lambda e: e.memset(ident_b[:], 0.0), writes=["ident"])
        E("G", lambda e: e.affine_select(out=ident_b[:], in_=ident_b[:],
                                         compare_op=ALU.not_equal, fill=1.0, base=0,
                                         pattern=[[-1, P]], channel_multiplier=1),
          reads=["ident"], writes=["ident"])
        E("G", lambda e: e.memset(ones_col[:], 1.0), writes=["ones"])
        E("G", lambda e: e.memset(ones_row[:], 1.0), writes=["ones_r"])

        with ExitStack() as pctx:
            sbp = lambda name, shape, dt: pctx.enter_context(nc.sbuf_tensor(name, shape, dt))
            bp_f32 = sbp("bp_f32", [1, D], F32)
            convw_sb = sbp("convw_sb", [P, KC, 3], F32)
            convb_sb = sbp("convb_sb", [P, KC], F32)

            dma(bp_f32[:], bp_d[:], [], ["bp_f32"])
            E("V", lambda e: e.tensor_copy(bp_b[:], bp_f32[:]), reads=["bp_f32"], writes=["bp_b"])
            dma(convw_sb[:], convw_d.rearrange("(c p) k -> p c k", p=P), [], ["convw"])
            dma(convb_sb[:], convb_d[:], [], ["convb"])
            E("V", lambda e: e.tensor_copy(convw_b[:], convw_sb[:]),
              reads=["convw"], writes=["convw_b"])
            E("V", lambda e: e.tensor_copy(convb_b[:], convb_sb[:]),
              reads=["convb"], writes=["convb_b"])

        em.barrier()

        with ExitStack() as p1ctx:
            sb1 = lambda name, shape, dt: p1ctx.enter_context(nc.sbuf_tensor(name, shape, dt))
            ps1 = lambda name, shape, dt: p1ctx.enter_context(nc.psum_tensor(name, shape, dt))
            wA = sb1("wA", [P, KC, D], BF16)
            wC = sb1("wC", [P, KC, D], BF16)
            stats = sb1("stats", [P, 2 * NT], F32)
            stscr = sb1("stscr", [P, 8 * NT], F32)
            x_sb = sb1("x_sb", [P, 2, D], F32)
            xn_tm = sb1("xn_tm", [P, 2, D], F32)
            u1p1 = sb1("u1p1", [P, 2, D], BF16)
            qe = sb1("qe", [P, 2, D], BF16)
            xq_sb = sb1("xq_sb", [P, 2, D], BF16)
            qs1 = sb1("qs1", [P, 2, D], BF16)
            eluA = sb1("eluA", [P, 2, D], F32)
            eluB = sb1("eluB", [P, 2, D], F32)
            fC = sb1("fC", [P, D], F32)
            fD = sb1("fD", [P, D], F32)
            xnfm_s = sb1("xnfm_s", [P, 2, KC, P], BF16)
            xqfm_s = sb1("xqfm_s", [P, 2, KC, P], BF16)
            qfm_s = sb1("qfm_s", [P, 2, KC, P], BF16)
            k_tm = sb1("k_tm", [P, 2, D], BF16)
            v_tm = sb1("v_tm", [P, 2, D], BF16)
            ks_sb = sb1("ks_sb", [P, NPAIR], F32)
            wstage = sb1("wstage", [P, 2, D], F32)
            wqchunk = sb1("wqchunk", [P, 2, D], BF16)
            wstg_out = sb1("wstg_out", [P, 2, KC, P], BF16)
            wload = sb1("wload", [P, 2, D], F32)
            psK = ps1("psK", [P, D], F32)
            psV = ps1("psV", [P, D], F32)
            psKV = ps1("psKV", [P, NPAIR, P], F32)
            psKS = ps1("psKS", [P, NPAIR], F32)

            wjobs = [(wkT_d, wA, "wA"), (wvT_d, wB, "wB"), (wqT_d, wC, "wC")]
            for jx, (w_ap, dst, dbuf) in enumerate(wjobs):
                for c in range(KC):
                    sl = (jx * KC + c) % 2
                    sem, eng = ("DW", "S") if sl == 0 else ("DY", "G")
                    dma(wload[:, sl, :], w_ap[c * P:(c + 1) * P, :],
                        [f"wload{sl}"], [f"wload{sl}"], sem=sem, eng=eng, cost=3000)
                    E("V", lambda e, cc=c, dd=dst, s_=sl: e.tensor_copy(
                        dd[:, cc, :], wload[:, s_, :]),
                      reads=[f"wload{sl}"], writes=[dbuf], cost=1000)

            def prep_task(w_dram, c, s0, sw, dst_v, ti):
                pr = ti % 2
                wst = wstage[:, pr, 0:sw]
                wqc = wqchunk[:, pr, :]
                dma(wst, w_dram[c * P:(c + 1) * P, s0:s0 + sw],
                    [f"wstage{pr}"], [f"wstage{pr}"], cost=5000)
                quant_wide(wst, f"wstage{pr}", wqc[:, 0:sw],
                           (f"wqcA{pr}", f"wqcB{pr}"), sw, g_off=True)
                g0 = s0 // P
                for j in range(0, sw // P, KC):
                    m = min(KC, sw // P - j)
                    transpose_group(
                        lambda jj, p_=pr, b_=j: wqchunk[:, p_, (b_ + jj) * P:(b_ + jj + 1) * P],
                        m, wstg_out[:, pr], f"wstg{pr}", evac="A",
                        src_bufs=(f"wqcA{pr}", f"wqcB{pr}"))
                    dma(dst_v[:, g0 + j:g0 + j + m, c * P:(c + 1) * P],
                        wstg_out[:, pr, 0:m, :],
                        [f"wstg{pr}"], [f"wstg{pr}", "wq_dram"], sem="DY", eng="G",
                        cost=3000)

            prep_tasks = []
            if "prep" in _phases:
                ti = 0
                for (w_dram, nrc, width, dst_v) in (
                        (wg_d, DC, D, wgq_v), (wu_d, DC, D, wuq_v),
                        (wd_d, KC, DFF, wdq_v)):
                    for c in range(nrc):
                        for s0 in range(0, width, 1024):
                            sw = min(1024, width - s0)
                            prep_tasks.append(
                                lambda w_=w_dram, c_=c, s_=s0, sw_=sw, d_=dst_v, t_=ti:
                                prep_task(w_, c_, s_, sw_, d_, t_))
                            ti += 1
            ntask = len(prep_tasks)
            tix = 0

            def drain_prep(n):
                nonlocal tix
                for _ in range(n):
                    if tix < ntask:
                        prep_tasks[tix]()
                        tix += 1

            nt1 = NT if "p1" in _phases else 0
            for i in range(nt1):
                seq = i // ST
                first = (i % ST == 0)
                last = (i % ST == ST - 1)
                if i % 8 == 0:
                    em.begin_sched()
                sp = i % 2
                stt = stscr[:, 8 * i:8 * i + 8]
                nmS = lambda s, sp_=sp: s + str(sp_)
                xv = x_sb[:, sp, :]
                xnv = xn_tm[:, sp, :]
                xbuf = nmS("x")
                xnbuf = nmS("xn")
                ksink = kvblk[:].rearrange("p a b -> p (a b)")
                dma(xv, x_t[i], [xbuf], [xbuf], sem="DX", eng="A", cost=5000)
                E("A", lambda e, st_=stt, x_=xv: e.activation(ksink, x_, AF.Square,
                                                       accum_out=st_[:, 0:1]),
                  reads=[xbuf, "kvblk", "brs"], writes=["kvblk", nmS("st_sq")], cost=1400)
                E("A", lambda e, st_=stt, x_=xv: e.activation(ksink, x_, AF.Copy,
                                                       accum_out=st_[:, 1:2]),
                  reads=[xbuf, "kvblk"], writes=["kvblk", nmS("st_sum")], cost=1400)
                E("V", lambda e, st_=stt: e.tensor_scalar(st_[:, 2:3], st_[:, 1:2],
                                                          1.0 / D, None, ALU.mult),
                  reads=[nmS("st_sum")], writes=[nmS("st_m")], cost=250)
                E("V", lambda e, st_=stt: e.tensor_tensor(st_[:, 3:4], st_[:, 2:3],
                                                          st_[:, 2:3], ALU.mult),
                  reads=[nmS("st_m")], writes=[nmS("st_m2")], cost=250)
                E("V", lambda e, st_=stt: e.scalar_tensor_tensor(
                    st_[:, 4:5], st_[:, 0:1], 1.0 / D, st_[:, 3:4],
                    ALU.mult, ALU.subtract),
                  reads=[nmS("st_sq"), nmS("st_m2")], writes=[nmS("st_u")], cost=250)
                E("V", lambda e, st_=stt: e.tensor_scalar(st_[:, 4:5], st_[:, 4:5], EPS,
                                                          None, ALU.add),
                  reads=[nmS("st_u")], writes=[nmS("st_u")], cost=250)
                E("A", lambda e, st_=stt: e.activation(st_[:, 5:6], st_[:, 4:5],
                                                       AF.Sqrt),
                  reads=[nmS("st_u")], writes=[nmS("st_sd")], cost=260)
                E("V", lambda e, ii=i, st_=stt: e.reciprocal(
                    stats[:, 2 * ii:2 * ii + 1], st_[:, 5:6]),
                  reads=[nmS("st_sd"), "stats"], writes=["stats"], cost=250)
                E("V", lambda e, ii=i, st_=stt: e.scalar_tensor_tensor(
                    stats[:, 2 * ii + 1:2 * ii + 2], st_[:, 2:3], -1.0,
                    stats[:, 2 * ii:2 * ii + 1], ALU.mult, ALU.mult),
                  reads=[nmS("st_m"), "stats"], writes=["stats"], cost=250)
                E("A", lambda e, ii=i, xn_=xnv, x_=xv: e.activation(
                    xn_, x_, AF.Identity,
                    bias=stats[:, 2 * ii + 1:2 * ii + 2], scale=stats[:, 2 * ii:2 * ii + 1]),
                  reads=[xbuf, "stats", xnbuf], writes=[xnbuf], cost=1400)
                u1v = u1p1[:, sp, :]
                u1b = nmS("u1p1")
                xnfmv = xnfm_s[:, sp, :, :]
                xnfmb = nmS("xnfm_s")
                ktv = k_tm[:, sp, :]
                ktb = nmS("k_tm")
                vtv = v_tm[:, sp, :]
                vtb = nmS("v_tm")
                qev = qe[:, sp, :]
                qeb = nmS("qe")
                qfmv = qfm_s[:, sp, :, :]
                qfmb = nmS("qfm_s")
                xqv_ = xq_sb[:, sp, :]
                xqfmv = xqfm_s[:, sp, :, :]
                xqfmb = nmS("xqfm_s")
                eAv = eluA[:, sp, :]
                eAb = nmS("eluA")
                eBv = eluB[:, sp, :]
                eBb = nmS("eluB")
                qsc = ((fA, "fA", "fA"), (qs1[:, 0, :], "qs1A0", "qs1B0"),
                       (fB, "fB", "fB")) if sp == 0 else                       ((fC, "fC", "fC"), (qs1[:, 1, :], "qs1A1", "qs1B1"),
                       (fD, "fD", "fD"))
                E("V", lambda e, xn_=xnv, u_=u1v, ub=u1b: e.tensor_copy(u_, xn_),
                  reads=[xnbuf, u1b], writes=[u1b], cost=1000)
                if _dbg:
                    dma(xn_dbg[i], xnv, [xnbuf], ["xdbg"], sem="DO", eng="S")
                    dma(u1_dbg[i], u1v, [u1b], ["udbg"], sem="DO", eng="S")
                    dma(x_dbg[i], xv, [xbuf], ["xdbg2"], sem="DO", eng="S")
                transpose_group(lambda j, u_=u1v: u_[:, j * P:(j + 1) * P], KC, xnfmv,
                                xnfmb, evac="A", src_bufs=(u1b,))
                dma(xnfm_d[i], xnfmv.rearrange("p c q -> p (c q)"),
                    [xnfmb], ["xnfm_dram"], sem="DY", eng="G", cost=3000)
                for kc in range(KC):
                    for (ps, wsrc, wbuf, pbuf) in ((psK, wA, "wA", "psK"),
                                                   (psV, wB, "wB", "psV")):
                        for oc in range(2):
                            E("T", lambda e, k_=kc, o_=oc, p_=ps, w_=wsrc, x_=xnfmv:
                                e.matmul(
                                    p_[:, o_ * 512:(o_ + 1) * 512], x_[:, k_, :],
                                    w_[:, k_, o_ * 512:(o_ + 1) * 512],
                                    start=(k_ == 0), stop=(k_ == KC - 1)),
                              reads=[xnfmb, wbuf, pbuf], writes=[pbuf])
                E("V", lambda e, v_=vtv: e.tensor_copy(v_, psV[:]),
                  reads=["psV", vtb], writes=[vtb], cost=1100)
                E("A", lambda e, a_=eAv: e.activation(a_, psK[:], AF.Relu, scale=-1.0),
                  reads=["psK", eAb], writes=[eAb], cost=1400)
                E("A", lambda e, a_=eAv, b_=eBv: e.activation(b_, a_, AF.Exp, scale=-1.0),
                  reads=[eAb, eBb], writes=[eBb], cost=1400)
                E("V", lambda e, k_=ktv, b_=eBv: e.scalar_tensor_tensor(
                    k_, psK[:], 1.0, b_, ALU.add, ALU.max),
                  reads=["psK", eBb, ktb], writes=[ktb], cost=1300)
                for oc in range(KC):
                    for kc in range(KC):
                        E("T", lambda e, k_=kc, o_=oc, x_=xnfmv: e.matmul(
                            psV[:, o_ * P:(o_ + 1) * P],
                            wC[:, k_, o_ * P:(o_ + 1) * P], x_[:, k_, :],
                            start=(k_ == 0), stop=(k_ == KC - 1)),
                          reads=[xnfmb, "wC", "psV"], writes=["psV"])
                E("A", lambda e, a_=eAv: e.activation(a_, psV[:], AF.Relu, scale=-1.0),
                  reads=["psV", eAb], writes=[eAb], cost=1400)
                E("A", lambda e, a_=eAv, b_=eBv: e.activation(b_, a_, AF.Exp, scale=-1.0),
                  reads=[eAb, eBb], writes=[eBb], cost=1400)
                E("V", lambda e, q_=qfmv, b_=eBv: e.scalar_tensor_tensor(
                    q_.rearrange("p c q -> p (c q)"), psV[:], 1.0, b_,
                    ALU.add, ALU.max),
                  reads=["psV", eBb, qfmb], writes=[qfmb], cost=1300)
                for pair in range(NPAIR):
                    E("T", lambda e, p_=pair, f_=first, l_=last, k_=ktv, v_=vtv:
                        e.matmul(
                            psKV[:, p_, :], k_[:, p_ * P:(p_ + 1) * P],
                            v_[:, p_ * P:(p_ + 1) * P],
                            start=(f_ and p_ in (0, 4)), stop=(l_ and p_ in (3, 7))),
                      reads=[ktb, vtb, "psKV"], writes=["psKV"])
                    E("T", lambda e, p_=pair, f_=first, l_=last, k_=ktv: e.matmul(
                        psKS[:, p_:p_ + 1], k_[:, p_ * P:(p_ + 1) * P], ones_col[:],
                        start=(f_ and p_ == 0), stop=(l_ and p_ == NPAIR - 1)),
                      reads=[ktb, "ones", "psKS"], writes=["psKS"])
                dma(qfm_d[i], qfmv.rearrange("p c q -> p (c q)"),
                    [qfmb], ["qfm_dram"], sem="DY", eng="G", cost=3000)
                quant_wide(xnv, xnbuf, xqv_, (nmS("xqA"), nmS("xqB")), D,
                           scratch=qsc, g_off=True)
                transpose_group(lambda j, x_=xqv_: x_[:, j * P:(j + 1) * P], KC, xqfmv,
                                xqfmb, evac="A", src_bufs=(nmS("xqA"), nmS("xqB")))
                dma(xqfm_d[i], xqfmv.rearrange("p c q -> p (c q)"),
                    [xqfmb], ["xqfm_dram"], sem="DY", eng="G", cost=3000)
                if last:
                    E("A", lambda e: e.activation(ks_sb[:], psKS[:], AF.Copy),
                      reads=["psKS", "ks_sb"], writes=["ks_sb"])
                    for pair in range(NPAIR):
                        E("A", lambda e, p_=pair: e.activation(
                            kvblk[:, p_, :], psKV[:, p_, :], AF.Copy),
                          reads=["psKV", "kvblk"], writes=["kvblk"])
                        E("V", lambda e, p_=pair: e.memset(kvblk[0:64, p_, 64:128], 0.0),
                          reads=["kvblk"], writes=["kvblk"])
                        E("V", lambda e, p_=pair: e.memset(kvblk[64:128, p_, 0:64], 0.0),
                          reads=["kvblk"], writes=["kvblk"])
                    dma(kv_dram[seq], kvblk[:].rearrange("p a b -> p (a b)"),
                        ["kvblk"], ["kv_dram"], sem="DY", eng="G")
                    E("V", lambda e, s_=seq: e.tensor_copy(KSp[:, s_, :, 0], ks_sb[:]),
                      reads=["ks_sb", "KSp"], writes=["KSp"])
                    E("V", lambda e, s_=seq: e.tensor_copy(KSp[:, s_, :, 1], ks_sb[:]),
                      reads=["ks_sb", "KSp"], writes=["KSp"])
                    E("V", lambda e, s_=seq: e.memset(KSp[64:128, s_, :, 0], 0.0),
                      reads=["KSp"], writes=["KSp"])
                    E("V", lambda e, s_=seq: e.memset(KSp[0:64, s_, :, 1], 0.0),
                      reads=["KSp"], writes=["KSp"])
                want = ((i + 1) * ntask) // max(nt1, 1)
                drain_prep(max(0, want - tix))
                if i % 8 == 7 or i == nt1 - 1:
                    em.end_sched()
            em.begin_sched()
            drain_prep(ntask)
            em.end_sched()
            if _dbg and nt1:
                dma(st_dbg[:], stats[:], ["stats"], ["sdbg"], sem="DO", eng="S")

        em.barrier()

        wgT_q = sb("wgT_q", [P, KC, DFF], BF16)
        wuT_q = sb("wuT_q", [P, KC, DFF], BF16)
        wdT_q = sb("wdT_q", [P, DC, D], BF16)
        xnr = sb("xnr", [P, 3, KC, P], BF16)
        xqr = sb("xqr", [P, 3, KC, P], BF16)
        qfr = sb("qfr", [P, 2, KC, P], BF16)
        edge = sb("edge", [P, 3, KC], BF16)
        fcol = sb("fcol", [P, 3, KC], BF16)
        loc = sb("loc", [P, 2, KC, P], BF16)
        xam = sb("xam", [P, KC, P], BF16)
        g_sb = sb("g_sb", [P, 2, D], BF16)
        xhq = sb("xhq", [P, 2, 768], BF16)
        u3 = sb("u3", [P, 2, 768], BF16)
        hB = sb("hB", [P, 768], F32)
        hqloc = sb("hqloc", [P, 6, P], BF16)
        psAD = psa("psAD", [P, 2 * P + HEADS], F32)
        psG1 = psa("psG1", [P, 512], F32)
        psG2 = psa("psG2", [P, 512], F32)
        psQ = psa("psQ", [P, P], F32)
        psG3 = psa("psG3", [P, 512], F32)
        psDown = psa("psDown", [P, D], F32)
        psA_v = psAD[:, 0:P]
        psDn_v = psAD[:, P:P + HEADS]
        psQ_v = psQ[:, :]
        psGs = [(psG1, "psG1"), (psG2, "psG2"), (psG3, "psG3")]
        gctr = [0]

        def p2_reloads():
            for si, (s0, sw) in enumerate(SEGS):
                dma(wgT_q[:, :, s0:s0 + sw], wgq_v[:, :, s0:s0 + sw],
                    ["wq_dram"], [f"wgTs{si}"], sem="DW", eng="S", cost=12000)
            for si, (s0, sw) in enumerate(SEGS):
                dma(wuT_q[:, :, s0:s0 + sw], wuq_v[:, :, s0:s0 + sw],
                    ["wq_dram"], [f"wuTs{si}"], sem="DY", eng="G", cost=12000)
            dma(wdT_q[:, :, 0:512], wdq_v[:, :, 0:512],
                ["wq_dram"], ["wdTh0"], sem="DW", eng="S", cost=16000)
            dma(wdT_q[:, :, 512:1024], wdq_v[:, :, 512:1024],
                ["wq_dram"], ["wdTh1"], sem="DY", eng="G", cost=16000)
            for c in range(KC):
                for h in range(2):
                    dma(qv[:, 0:512], wpT_d[c * P:(c + 1) * P, h * 512:(h + 1) * 512],
                        ["qv0"], ["qv0"], sem="DX", eng="A", cost=2000)
                    E("A", lambda e, cc=c, h_=h: e.activation(
                        wB[:, cc, h_ * 512:(h_ + 1) * 512], qv[:, 0:512], AF.Copy),
                      reads=["qv0", "wB"], writes=["wB"], cost=700)

        def p2_prefetch(j):
            r3 = j % 3
            dma(xnr[:, r3, :, :].rearrange("p c q -> p (c q)"), xnfm_d[j],
                ["xnfm_dram", f"xnr{r3}"], [f"xnr{r3}"],
                sem="DX", eng="A", cost=2500)
            E("V", lambda e, r_=r3: e.tensor_copy(edge[:, r_, :], xnr[:, r_, :, P - 1]),
              reads=[f"xnr{r3}"], writes=[f"edge{r3}"], cost=200)
            E("V", lambda e, r_=r3: e.tensor_copy(fcol[:, r_, :], xnr[:, r_, :, 0]),
              reads=[f"xnr{r3}"], writes=[f"fcol{r3}"], cost=200)
            dma(xqr[:, r3, :, :].rearrange("p c q -> p (c q)"), xqfm_d[j],
                ["xqfm_dram", f"xqr{r3}"], [f"xqr{r3}"], sem="DQ", eng="S", cost=2500)
            r2q = j % 2
            dma(qfr[:, r2q, :, :].rearrange("p c q -> p (c q)"), qfm_d[j],
                ["qfm_dram", f"qfr{r2q}"], [f"qfr{r2q}"], sem="DQ", eng="S", cost=2500)

        def p2_tile(j):
            seq = j // ST
            first = (j % ST == 0)
            last = (j % ST == ST - 1)
            r3 = j % 3
            sp = j % 2
            xqv = xqr[:, r3, :, :]
            xqb = f"xqr{r3}"
            qfv = qfr[:, sp, :, :]
            qfb = f"qfr{sp}"
            locv = loc[:, sp, :, :]
            locb = f"loc{sp}"
            xamv = xam[:, :, :]
            xamb = "xam"
            gv = g_sb[:, sp, :]
            gb = f"g_sb{sp}"
            if first:
                dma(kvblk[:].rearrange("p a b -> p (a b)"), kv_dram[seq],
                    ["kv_dram", "kvblk"], ["kvblk"], sem="DQ", eng="S", cost=3000)
            xv = xnr[:, r3, :, :]
            xb = f"xnr{r3}"
            w0b = convw_b[:, :, 0].unsqueeze(2).broadcast_to([P, KC, P])
            w1b = convw_b[:, :, 1].unsqueeze(2).broadcast_to([P, KC, P])
            w2b = convw_b[:, :, 2].unsqueeze(2).broadcast_to([P, KC, P])
            cbb = convb_b[:, :].unsqueeze(2).broadcast_to([P, KC, P])
            s1v = fA[:, 0:D].rearrange("p (c q) -> p c q", q=P)
            s2v = fB[:, 0:D].rearrange("p (c q) -> p c q", q=P)
            E("G", lambda e: e.tensor_tensor(s1v, xv, w1b, ALU.mult),
              reads=[xb, "convw_b", "fA"], writes=["fA"], cost=2400)
            E("G", lambda e: e.tensor_tensor(s2v[:, :, 1:P], xv[:, :, 0:P - 1],
                                             w0b[:, :, 1:P], ALU.mult),
              reads=[xb, "convw_b", "fB"], writes=["fB"], cost=2400)
            E("G", lambda e: e.tensor_tensor(s1v, s1v, cbb, ALU.add),
              reads=["fA", "convb_b"], writes=["fA"], cost=2400)
            if first:
                E("G", lambda e: e.memset(s2v[:, :, 0:1], 0.0),
                  reads=["fB"], writes=["fB"], cost=300)
            else:
                lsl = (j - 1) % 3
                E("G", lambda e, l_=lsl: e.tensor_tensor(
                    s2v[:, :, 0:1], edge[:, l_, :].unsqueeze(2),
                    convw_b[:, :, 0].unsqueeze(2), ALU.mult),
                  reads=[f"edge{lsl}", "convw_b", "fB"], writes=["fB"], cost=300)
            E("G", lambda e: e.tensor_tensor(fA[:, 0:D], fA[:, 0:D], fB[:, 0:D], ALU.add),
              reads=["fA", "fB"], writes=["fA"], cost=2200)
            E("G", lambda e: e.tensor_tensor(s2v[:, :, 0:P - 1], xv[:, :, 1:P],
                                             w2b[:, :, 0:P - 1], ALU.mult),
              reads=[xb, "convw_b", "fB"], writes=["fB"], cost=2400)
            if last:
                E("G", lambda e: e.memset(s2v[:, :, P - 1:P], 0.0),
                  reads=["fB"], writes=["fB"], cost=300)
            else:
                rsl = (j + 1) % 3
                E("G", lambda e, r_=rsl: e.tensor_tensor(
                    s2v[:, :, P - 1:P], fcol[:, r_, :].unsqueeze(2),
                    convw_b[:, :, 2].unsqueeze(2), ALU.mult),
                  reads=[f"fcol{rsl}", "convw_b", "fB"], writes=["fB"], cost=300)
            E("G", lambda e, lv=locv, lb=locb: e.tensor_tensor(lv, s1v, s2v, ALU.add),
              reads=["fA", "fB", locb], writes=[locb], cost=2200)
            if j + 2 < NT:
                p2_prefetch(j + 2)

            def seg_list(s):
                s0, sw = SEGS[s]
                ops = []
                for b0 in range(0, sw, 512):
                    bw = min(512, sw - b0)
                    pg, ng = psGs[gctr[0] % 3]
                    pu, nu = psGs[(gctr[0] + 1) % 3]
                    gctr[0] += 2
                    for kc in range(KC):
                        ops.append(("T", (lambda e, k_=kc, a_=s0 + b0, w_=bw, p_=pg:
                            e.matmul(
                                p_[:, 0:w_], xqv[:, k_, :], wgT_q[:, k_, a_:a_ + w_],
                                start=(k_ == 0), stop=(k_ == KC - 1))),
                            [xqb, f"wgTs{s}", ng], [ng]))
                        ops.append(("T", (lambda e, k_=kc, a_=s0 + b0, w_=bw, p_=pu:
                            e.matmul(
                                p_[:, 0:w_], xqv[:, k_, :], wuT_q[:, k_, a_:a_ + w_],
                                start=(k_ == 0), stop=(k_ == KC - 1))),
                            [xqb, f"wuTs{s}", nu], [nu]))
                    hv, hb = (fB, "fB") if s % 2 == 0 else (hB, "hB")
                    u3v = u3[:, s % 2, :]
                    u3b = f"u3{s % 2}"
                    ops += [
                        ("A", (lambda e, w_=bw, p_=pg: e.activation(
                            fA[:, 0:w_], p_[:, 0:w_], AF.Tanh, scale=0.5)),
                         [ng, "fA"], ["fA"]),
                        ("V", (lambda e, w_=bw, p_=pg: e.scalar_tensor_tensor(
                            fA[:, 0:w_], fA[:, 0:w_], 1.0, p_[:, 0:w_],
                            ALU.add, ALU.mult)), ["fA", ng], ["fA"]),
                        ("V", (lambda e, b_=b0, w_=bw, p_=pu, h_=hv: e.scalar_tensor_tensor(
                            h_[:, b_:b_ + w_], fA[:, 0:w_], 0.5, p_[:, 0:w_],
                            ALU.mult, ALU.mult)), ["fA", nu, hb], [hb]),
                        ("A", (lambda e, b_=b0, w_=bw, h_=hv, u_=u3v: e.activation(
                            u_[:, b_:b_ + w_], h_[:, b_:b_ + w_], AF.Sign)),
                         [hb, u3b], [u3b]),
                    ]
                return ops

            def qnt_list(s):
                s0, sw = SEGS[s]
                hv, hb = (fB, "fB") if s % 2 == 0 else (hB, "hB")
                return quant_ops(hv[:, 0:sw], hb, xhq[:, s % 2, 0:sw],
                                 (f"hqA{s % 2}", f"hqB{s % 2}"),
                                 sw, 0, sgn_pre=(u3[:, s % 2, 0:sw], f"u3{s % 2}"),
                                 force_wide=True, g_off=True)

            def tdown(s):
                s0, sw = SEGS[s]
                nch = sw // P
                sp2 = s % 2
                hql = hqloc
                hqb = "hqloc"
                transpose_group(lambda jj, s_=sp2: xhq[:, s_, jj * P:(jj + 1) * P], nch,
                                hql, hqb, evac="A",
                                src_bufs=(f"hqA{sp2}", f"hqB{sp2}"))
                for lc in range(nch):
                    gk = s0 // P + lc
                    for half in range(2):
                        E("T", lambda e, l_=lc, g_=gk, h_=half, q_=hql: e.matmul(
                            psDown[:, h_ * 512:(h_ + 1) * 512], q_[:, l_, :],
                            wdT_q[:, g_, h_ * 512:(h_ + 1) * 512],
                            start=(g_ == 0), stop=(g_ == DC - 1)),
                          reads=[hqb, f"wdTh{half}", f"psDn{half}"],
                          writes=[f"psDn{half}"])

            nseg = len(SEGS)
            pend = None
            for s in range(nseg):
                if s == 0:
                    emit_list(seg_list(0))
                else:
                    emit_zip(pend, seg_list(s))
                    tdown(s - 1)
                pend = qnt_list(s)
            emit_list(pend)
            tdown(nseg - 1)
            E("A", lambda e: e.activation(fA[:, 0:D], psDown[:], AF.Tanh, scale=0.5),
              reads=["psDn0", "psDn1", "fA"], writes=["fA"], cost=1400)
            E("V", lambda e: e.scalar_tensor_tensor(fA[:, 0:D], fA[:, 0:D], 1.0,
                                                    psDown[:], ALU.add, ALU.mult),
              reads=["fA", "psDn0", "psDn1"], writes=["fA"], cost=1300)
            E("A", lambda e, g_=gv, gb_=gb: e.activation(g_, fA[:, 0:D], AF.Copy, scale=0.5),
              reads=["fA", gb], writes=[gb], cost=1400)

            dma(qv[:], x_t[j], ["qv0", "qv1"], ["qv0", "qv1"], sem="DX", eng="A",
                cost=5000)

            for pair in range(NPAIR):
                E("T", lambda e, p_=pair, q_=seq: e.matmul(
                    psDn_v[:, 2 * p_:2 * p_ + 2], qfv[:, p_, :], KSp[:, q_, p_, :],
                    start=True, stop=True),
                  reads=[qfb, "KSp", "psAD"], writes=["psAD"])
            E("V", lambda e: e.tensor_scalar(rden[:, :], psDn_v, 1e-6, None, ALU.max),
              reads=["psAD", "rden"], writes=["rden"], cost=250)
            E("V", lambda e: e.reciprocal(rden[:, :], rden[:, :]),
              reads=["rden"], writes=["rden"], cost=300)
            for pair in range(NPAIR):
                pv = psA_v if pair % 2 == 0 else psQ_v
                pb = "psAD" if pair % 2 == 0 else "psQ"
                E("T", lambda e, p_=pair, v_=pv, b_=pb: e.matmul(
                    v_, qfv[:, p_, :], kvblk[:, p_, :], start=True, stop=True),
                  reads=[qfb, "kvblk", pb], writes=[pb])
                rb = rden[:, 2 * pair:2 * pair + 2].unsqueeze(2).broadcast_to([P, 2, HD])
                E("V", lambda e, p_=pair, rb_=rb, v_=pv, b_=pb: e.tensor_tensor(
                    u2am[:, p_ * P:(p_ + 1) * P].rearrange("p (h q) -> p h q", q=HD),
                    v_.rearrange("p (h q) -> p h q", q=HD), rb_, ALU.mult),
                  reads=[pb, "rden", "u20", "u21"], writes=["u20", "u21"], cost=400)
            transpose_group(lambda jj: u2am[:, jj * P:(jj + 1) * P], KC, xamv,
                            xamb, loc_fuse=(locv, locb), src_bufs=("u20", "u21"))
            for half in range(2):
                for kc in range(KC):
                    E("T", lambda e, k_=kc, h_=half: e.matmul(
                        psDown[:, h_ * 512:(h_ + 1) * 512], xamv[:, k_, :],
                        wB[:, k_, h_ * 512:(h_ + 1) * 512], start=(k_ == 0), stop=False),
                      reads=[xamb, "wB", f"psDn{half}"], writes=[f"psDn{half}"])
                E("T", lambda e, h_=half: e.matmul(
                    psDown[:, h_ * 512:(h_ + 1) * 512], ones_row[:, :],
                    bp_b[:, h_ * 512:(h_ + 1) * 512], start=False, stop=True),
                  reads=["ones_r", "bp_b", f"psDn{half}"], writes=[f"psDn{half}"])
                E("V", lambda e, h_=half: e.tensor_tensor(
                    u2am[:, h_ * 512:(h_ + 1) * 512], psDown[:, h_ * 512:(h_ + 1) * 512],
                    gv[:, h_ * 512:(h_ + 1) * 512], ALU.mult),
                  reads=[f"psDn{half}", gb, "u20" if half == 0 else "u21"],
                  writes=["u20" if half == 0 else "u21"])
                E("V", lambda e, h_=half: e.tensor_tensor(
                    qv[:, h_ * 512:(h_ + 1) * 512], qv[:, h_ * 512:(h_ + 1) * 512],
                    u2am[:, h_ * 512:(h_ + 1) * 512], ALU.add),
                  reads=[f"qv{half}", "u20" if half == 0 else "u21"],
                  writes=[f"qv{half}"])
                dma(out_t[j][:, half * 512:(half + 1) * 512], qv[:, half * 512:(half + 1) * 512],
                    [f"qv{half}"], ["out_dma"], sem="DO")

        if "p2" in _phases:
            em.begin_sched()
            p2_reloads()
            for jj in range(min(2, NT)):
                p2_prefetch(jj)
            em.end_sched()
            for j in range(NT):
                if j % 4 == 0:
                    em.begin_sched()
                p2_tile(j)
                if j % 4 == 3 or j == NT - 1:
                    em.end_sched()
        else:
            for j in range(NT):
                dma(out_t[j], qv[:], ["qv0", "qv1"], ["out_dma"], sem="DO")

        with nc.semaphore("sT") as sT, nc.semaphore("sV") as sV, nc.semaphore("sA") as sA, \
             nc.semaphore("sG") as sG, nc.semaphore("sDW") as sDW, nc.semaphore("sDX") as sDX, \
             nc.semaphore("sDQ") as sDQ, nc.semaphore("sDO") as sDO, \
             nc.semaphore("sDY") as sDY, nc.Block() as blk:
            sems = {"T": sT, "V": sV, "A": sA, "G": sG, "DW": sDW, "DX": sDX,
                    "DQ": sDQ, "DO": sDO, "DY": sDY}

            def runner(ename):
                def body(eng):
                    for waits, fn, sem, inc in em.ops[ename]:
                        for s, c in waits:
                            eng.wait_ge(sems[s], c)
                        if fn is not None:
                            fn(eng).then_inc(sems[sem], inc)
                    if ename == "S":
                        eng.wait_ge(sDO, em.cnt["DO"])
                return body

            blk.tensor(runner("T"))
            blk.vector(runner("V"))
            blk.scalar(runner("A"))
            blk.gpsimd(runner("G"))
            blk.sync(runner("S"))

    return nc


def kernel(L0_attn_x, L0_attn_Wq, L0_attn_Wk, L0_attn_Wv, L0_attn_Wp, L0_attn_bp,
           L0_attn_conv_w, L0_attn_conv_b, L0_ffn_w_gate, L0_ffn_w_up, L0_ffn_w_down):
    x = np.ascontiguousarray(np.asarray(L0_attn_x, dtype=np.float32))
    B, SL, _ = x.shape
    per = B // 8
    wqT = np.ascontiguousarray(np.asarray(L0_attn_Wq, np.float32).T)
    wkT = np.ascontiguousarray(np.asarray(L0_attn_Wk, np.float32).T)
    wvT = np.ascontiguousarray(np.asarray(L0_attn_Wv, np.float32).T)
    wpT = np.ascontiguousarray(np.asarray(L0_attn_Wp, np.float32).T)
    bp = np.ascontiguousarray(np.asarray(L0_attn_bp, np.float32).reshape(1, D))
    convw = np.ascontiguousarray(np.asarray(L0_attn_conv_w, np.float32))
    convb = np.ascontiguousarray(np.asarray(L0_attn_conv_b, np.float32).reshape(KC, P).T)
    wg = np.ascontiguousarray(np.asarray(L0_ffn_w_gate, np.float32))
    wu = np.ascontiguousarray(np.asarray(L0_ffn_w_up, np.float32))
    wd = np.ascontiguousarray(np.asarray(L0_ffn_w_down, np.float32))

    nc = build_program(seq_len=SL, n_seq=per)
    in_maps = []
    for c in range(8):
        shard = np.ascontiguousarray(x[c * per:(c + 1) * per].reshape(per * SL, D))
        in_maps.append({"x": shard, "wqT": wqT, "wkT": wkT, "wvT": wvT, "wpT": wpT,
                        "bp": bp, "convw": convw, "convb": convb,
                        "wg": wg, "wu": wu, "wd": wd})
    res = run_bass_kernel_spmd(nc, in_maps, core_ids=list(range(8)))
    out = np.concatenate([res.results[c]["out"].reshape(per, SL, D) for c in range(8)], axis=0)
    return np.ascontiguousarray(out.astype(np.float32))


# version 32
# speedup vs baseline: 1.0059x; 1.0059x over previous
import os
import numpy as np
from contextlib import ExitStack

import concourse.bass as bass
import concourse.mybir as mybir
from concourse.bass_utils import run_bass_kernel_spmd

F32 = mybir.dt.float32
BF16 = mybir.dt.bfloat16
I16 = mybir.dt.int16
AF = mybir.ActivationFunctionType
ALU = mybir.AluOpType
AX = mybir.AxisListType

P = 128
D = 1024
DFF = 2816
KC = D // P
DC = DFF // P
HEADS = 16
HD = 64
NPAIR = 8
EPS = 1e-5
SEGS = [(0, 768), (768, 768), (1536, 768), (2304, 512)]

_SIM = os.environ.get("BASS_SIM_FP64") == "1"
CB = 0.5 if _SIM else 0.0
_TRACE = []
_LAST_EM = None


class Emit:

    DMA_SEMS = ("DW", "DX", "DQ", "DO", "DY")

    def __init__(self):
        self.ops = {e: [] for e in ("T", "V", "A", "G", "S")}
        self.cnt = {s: 0 for s in ("T", "V", "A", "G") + self.DMA_SEMS}
        self.vc = {e: {} for e in self.ops}
        self.buf = {}
        self.sched = None

    def op(self, eng, fn, reads=(), writes=(), dma_sem=None, cost=None):
        if self.sched is not None:
            self.sched.append((eng, fn, tuple(reads), tuple(writes), dma_sem, cost))
            return
        self._op(eng, fn, reads, writes, dma_sem)

    def begin_sched(self):
        assert self.sched is None
        if os.environ.get("KSCHED") == "0":
            return
        self.sched = []

    def end_sched(self):
        if self.sched is None:
            return
        ops, self.sched = self.sched, None
        for k in _list_schedule(ops):
            eng, fn, reads, writes, dma_sem, _ = ops[k]
            self._op(eng, fn, reads, writes, dma_sem)

    def _op(self, eng, fn, reads=(), writes=(), dma_sem=None):
        if os.environ.get("KTRACE") == "1":
            _TRACE.append((eng, dma_sem, tuple(reads), tuple(writes)))
        sem = dma_sem if dma_sem else eng
        inc = 16 if dma_sem else 1
        if dma_sem:
            reads = list(reads) + [f"_tok_{sem}"]
            writes = list(writes) + [f"_tok_{sem}"]
        needed = {}
        for b in reads:
            st = self.buf.get(b)
            if st and st["w"]:
                s, c = st["w"]
                needed[s] = max(needed.get(s, 0), c)
        for b in writes:
            st = self.buf.get(b)
            if st:
                if st["w"]:
                    s, c = st["w"]
                    needed[s] = max(needed.get(s, 0), c)
                for s, c in st["r"].items():
                    needed[s] = max(needed.get(s, 0), c)
        waits = []
        vc = self.vc[eng]
        for s, c in sorted(needed.items()):
            if eng == "T" and s == "T":
                continue
            if vc.get(s, 0) < c:
                waits.append((s, c))
                vc[s] = c
        self.cnt[sem] += inc
        cnt = self.cnt[sem]
        self.ops[eng].append((waits, fn, sem, inc))
        for b in reads:
            st = self.buf.setdefault(b, {"w": None, "r": {}})
            st["r"][sem] = max(st["r"].get(sem, 0), cnt)
        for b in writes:
            self.buf[b] = {"w": (sem, cnt), "r": {}}

    def barrier(self):
        assert self.sched is None
        snapshot = dict(self.cnt)
        for e in self.ops:
            vc = self.vc[e]
            waits = []
            for s, c in sorted(snapshot.items()):
                if e == "T" and s == "T":
                    continue
                if c > 0 and vc.get(s, 0) < c:
                    waits.append((s, c))
                    vc[s] = c
            if waits:
                self.ops[e].append((waits, None, None, 0))


_ENG_COST = {"T": 260, "V": 700, "A": 720, "G": 1900, "S": 150}


def _list_schedule(ops):
    n = len(ops)
    busy = [0.0] * n
    lat = [0.0] * n
    for i, (eng, fn, rd, wr, dsem, cost) in enumerate(ops):
        if dsem:
            busy[i] = 150.0
            lat[i] = float(cost) if cost is not None else 4000.0
        else:
            busy[i] = float(cost) if cost is not None else _ENG_COST[eng]
            lat[i] = busy[i] + 100.0
    last_w = {}
    readers = {}
    preds = [None] * n
    succ = [[] for _ in range(n)]
    indeg = [0] * n
    for i, (eng, fn, rd, wr, dsem, cost) in enumerate(ops):
        r = list(rd)
        w = list(wr)
        if dsem:
            tok = "_tok_" + dsem
            r.append(tok)
            w.append(tok)
        dset = set()
        for b in r:
            j = last_w.get(b)
            if j is not None:
                dset.add(j)
        for b in w:
            j = last_w.get(b)
            if j is not None:
                dset.add(j)
            dset.update(readers.get(b, ()))
        dset.discard(i)
        preds[i] = dset
        for j in dset:
            succ[j].append(i)
        indeg[i] = len(dset)
        for b in r:
            readers.setdefault(b, set()).add(i)
        for b in w:
            last_w[b] = i
            readers[b] = set()
    prio = [0.0] * n
    for i in range(n - 1, -1, -1):
        m = 0.0
        for j in succ[i]:
            if prio[j] > m:
                m = prio[j]
        prio[i] = m + lat[i]
    eng_t = {e: 0.0 for e in _ENG_COST}
    avail = [0.0] * n
    ready = [i for i in range(n) if indeg[i] == 0]
    order = []
    while ready:
        best = -1
        best_key = None
        best_est = 0.0
        m = None
        ests = []
        for i in ready:
            est = eng_t[ops[i][0]]
            for j in preds[i]:
                if avail[j] > est:
                    est = avail[j]
            ests.append(est)
            if m is None or est < m:
                m = est
        for i, est in zip(ready, ests):
            if est <= m + 400.0:
                key = (-prio[i], i)
                if best < 0 or key < best_key:
                    best, best_key, best_est = i, key, est
        i = best
        eng_t[ops[i][0]] = best_est + busy[i]
        avail[i] = best_est + lat[i]
        order.append(i)
        ready.remove(i)
        for j in succ[i]:
            indeg[j] -= 1
            if indeg[j] == 0:
                ready.append(j)
    assert len(order) == n
    return order


def build_program(seq_len=2048, n_seq=2, _phases=("prep", "p1", "p2"), _dbg=False):
    S = seq_len * n_seq
    NT = S // P
    ST = seq_len // P
    assert seq_len % P == 0 and ST >= 2
    _ik = "ExternalOutput" if _dbg else "Internal"

    nc = bass.Bass("TRN2", target_bir_lowering=False, debug=False)

    x_d = nc.dram_tensor("x", [S, D], F32, kind="ExternalInput").ap()
    wqT_d = nc.dram_tensor("wqT", [D, D], F32, kind="ExternalInput").ap()
    wkT_d = nc.dram_tensor("wkT", [D, D], F32, kind="ExternalInput").ap()
    wvT_d = nc.dram_tensor("wvT", [D, D], F32, kind="ExternalInput").ap()
    wpT_d = nc.dram_tensor("wpT", [D, D], F32, kind="ExternalInput").ap()
    bp_d = nc.dram_tensor("bp", [1, D], F32, kind="ExternalInput").ap()
    convw_d = nc.dram_tensor("convw", [D, 3], F32, kind="ExternalInput").ap()
    convb_d = nc.dram_tensor("convb", [P, KC], F32, kind="ExternalInput").ap()
    wg_d = nc.dram_tensor("wg", [DFF, D], F32, kind="ExternalInput").ap()
    wu_d = nc.dram_tensor("wu", [DFF, D], F32, kind="ExternalInput").ap()
    wd_d = nc.dram_tensor("wd", [D, DFF], F32, kind="ExternalInput").ap()
    out_d = nc.dram_tensor("out", [S, D], F32, kind="ExternalOutput").ap()
    kv_dram = nc.dram_tensor("kv_scratch", [n_seq, P, NPAIR * P], BF16, kind=_ik).ap()
    xnfm_d = nc.dram_tensor("xnfm_d", [NT, P, D], BF16, kind=_ik).ap()
    xqfm_d = nc.dram_tensor("xqfm_d", [NT, P, D], BF16, kind=_ik).ap()
    qfm_d = nc.dram_tensor("qfm_d", [NT, P, D], BF16, kind=_ik).ap()
    wgq_d = nc.dram_tensor("wgq_d", [P, KC * DFF], BF16, kind=_ik).ap()
    wuq_d = nc.dram_tensor("wuq_d", [P, KC * DFF], BF16, kind=_ik).ap()
    wdq_d = nc.dram_tensor("wdq_d", [P, DC * D], BF16, kind=_ik).ap()
    wgq_v = wgq_d.rearrange("p (k f) -> p k f", f=DFF)
    wuq_v = wuq_d.rearrange("p (k f) -> p k f", f=DFF)
    wdq_v = wdq_d.rearrange("p (k f) -> p k f", f=D)
    if _dbg:
        xn_dbg = nc.dram_tensor("xn_dbg", [NT, P, D], F32, kind="ExternalOutput").ap()
        u1_dbg = nc.dram_tensor("u1_dbg", [NT, P, D], BF16, kind="ExternalOutput").ap()
        x_dbg = nc.dram_tensor("x_dbg", [NT, P, D], F32, kind="ExternalOutput").ap()
        st_dbg = nc.dram_tensor("st_dbg", [P, 2 * NT], F32, kind="ExternalOutput").ap()

    x_t = x_d.rearrange("(t p) d -> t p d", p=P)
    out_t = out_d.rearrange("(t p) d -> t p d", p=P)

    em = Emit()
    global _LAST_EM
    _LAST_EM = em
    E = em.op

    def dma(dst, src, rb, wb, sem="DW", eng="S", cost=None):
        E(eng, lambda e, d_=dst, s_=src: e.dma_start(out=d_, in_=s_), reads=rb, writes=wb,
          dma_sem=sem, cost=cost)

    with ExitStack() as ctx:
        sb = lambda name, shape, dt=F32: ctx.enter_context(nc.sbuf_tensor(name, shape, dt))
        psa = lambda name, shape, dt=F32: ctx.enter_context(nc.psum_tensor(name, shape, dt))

        ident_b = sb("ident_b", [P, P], BF16)
        ones_col = sb("ones_col", [P, 1], BF16)
        ones_row = sb("ones_row", [1, P], BF16)
        bp_b = sb("bp_b", [1, D], BF16)
        convw_b = sb("convw_b", [P, KC, 3], BF16)
        convb_b = sb("convb_b", [P, KC], BF16)
        qv = sb("qv", [P, D], F32)
        u1 = sb("u1", [P, D], BF16)
        u2am = sb("u2am", [P, D], BF16)
        fA = sb("fA", [P, D], F32)
        fB = sb("fB", [P, D], F32)
        bscale = sb("bscale", [P, 112], F32)
        brs = sb("brs", [P, 128], F32)
        psT = psa("psT", [P, KC, P], BF16)
        rden = brs[:, 112:128]
        kvblk = sb("kvblk", [P, NPAIR, P], BF16)
        KSp = sb("KSp", [P, n_seq, NPAIR, 2], BF16)
        wB = sb("wB", [P, KC, D], BF16)

        def quant_ops(src_ap, src_buf, out_ap, out_buf, width, par, scratch=None,
                      sgn_pre=None, g_off=False, force_wide=False):
            nb = width // 16
            wide = force_wide or width > 512
            if scratch is None:
                scratch = ((qv, "qv0", "qv1"), (u1, "u10", "u11"), (u2am, "u20", "u21"))
            (qvt, qn0, qn1), (u1t, un0, un1), (u2t, mn0, mn1) = scratch
            if wide:
                assert width <= 1024 and par == 0
                o = 0
                ob = 0
                qb, ub1, ub2 = [qn0, qn1], [un0, un1], [mn0, mn1]
            else:
                o = par * 512
                ob = par * 32
                qb, ub1, ub2 = [qn0 if par == 0 else qn1], \
                               [un0 if par == 0 else un1], [mn0 if par == 0 else mn1]
            sv = src_ap.rearrange("p (b k) -> p b k", k=16)
            qvp = qvt[:, o:o + width]
            u1p = u1t[:, o:o + width]
            u1i = u1p.bitcast(I16)
            u2p = u2t[:, o:o + width]
            if wide:
                bsc = bscale[:, 64:64 + nb]
                brp = brs[:, 64:64 + nb]
            else:
                bsc = bscale[:, ob:ob + nb]
                brp = brs[:, ob:ob + nb]
            rs_b = brp.unsqueeze(2).broadcast_to([P, nb, 16])
            sc_b = bsc.unsqueeze(2).broadcast_to([P, nb, 16])
            bb, rb = ("bscW", "brsW") if wide else (f"bsc{par}", f"brs{par}")
            ob_ = list(out_buf) if isinstance(out_buf, (tuple, list)) else [out_buf]
            wv = 160 + int(1.12 * width)
            wa = 230 + width
            wg = (160 + int(2.2 * width)) if g_off else wv
            GV = "G" if g_off else "V"
            if _SIM:
                ops = [
                    ("V", lambda e: e.tensor_reduce(bsc, sv, AX.X, ALU.max,
                                                    apply_absolute_value=True),
                     [src_buf, bb], [bb]),
                    ("A", lambda e: e.activation(qvp, src_ap, AF.Abs), [src_buf] + qb, qb),
                    ("V", lambda e: e.tensor_scalar(bsc, bsc, 1.0 / 6.0, 1e-12,
                                                    ALU.mult, ALU.max), [bb], [bb]),
                    ("V", lambda e: e.reciprocal(brp, bsc), [bb, rb], [rb]),
                    (GV,
                     lambda e: e.tensor_tensor(qvp.rearrange("p (b k) -> p b k", k=16),
                                               qvp.rearrange("p (b k) -> p b k", k=16),
                                               rs_b, ALU.mult), qb + [rb], qb),
                    ("V", lambda e: e.tensor_scalar(out_ap, qvp, 5.0, -2.0,
                                                    ALU.is_le, ALU.mult), qb + ob_, ob_),
                    ("A", lambda e: e.activation(u1i, qvp, AF.Copy, bias=CB, scale=2.0),
                     qb + ub1, ub1),
                    ("V", lambda e: e.tensor_scalar(u2p, u1i, 4.0, 0.5, ALU.min, ALU.mult),
                     ub1, ub2),
                    ("A", lambda e: e.activation(u1i, qvp, AF.Copy, bias=CB, scale=1.0),
                     qb + ub1, ub1),
                    ("V", lambda e: e.tensor_scalar(u1p, u1i, 2.0, 4.0, ALU.max, ALU.min),
                     ub1, ub1),
                    (GV, lambda e: e.tensor_add(u2p, u2p, u1p), ub2 + ub1, ub2),
                    (GV, lambda e: e.tensor_add(u2p, u2p, out_ap), ub2 + ob_, ub2),
                ]
                cst = [wv, wa, 300, 400, wg, wv, wa, wv, wa, wv, wg, wg]
                if sgn_pre is None:
                    ops.append(("A", lambda e: e.activation(out_ap, src_ap, AF.Sign),
                                [src_buf] + ob_, ob_))
                    ops.append((GV, lambda e: e.tensor_tensor(u2p, u2p, out_ap, ALU.mult),
                                ub2 + ob_, ub2))
                    cst += [wa, wg]
                else:
                    sga, sgb = sgn_pre
                    ops.append(("V", lambda e: e.tensor_tensor(u2p, u2p, sga, ALU.mult),
                                ub2 + [sgb], ub2))
                    cst += [wv]
            else:
                C2 = float(1.5 * 2 ** 21)
                C3 = float(1.5 * 2 ** 22)
                ops = [
                    ("V", lambda e: e.tensor_reduce(bsc, sv, AX.X, ALU.max,
                                                    apply_absolute_value=True),
                     [src_buf, bb], [bb]),
                    ("V", lambda e: e.tensor_scalar(
                        qvp.bitcast(mybir.dt.int32), src_ap.bitcast(mybir.dt.int32),
                        0x7FFFFFFF, None, ALU.bitwise_and), [src_buf] + qb, qb),
                    ("V", lambda e: e.tensor_scalar(bsc, bsc, 1.0 / 3.0, 2e-12,
                                                    ALU.mult, ALU.max), [bb], [bb]),
                    ("V", lambda e: e.reciprocal(brp, bsc), [bb, rb], [rb]),
                    (GV,
                     lambda e: e.tensor_tensor(qvp.rearrange("p (b k) -> p b k", k=16),
                                               qvp.rearrange("p (b k) -> p b k", k=16),
                                               rs_b, ALU.mult), qb + [rb], qb),
                    ("V", lambda e: e.tensor_scalar(u1p, qvp, C2, C2,
                                                    ALU.add, ALU.subtract), qb + ub1, ub1),
                    ("V", lambda e: e.tensor_scalar(u2p, qvp, C3, C3 + 1.0,
                                                    ALU.add, ALU.subtract), qb + ub2, ub2),
                    ("V", lambda e: e.tensor_scalar(u2p, u2p, 1.0, 0.0,
                                                    ALU.min, ALU.max), ub2, ub2),
                    ("V", lambda e: e.scalar_tensor_tensor(u2p, u1p, 1.0, u2p,
                                                           ALU.min, ALU.add),
                     ub1 + ub2, ub2),
                    ("V", lambda e: e.scalar_tensor_tensor(u2p, qvp, 2.5, u2p,
                                                           ALU.is_gt, ALU.add),
                     qb + ub2, ub2),
                ]
                cst = [wv, wv, 300, 400, wg, wv, wv, wv, wv, wv]
                if sgn_pre is None:
                    ops.append(("A", lambda e: e.activation(out_ap, src_ap, AF.Sign),
                                [src_buf] + ob_, ob_))
                    ops.append((GV, lambda e: e.tensor_tensor(u2p, u2p, out_ap, ALU.mult),
                                ub2 + ob_, ub2))
                    cst += [wa, wg]
                else:
                    sga, sgb = sgn_pre
                    ops.append((GV, lambda e: e.tensor_tensor(u2p, u2p, sga, ALU.mult),
                                ub2 + [sgb], ub2))
                    cst += [wg]
            ops.append((GV, lambda e: e.tensor_tensor(
                out_ap.rearrange("p (b k) -> p b k", k=16),
                u2p.rearrange("p (b k) -> p b k", k=16), sc_b, ALU.mult),
                ub2 + [bb], ob_))
            cst += [wg]
            return [t + (c,) for t, c in zip(ops, cst)]

        def emit_list(ops):
            for t in ops:
                E(t[0], t[1], reads=t[2], writes=t[3],
                  cost=(t[4] if len(t) > 4 else None))

        def emit_zip(la, lb):
            i = j = 0
            while i < len(la) or j < len(lb):
                if i < len(la):
                    t = la[i]
                    E(t[0], t[1], reads=t[2], writes=t[3],
                      cost=(t[4] if len(t) > 4 else None))
                    i += 1
                if j < len(lb):
                    t = lb[j]
                    E(t[0], t[1], reads=t[2], writes=t[3],
                      cost=(t[4] if len(t) > 4 else None))
                    j += 1

        def quant_wide(src_ap, src_buf, out_ap, out_names, width, scratch=None,
                       g_off=False):
            nA, nB = out_names
            if width <= 512:
                emit_list(quant_ops(src_ap, src_buf, out_ap, nA, width, 0, scratch,
                                    g_off=g_off))
                return
            w0 = 512
            w1 = width - 512
            emit_zip(quant_ops(src_ap[:, 0:w0], src_buf, out_ap[:, 0:w0], nA, w0, 0,
                               scratch, g_off=g_off),
                     quant_ops(src_ap[:, w0:width], src_buf, out_ap[:, w0:width],
                               nB, w1, 1, scratch, g_off=g_off))

        def transpose_group(src_ap_fn, nchunks, dst_t, dst_buf, loc_fuse=None,
                            evac="V", dst_off=0, src_bufs=()):
            rsrc = list(src_bufs)
            step = 4 if os.environ.get("KTS", "0") == "1" else KC
            for j in range(0, nchunks, step):
                m = min(step, nchunks - j)
                hf = (j // 4) % 2 if step == 4 else 0
                pb = f"psT{hf}"
                for c in range(m):
                    E("T", lambda e, jj=j + c, cc=4 * hf + c: e.transpose(
                        psT[:, cc, :], src_ap_fn(jj), ident_b[:]),
                      reads=rsrc + ["ident", pb], writes=[pb])
                if loc_fuse is None:
                    if evac == "A":
                        E("A", lambda e, jj=j + dst_off, mm=m, h_=hf: e.activation(
                            dst_t[:, jj:jj + mm, :].rearrange("p c q -> p (c q)"),
                            psT[:, 4 * h_:4 * h_ + mm, :].rearrange("p c q -> p (c q)"),
                            AF.Copy),
                          reads=[pb, dst_buf], writes=[dst_buf], cost=230 + m * P)
                    else:
                        E("V", lambda e, jj=j + dst_off, mm=m, h_=hf: e.tensor_copy(
                            dst_t[:, jj:jj + mm, :].rearrange("p c q -> p (c q)"),
                            psT[:, 4 * h_:4 * h_ + mm, :].rearrange("p c q -> p (c q)")),
                          reads=[pb], writes=[dst_buf], cost=160 + m * P)
                else:
                    loc_t, loc_buf = loc_fuse
                    E("V", lambda e, jj=j, mm=m, h_=hf: e.scalar_tensor_tensor(
                        dst_t[:, jj:jj + mm, :].rearrange("p c q -> p (c q)"),
                        psT[:, 4 * h_:4 * h_ + mm, :].rearrange("p c q -> p (c q)"), 1.0,
                        loc_t[:, jj:jj + mm, :].rearrange("p c q -> p (c q)"),
                        ALU.mult, ALU.add),
                      reads=[pb, loc_buf], writes=[dst_buf], cost=160 + m * P)

        E("G", lambda e: e.memset(ident_b[:], 0.0), writes=["ident"])
        E("G", lambda e: e.affine_select(out=ident_b[:], in_=ident_b[:],
                                         compare_op=ALU.not_equal, fill=1.0, base=0,
                                         pattern=[[-1, P]], channel_multiplier=1),
          reads=["ident"], writes=["ident"])
        E("G", lambda e: e.memset(ones_col[:], 1.0), writes=["ones"])
        E("G", lambda e: e.memset(ones_row[:], 1.0), writes=["ones_r"])

        with ExitStack() as pctx:
            sbp = lambda name, shape, dt: pctx.enter_context(nc.sbuf_tensor(name, shape, dt))
            bp_f32 = sbp("bp_f32", [1, D], F32)
            convw_sb = sbp("convw_sb", [P, KC, 3], F32)
            convb_sb = sbp("convb_sb", [P, KC], F32)

            dma(bp_f32[:], bp_d[:], [], ["bp_f32"])
            E("V", lambda e: e.tensor_copy(bp_b[:], bp_f32[:]), reads=["bp_f32"], writes=["bp_b"])
            dma(convw_sb[:], convw_d.rearrange("(c p) k -> p c k", p=P), [], ["convw"])
            dma(convb_sb[:], convb_d[:], [], ["convb"])
            E("V", lambda e: e.tensor_copy(convw_b[:], convw_sb[:]),
              reads=["convw"], writes=["convw_b"])
            E("V", lambda e: e.tensor_copy(convb_b[:], convb_sb[:]),
              reads=["convb"], writes=["convb_b"])

        em.barrier()

        with ExitStack() as p1ctx:
            sb1 = lambda name, shape, dt: p1ctx.enter_context(nc.sbuf_tensor(name, shape, dt))
            ps1 = lambda name, shape, dt: p1ctx.enter_context(nc.psum_tensor(name, shape, dt))
            wA = sb1("wA", [P, KC, D], BF16)
            wC = sb1("wC", [P, KC, D], BF16)
            stats = sb1("stats", [P, 2 * NT], F32)
            stscr = sb1("stscr", [P, 8 * NT], F32)
            x_sb = sb1("x_sb", [P, 2, D], F32)
            xn_tm = sb1("xn_tm", [P, 2, D], F32)
            u1p1 = sb1("u1p1", [P, 2, D], BF16)
            qe = sb1("qe", [P, 2, D], BF16)
            xq_sb = sb1("xq_sb", [P, 2, D], BF16)
            qs1 = sb1("qs1", [P, 2, D], BF16)
            eluA = sb1("eluA", [P, 2, D], F32)
            eluB = sb1("eluB", [P, 2, D], F32)
            fC = sb1("fC", [P, D], F32)
            fD = sb1("fD", [P, D], F32)
            xnfm_s = sb1("xnfm_s", [P, 2, KC, P], BF16)
            xqfm_s = sb1("xqfm_s", [P, 2, KC, P], BF16)
            qfm_s = sb1("qfm_s", [P, 2, KC, P], BF16)
            k_tm = sb1("k_tm", [P, 2, D], BF16)
            v_tm = sb1("v_tm", [P, 2, D], BF16)
            ks_sb = sb1("ks_sb", [P, NPAIR], F32)
            wstage = sb1("wstage", [P, 2, D], F32)
            wqchunk = sb1("wqchunk", [P, 2, D], BF16)
            wstg_out = sb1("wstg_out", [P, 2, KC, P], BF16)
            wload = sb1("wload", [P, 2, D], F32)
            psK = ps1("psK", [P, D], F32)
            psV = ps1("psV", [P, D], F32)
            psKV = ps1("psKV", [P, NPAIR, P], F32)
            psKS = ps1("psKS", [P, NPAIR], F32)

            wjobs = [(wkT_d, wA, "wA"), (wvT_d, wB, "wB"), (wqT_d, wC, "wC")]
            for jx, (w_ap, dst, dbuf) in enumerate(wjobs):
                for c in range(KC):
                    sl = (jx * KC + c) % 2
                    sem, eng = ("DW", "S") if sl == 0 else ("DY", "G")
                    dma(wload[:, sl, :], w_ap[c * P:(c + 1) * P, :],
                        [f"wload{sl}"], [f"wload{sl}"], sem=sem, eng=eng, cost=3000)
                    E("V", lambda e, cc=c, dd=dst, s_=sl: e.tensor_copy(
                        dd[:, cc, :], wload[:, s_, :]),
                      reads=[f"wload{sl}"], writes=[dbuf], cost=1000)

            def prep_task(w_dram, c, s0, sw, dst_v, ti):
                pr = ti % 2
                wst = wstage[:, pr, 0:sw]
                wqc = wqchunk[:, pr, :]
                dma(wst, w_dram[c * P:(c + 1) * P, s0:s0 + sw],
                    [f"wstage{pr}"], [f"wstage{pr}"], cost=5000)
                quant_wide(wst, f"wstage{pr}", wqc[:, 0:sw],
                           (f"wqcA{pr}", f"wqcB{pr}"), sw, g_off=True)
                g0 = s0 // P
                for j in range(0, sw // P, KC):
                    m = min(KC, sw // P - j)
                    transpose_group(
                        lambda jj, p_=pr, b_=j: wqchunk[:, p_, (b_ + jj) * P:(b_ + jj + 1) * P],
                        m, wstg_out[:, pr], f"wstg{pr}", evac="A",
                        src_bufs=(f"wqcA{pr}", f"wqcB{pr}"))
                    dma(dst_v[:, g0 + j:g0 + j + m, c * P:(c + 1) * P],
                        wstg_out[:, pr, 0:m, :],
                        [f"wstg{pr}"], [f"wstg{pr}", "wq_dram"], sem="DY", eng="G",
                        cost=3000)

            prep_tasks = []
            if "prep" in _phases:
                ti = 0
                for (w_dram, nrc, width, dst_v) in (
                        (wg_d, DC, D, wgq_v), (wu_d, DC, D, wuq_v),
                        (wd_d, KC, DFF, wdq_v)):
                    for c in range(nrc):
                        for s0 in range(0, width, 1024):
                            sw = min(1024, width - s0)
                            prep_tasks.append(
                                lambda w_=w_dram, c_=c, s_=s0, sw_=sw, d_=dst_v, t_=ti:
                                prep_task(w_, c_, s_, sw_, d_, t_))
                            ti += 1
            ntask = len(prep_tasks)
            tix = 0

            def drain_prep(n):
                nonlocal tix
                for _ in range(n):
                    if tix < ntask:
                        prep_tasks[tix]()
                        tix += 1

            nt1 = NT if "p1" in _phases else 0
            for i in range(nt1):
                seq = i // ST
                first = (i % ST == 0)
                last = (i % ST == ST - 1)
                if i % 8 == 0:
                    em.begin_sched()
                sp = i % 2
                stt = stscr[:, 8 * i:8 * i + 8]
                nmS = lambda s, sp_=sp: s + str(sp_)
                xv = x_sb[:, sp, :]
                xnv = xn_tm[:, sp, :]
                xbuf = nmS("x")
                xnbuf = nmS("xn")
                ksink = kvblk[:].rearrange("p a b -> p (a b)")
                dma(xv, x_t[i], [xbuf], [xbuf], sem="DX", eng="A", cost=5000)
                E("A", lambda e, st_=stt, x_=xv: e.activation(ksink, x_, AF.Square,
                                                       accum_out=st_[:, 0:1]),
                  reads=[xbuf, "kvblk", "brs"], writes=["kvblk", nmS("st_sq")], cost=1400)
                E("A", lambda e, st_=stt, x_=xv: e.activation(ksink, x_, AF.Copy,
                                                       accum_out=st_[:, 1:2]),
                  reads=[xbuf, "kvblk"], writes=["kvblk", nmS("st_sum")], cost=1400)
                E("V", lambda e, st_=stt: e.tensor_scalar(st_[:, 2:3], st_[:, 1:2],
                                                          1.0 / D, None, ALU.mult),
                  reads=[nmS("st_sum")], writes=[nmS("st_m")], cost=250)
                E("V", lambda e, st_=stt: e.tensor_tensor(st_[:, 3:4], st_[:, 2:3],
                                                          st_[:, 2:3], ALU.mult),
                  reads=[nmS("st_m")], writes=[nmS("st_m2")], cost=250)
                E("V", lambda e, st_=stt: e.scalar_tensor_tensor(
                    st_[:, 4:5], st_[:, 0:1], 1.0 / D, st_[:, 3:4],
                    ALU.mult, ALU.subtract),
                  reads=[nmS("st_sq"), nmS("st_m2")], writes=[nmS("st_u")], cost=250)
                E("V", lambda e, st_=stt: e.tensor_scalar(st_[:, 4:5], st_[:, 4:5], EPS,
                                                          None, ALU.add),
                  reads=[nmS("st_u")], writes=[nmS("st_u")], cost=250)
                E("A", lambda e, st_=stt: e.activation(st_[:, 5:6], st_[:, 4:5],
                                                       AF.Sqrt),
                  reads=[nmS("st_u")], writes=[nmS("st_sd")], cost=260)
                E("V", lambda e, ii=i, st_=stt: e.reciprocal(
                    stats[:, 2 * ii:2 * ii + 1], st_[:, 5:6]),
                  reads=[nmS("st_sd"), "stats"], writes=["stats"], cost=250)
                E("V", lambda e, ii=i, st_=stt: e.scalar_tensor_tensor(
                    stats[:, 2 * ii + 1:2 * ii + 2], st_[:, 2:3], -1.0,
                    stats[:, 2 * ii:2 * ii + 1], ALU.mult, ALU.mult),
                  reads=[nmS("st_m"), "stats"], writes=["stats"], cost=250)
                E("A", lambda e, ii=i, xn_=xnv, x_=xv: e.activation(
                    xn_, x_, AF.Identity,
                    bias=stats[:, 2 * ii + 1:2 * ii + 2], scale=stats[:, 2 * ii:2 * ii + 1]),
                  reads=[xbuf, "stats", xnbuf], writes=[xnbuf], cost=1400)
                u1v = u1p1[:, sp, :]
                u1b = nmS("u1p1")
                xnfmv = xnfm_s[:, sp, :, :]
                xnfmb = nmS("xnfm_s")
                ktv = k_tm[:, sp, :]
                ktb = nmS("k_tm")
                vtv = v_tm[:, sp, :]
                vtb = nmS("v_tm")
                qev = qe[:, sp, :]
                qeb = nmS("qe")
                qfmv = qfm_s[:, sp, :, :]
                qfmb = nmS("qfm_s")
                xqv_ = xq_sb[:, sp, :]
                xqfmv = xqfm_s[:, sp, :, :]
                xqfmb = nmS("xqfm_s")
                eAv = eluA[:, sp, :]
                eAb = nmS("eluA")
                eBv = eluB[:, sp, :]
                eBb = nmS("eluB")
                qsc = ((fA, "fA", "fA"), (qs1[:, 0, :], "qs1A0", "qs1B0"),
                       (fB, "fB", "fB")) if sp == 0 else                       ((fC, "fC", "fC"), (qs1[:, 1, :], "qs1A1", "qs1B1"),
                       (fD, "fD", "fD"))
                E("V", lambda e, xn_=xnv, u_=u1v, ub=u1b: e.tensor_copy(u_, xn_),
                  reads=[xnbuf, u1b], writes=[u1b], cost=1000)
                if _dbg:
                    dma(xn_dbg[i], xnv, [xnbuf], ["xdbg"], sem="DO", eng="S")
                    dma(u1_dbg[i], u1v, [u1b], ["udbg"], sem="DO", eng="S")
                    dma(x_dbg[i], xv, [xbuf], ["xdbg2"], sem="DO", eng="S")
                transpose_group(lambda j, u_=u1v: u_[:, j * P:(j + 1) * P], KC, xnfmv,
                                xnfmb, evac="A", src_bufs=(u1b,))
                dma(xnfm_d[i], xnfmv.rearrange("p c q -> p (c q)"),
                    [xnfmb], ["xnfm_dram"], sem="DY", eng="G", cost=3000)
                quant_wide(xnv, xnbuf, xqv_, (nmS("xqA"), nmS("xqB")), D,
                           scratch=qsc, g_off=True)
                transpose_group(lambda j, x_=xqv_: x_[:, j * P:(j + 1) * P], KC, xqfmv,
                                xqfmb, evac="A", src_bufs=(nmS("xqA"), nmS("xqB")))
                dma(xqfm_d[i], xqfmv.rearrange("p c q -> p (c q)"),
                    [xqfmb], ["xqfm_dram"], sem="DY", eng="G", cost=3000)
                for kc in range(KC):
                    for (ps, wsrc, wbuf, pbuf) in ((psK, wA, "wA", "psK"),
                                                   (psV, wB, "wB", "psV")):
                        for oc in range(2):
                            E("T", lambda e, k_=kc, o_=oc, p_=ps, w_=wsrc, x_=xnfmv:
                                e.matmul(
                                    p_[:, o_ * 512:(o_ + 1) * 512], x_[:, k_, :],
                                    w_[:, k_, o_ * 512:(o_ + 1) * 512],
                                    start=(k_ == 0), stop=(k_ == KC - 1)),
                              reads=[xnfmb, wbuf, pbuf], writes=[pbuf])
                E("V", lambda e, v_=vtv: e.tensor_copy(v_, psV[:]),
                  reads=["psV", vtb], writes=[vtb], cost=1100)
                E("A", lambda e, a_=eAv: e.activation(a_, psK[:], AF.Relu, scale=-1.0),
                  reads=["psK", eAb], writes=[eAb], cost=1400)
                E("A", lambda e, a_=eAv, b_=eBv: e.activation(b_, a_, AF.Exp, scale=-1.0),
                  reads=[eAb, eBb], writes=[eBb], cost=1400)
                E("V", lambda e, k_=ktv, b_=eBv: e.scalar_tensor_tensor(
                    k_, psK[:], 1.0, b_, ALU.add, ALU.max),
                  reads=["psK", eBb, ktb], writes=[ktb], cost=1300)
                for oc in range(KC):
                    for kc in range(KC):
                        E("T", lambda e, k_=kc, o_=oc, x_=xnfmv: e.matmul(
                            psV[:, o_ * P:(o_ + 1) * P],
                            wC[:, k_, o_ * P:(o_ + 1) * P], x_[:, k_, :],
                            start=(k_ == 0), stop=(k_ == KC - 1)),
                          reads=[xnfmb, "wC", "psV"], writes=["psV"])
                E("A", lambda e, a_=eAv: e.activation(a_, psV[:], AF.Relu, scale=-1.0),
                  reads=["psV", eAb], writes=[eAb], cost=1400)
                E("A", lambda e, a_=eAv, b_=eBv: e.activation(b_, a_, AF.Exp, scale=-1.0),
                  reads=[eAb, eBb], writes=[eBb], cost=1400)
                E("V", lambda e, q_=qfmv, b_=eBv: e.scalar_tensor_tensor(
                    q_.rearrange("p c q -> p (c q)"), psV[:], 1.0, b_,
                    ALU.add, ALU.max),
                  reads=["psV", eBb, qfmb], writes=[qfmb], cost=1300)
                for pair in range(NPAIR):
                    E("T", lambda e, p_=pair, f_=first, l_=last, k_=ktv, v_=vtv:
                        e.matmul(
                            psKV[:, p_, :], k_[:, p_ * P:(p_ + 1) * P],
                            v_[:, p_ * P:(p_ + 1) * P],
                            start=(f_ and p_ in (0, 4)), stop=(l_ and p_ in (3, 7))),
                      reads=[ktb, vtb, "psKV"], writes=["psKV"])
                    E("T", lambda e, p_=pair, f_=first, l_=last, k_=ktv: e.matmul(
                        psKS[:, p_:p_ + 1], k_[:, p_ * P:(p_ + 1) * P], ones_col[:],
                        start=(f_ and p_ == 0), stop=(l_ and p_ == NPAIR - 1)),
                      reads=[ktb, "ones", "psKS"], writes=["psKS"])
                dma(qfm_d[i], qfmv.rearrange("p c q -> p (c q)"),
                    [qfmb], ["qfm_dram"], sem="DY", eng="G", cost=3000)
                if last:
                    E("A", lambda e: e.activation(ks_sb[:], psKS[:], AF.Copy),
                      reads=["psKS", "ks_sb"], writes=["ks_sb"])
                    for pair in range(NPAIR):
                        E("A", lambda e, p_=pair: e.activation(
                            kvblk[:, p_, :], psKV[:, p_, :], AF.Copy),
                          reads=["psKV", "kvblk"], writes=["kvblk"])
                        E("V", lambda e, p_=pair: e.memset(kvblk[0:64, p_, 64:128], 0.0),
                          reads=["kvblk"], writes=["kvblk"])
                        E("V", lambda e, p_=pair: e.memset(kvblk[64:128, p_, 0:64], 0.0),
                          reads=["kvblk"], writes=["kvblk"])
                    dma(kv_dram[seq], kvblk[:].rearrange("p a b -> p (a b)"),
                        ["kvblk"], ["kv_dram"], sem="DY", eng="G")
                    E("V", lambda e, s_=seq: e.tensor_copy(KSp[:, s_, :, 0], ks_sb[:]),
                      reads=["ks_sb", "KSp"], writes=["KSp"])
                    E("V", lambda e, s_=seq: e.tensor_copy(KSp[:, s_, :, 1], ks_sb[:]),
                      reads=["ks_sb", "KSp"], writes=["KSp"])
                    E("V", lambda e, s_=seq: e.memset(KSp[64:128, s_, :, 0], 0.0),
                      reads=["KSp"], writes=["KSp"])
                    E("V", lambda e, s_=seq: e.memset(KSp[0:64, s_, :, 1], 0.0),
                      reads=["KSp"], writes=["KSp"])
                want = ((i + 1) * ntask) // max(nt1, 1)
                drain_prep(max(0, want - tix))
                if i % 8 == 7 or i == nt1 - 1:
                    em.end_sched()
            em.begin_sched()
            drain_prep(ntask)
            em.end_sched()
            if _dbg and nt1:
                dma(st_dbg[:], stats[:], ["stats"], ["sdbg"], sem="DO", eng="S")

        em.barrier()

        wgT_q = sb("wgT_q", [P, KC, DFF], BF16)
        wuT_q = sb("wuT_q", [P, KC, DFF], BF16)
        wdT_q = sb("wdT_q", [P, DC, D], BF16)
        xnr = sb("xnr", [P, 3, KC, P], BF16)
        xqr = sb("xqr", [P, 3, KC, P], BF16)
        qfr = sb("qfr", [P, 2, KC, P], BF16)
        edge = sb("edge", [P, 3, KC], BF16)
        fcol = sb("fcol", [P, 3, KC], BF16)
        loc = sb("loc", [P, 2, KC, P], BF16)
        xam = sb("xam", [P, KC, P], BF16)
        g_sb = sb("g_sb", [P, 2, D], BF16)
        xhq = sb("xhq", [P, 2, 768], BF16)
        u3 = sb("u3", [P, 2, 768], BF16)
        hB = sb("hB", [P, 768], F32)
        hqloc = sb("hqloc", [P, 6, P], BF16)
        psAD = psa("psAD", [P, 2 * P + HEADS], F32)
        psG1 = psa("psG1", [P, 512], F32)
        psG2 = psa("psG2", [P, 512], F32)
        psQ = psa("psQ", [P, P], F32)
        psG3 = psa("psG3", [P, 512], F32)
        psDown = psa("psDown", [P, D], F32)
        psA_v = psAD[:, 0:P]
        psDn_v = psAD[:, P:P + HEADS]
        psQ_v = psQ[:, :]
        psGs = [(psG1, "psG1"), (psG2, "psG2"), (psG3, "psG3")]
        gctr = [0]

        def p2_reloads():
            for si, (s0, sw) in enumerate(SEGS):
                dma(wgT_q[:, :, s0:s0 + sw], wgq_v[:, :, s0:s0 + sw],
                    ["wq_dram"], [f"wgTs{si}"], sem="DW", eng="S", cost=12000)
            for si, (s0, sw) in enumerate(SEGS):
                dma(wuT_q[:, :, s0:s0 + sw], wuq_v[:, :, s0:s0 + sw],
                    ["wq_dram"], [f"wuTs{si}"], sem="DY", eng="G", cost=12000)
            dma(wdT_q[:, :, 0:512], wdq_v[:, :, 0:512],
                ["wq_dram"], ["wdTh0"], sem="DW", eng="S", cost=16000)
            dma(wdT_q[:, :, 512:1024], wdq_v[:, :, 512:1024],
                ["wq_dram"], ["wdTh1"], sem="DY", eng="G", cost=16000)
            for c in range(KC):
                for h in range(2):
                    dma(qv[:, 0:512], wpT_d[c * P:(c + 1) * P, h * 512:(h + 1) * 512],
                        ["qv0"], ["qv0"], sem="DX", eng="A", cost=2000)
                    E("A", lambda e, cc=c, h_=h: e.activation(
                        wB[:, cc, h_ * 512:(h_ + 1) * 512], qv[:, 0:512], AF.Copy),
                      reads=["qv0", "wB"], writes=["wB"], cost=700)

        def p2_prefetch(j):
            r3 = j % 3
            dma(xnr[:, r3, :, :].rearrange("p c q -> p (c q)"), xnfm_d[j],
                ["xnfm_dram", f"xnr{r3}"], [f"xnr{r3}"],
                sem="DX", eng="A", cost=2500)
            E("V", lambda e, r_=r3: e.tensor_copy(edge[:, r_, :], xnr[:, r_, :, P - 1]),
              reads=[f"xnr{r3}"], writes=[f"edge{r3}"], cost=200)
            E("V", lambda e, r_=r3: e.tensor_copy(fcol[:, r_, :], xnr[:, r_, :, 0]),
              reads=[f"xnr{r3}"], writes=[f"fcol{r3}"], cost=200)
            dma(xqr[:, r3, :, :].rearrange("p c q -> p (c q)"), xqfm_d[j],
                ["xqfm_dram", f"xqr{r3}"], [f"xqr{r3}"], sem="DQ", eng="S", cost=2500)
            r2q = j % 2
            dma(qfr[:, r2q, :, :].rearrange("p c q -> p (c q)"), qfm_d[j],
                ["qfm_dram", f"qfr{r2q}"], [f"qfr{r2q}"], sem="DQ", eng="S", cost=2500)

        def p2_tile(j):
            seq = j // ST
            first = (j % ST == 0)
            last = (j % ST == ST - 1)
            r3 = j % 3
            sp = j % 2
            xqv = xqr[:, r3, :, :]
            xqb = f"xqr{r3}"
            qfv = qfr[:, sp, :, :]
            qfb = f"qfr{sp}"
            locv = loc[:, sp, :, :]
            locb = f"loc{sp}"
            xamv = xam[:, :, :]
            xamb = "xam"
            gv = g_sb[:, sp, :]
            gb = f"g_sb{sp}"
            if first:
                dma(kvblk[:].rearrange("p a b -> p (a b)"), kv_dram[seq],
                    ["kv_dram", "kvblk"], ["kvblk"], sem="DQ", eng="S", cost=3000)
            xv = xnr[:, r3, :, :]
            xb = f"xnr{r3}"
            w0b = convw_b[:, :, 0].unsqueeze(2).broadcast_to([P, KC, P])
            w1b = convw_b[:, :, 1].unsqueeze(2).broadcast_to([P, KC, P])
            w2b = convw_b[:, :, 2].unsqueeze(2).broadcast_to([P, KC, P])
            cbb = convb_b[:, :].unsqueeze(2).broadcast_to([P, KC, P])
            s1v = fA[:, 0:D].rearrange("p (c q) -> p c q", q=P)
            s2v = fB[:, 0:D].rearrange("p (c q) -> p c q", q=P)
            E("G", lambda e: e.tensor_tensor(s1v, xv, w1b, ALU.mult),
              reads=[xb, "convw_b", "fA"], writes=["fA"], cost=2400)
            E("G", lambda e: e.tensor_tensor(s2v[:, :, 1:P], xv[:, :, 0:P - 1],
                                             w0b[:, :, 1:P], ALU.mult),
              reads=[xb, "convw_b", "fB"], writes=["fB"], cost=2400)
            E("G", lambda e: e.tensor_tensor(s1v, s1v, cbb, ALU.add),
              reads=["fA", "convb_b"], writes=["fA"], cost=2400)
            if first:
                E("G", lambda e: e.memset(s2v[:, :, 0:1], 0.0),
                  reads=["fB"], writes=["fB"], cost=300)
            else:
                lsl = (j - 1) % 3
                E("G", lambda e, l_=lsl: e.tensor_tensor(
                    s2v[:, :, 0:1], edge[:, l_, :].unsqueeze(2),
                    convw_b[:, :, 0].unsqueeze(2), ALU.mult),
                  reads=[f"edge{lsl}", "convw_b", "fB"], writes=["fB"], cost=300)
            E("G", lambda e: e.tensor_tensor(fA[:, 0:D], fA[:, 0:D], fB[:, 0:D], ALU.add),
              reads=["fA", "fB"], writes=["fA"], cost=2200)
            E("G", lambda e: e.tensor_tensor(s2v[:, :, 0:P - 1], xv[:, :, 1:P],
                                             w2b[:, :, 0:P - 1], ALU.mult),
              reads=[xb, "convw_b", "fB"], writes=["fB"], cost=2400)
            if last:
                E("G", lambda e: e.memset(s2v[:, :, P - 1:P], 0.0),
                  reads=["fB"], writes=["fB"], cost=300)
            else:
                rsl = (j + 1) % 3
                E("G", lambda e, r_=rsl: e.tensor_tensor(
                    s2v[:, :, P - 1:P], fcol[:, r_, :].unsqueeze(2),
                    convw_b[:, :, 2].unsqueeze(2), ALU.mult),
                  reads=[f"fcol{rsl}", "convw_b", "fB"], writes=["fB"], cost=300)
            E("G", lambda e, lv=locv, lb=locb: e.tensor_tensor(lv, s1v, s2v, ALU.add),
              reads=["fA", "fB", locb], writes=[locb], cost=2200)
            if j + 2 < NT:
                p2_prefetch(j + 2)

            def seg_list(s):
                s0, sw = SEGS[s]
                ops = []
                for b0 in range(0, sw, 512):
                    bw = min(512, sw - b0)
                    pg, ng = psGs[gctr[0] % 3]
                    pu, nu = psGs[(gctr[0] + 1) % 3]
                    gctr[0] += 2
                    for kc in range(KC):
                        ops.append(("T", (lambda e, k_=kc, a_=s0 + b0, w_=bw, p_=pg:
                            e.matmul(
                                p_[:, 0:w_], xqv[:, k_, :], wgT_q[:, k_, a_:a_ + w_],
                                start=(k_ == 0), stop=(k_ == KC - 1))),
                            [xqb, f"wgTs{s}", ng], [ng]))
                        ops.append(("T", (lambda e, k_=kc, a_=s0 + b0, w_=bw, p_=pu:
                            e.matmul(
                                p_[:, 0:w_], xqv[:, k_, :], wuT_q[:, k_, a_:a_ + w_],
                                start=(k_ == 0), stop=(k_ == KC - 1))),
                            [xqb, f"wuTs{s}", nu], [nu]))
                    hv, hb = (fB, "fB") if s % 2 == 0 else (hB, "hB")
                    u3v = u3[:, s % 2, :]
                    u3b = f"u3{s % 2}"
                    ops += [
                        ("A", (lambda e, w_=bw, p_=pg: e.activation(
                            fA[:, 0:w_], p_[:, 0:w_], AF.Tanh, scale=0.5)),
                         [ng, "fA"], ["fA"]),
                        ("V", (lambda e, w_=bw, p_=pg: e.scalar_tensor_tensor(
                            fA[:, 0:w_], fA[:, 0:w_], 1.0, p_[:, 0:w_],
                            ALU.add, ALU.mult)), ["fA", ng], ["fA"]),
                        ("V", (lambda e, b_=b0, w_=bw, p_=pu, h_=hv: e.scalar_tensor_tensor(
                            h_[:, b_:b_ + w_], fA[:, 0:w_], 0.5, p_[:, 0:w_],
                            ALU.mult, ALU.mult)), ["fA", nu, hb], [hb]),
                        ("A", (lambda e, b_=b0, w_=bw, h_=hv, u_=u3v: e.activation(
                            u_[:, b_:b_ + w_], h_[:, b_:b_ + w_], AF.Sign)),
                         [hb, u3b], [u3b]),
                    ]
                return ops

            def qnt_list(s):
                s0, sw = SEGS[s]
                hv, hb = (fB, "fB") if s % 2 == 0 else (hB, "hB")
                return quant_ops(hv[:, 0:sw], hb, xhq[:, s % 2, 0:sw],
                                 (f"hqA{s % 2}", f"hqB{s % 2}"),
                                 sw, 0, sgn_pre=(u3[:, s % 2, 0:sw], f"u3{s % 2}"),
                                 force_wide=True, g_off=True)

            def tdown(s):
                s0, sw = SEGS[s]
                nch = sw // P
                sp2 = s % 2
                hql = hqloc
                hqb = "hqloc"
                transpose_group(lambda jj, s_=sp2: xhq[:, s_, jj * P:(jj + 1) * P], nch,
                                hql, hqb, evac="A",
                                src_bufs=(f"hqA{sp2}", f"hqB{sp2}"))
                for lc in range(nch):
                    gk = s0 // P + lc
                    for half in range(2):
                        E("T", lambda e, l_=lc, g_=gk, h_=half, q_=hql: e.matmul(
                            psDown[:, h_ * 512:(h_ + 1) * 512], q_[:, l_, :],
                            wdT_q[:, g_, h_ * 512:(h_ + 1) * 512],
                            start=(g_ == 0), stop=(g_ == DC - 1)),
                          reads=[hqb, f"wdTh{half}", f"psDn{half}"],
                          writes=[f"psDn{half}"])

            nseg = len(SEGS)
            pend = None
            for s in range(nseg):
                if s == 0:
                    emit_list(seg_list(0))
                else:
                    emit_zip(pend, seg_list(s))
                    tdown(s - 1)
                pend = qnt_list(s)
            emit_list(pend)
            tdown(nseg - 1)
            E("A", lambda e: e.activation(fA[:, 0:D], psDown[:], AF.Tanh, scale=0.5),
              reads=["psDn0", "psDn1", "fA"], writes=["fA"], cost=1400)
            E("V", lambda e: e.scalar_tensor_tensor(fA[:, 0:D], fA[:, 0:D], 1.0,
                                                    psDown[:], ALU.add, ALU.mult),
              reads=["fA", "psDn0", "psDn1"], writes=["fA"], cost=1300)
            E("A", lambda e, g_=gv, gb_=gb: e.activation(g_, fA[:, 0:D], AF.Copy, scale=0.5),
              reads=["fA", gb], writes=[gb], cost=1400)

            dma(qv[:], x_t[j], ["qv0", "qv1"], ["qv0", "qv1"], sem="DX", eng="A",
                cost=5000)

            for pair in range(NPAIR):
                E("T", lambda e, p_=pair, q_=seq: e.matmul(
                    psDn_v[:, 2 * p_:2 * p_ + 2], qfv[:, p_, :], KSp[:, q_, p_, :],
                    start=True, stop=True),
                  reads=[qfb, "KSp", "psAD"], writes=["psAD"])
            E("V", lambda e: e.tensor_scalar(rden[:, :], psDn_v, 1e-6, None, ALU.max),
              reads=["psAD", "rden"], writes=["rden"], cost=250)
            E("V", lambda e: e.reciprocal(rden[:, :], rden[:, :]),
              reads=["rden"], writes=["rden"], cost=300)
            for pair in range(NPAIR):
                pv = psA_v if pair % 2 == 0 else psQ_v
                pb = "psAD" if pair % 2 == 0 else "psQ"
                E("T", lambda e, p_=pair, v_=pv, b_=pb: e.matmul(
                    v_, qfv[:, p_, :], kvblk[:, p_, :], start=True, stop=True),
                  reads=[qfb, "kvblk", pb], writes=[pb])
                rb = rden[:, 2 * pair:2 * pair + 2].unsqueeze(2).broadcast_to([P, 2, HD])
                E("V", lambda e, p_=pair, rb_=rb, v_=pv, b_=pb: e.tensor_tensor(
                    u2am[:, p_ * P:(p_ + 1) * P].rearrange("p (h q) -> p h q", q=HD),
                    v_.rearrange("p (h q) -> p h q", q=HD), rb_, ALU.mult),
                  reads=[pb, "rden", "u20", "u21"], writes=["u20", "u21"], cost=400)
            transpose_group(lambda jj: u2am[:, jj * P:(jj + 1) * P], KC, xamv,
                            xamb, loc_fuse=(locv, locb), src_bufs=("u20", "u21"))
            for half in range(2):
                for kc in range(KC):
                    E("T", lambda e, k_=kc, h_=half: e.matmul(
                        psDown[:, h_ * 512:(h_ + 1) * 512], xamv[:, k_, :],
                        wB[:, k_, h_ * 512:(h_ + 1) * 512], start=(k_ == 0), stop=False),
                      reads=[xamb, "wB", f"psDn{half}"], writes=[f"psDn{half}"])
                E("T", lambda e, h_=half: e.matmul(
                    psDown[:, h_ * 512:(h_ + 1) * 512], ones_row[:, :],
                    bp_b[:, h_ * 512:(h_ + 1) * 512], start=False, stop=True),
                  reads=["ones_r", "bp_b", f"psDn{half}"], writes=[f"psDn{half}"])
                E("V", lambda e, h_=half: e.tensor_tensor(
                    u2am[:, h_ * 512:(h_ + 1) * 512], psDown[:, h_ * 512:(h_ + 1) * 512],
                    gv[:, h_ * 512:(h_ + 1) * 512], ALU.mult),
                  reads=[f"psDn{half}", gb, "u20" if half == 0 else "u21"],
                  writes=["u20" if half == 0 else "u21"])
                E("V", lambda e, h_=half: e.tensor_tensor(
                    qv[:, h_ * 512:(h_ + 1) * 512], qv[:, h_ * 512:(h_ + 1) * 512],
                    u2am[:, h_ * 512:(h_ + 1) * 512], ALU.add),
                  reads=[f"qv{half}", "u20" if half == 0 else "u21"],
                  writes=[f"qv{half}"])
                dma(out_t[j][:, half * 512:(half + 1) * 512], qv[:, half * 512:(half + 1) * 512],
                    [f"qv{half}"], ["out_dma"], sem="DO")

        if "p2" in _phases:
            em.begin_sched()
            p2_reloads()
            for jj in range(min(2, NT)):
                p2_prefetch(jj)
            em.end_sched()
            for j in range(NT):
                if j % 4 == 0:
                    em.begin_sched()
                p2_tile(j)
                if j % 4 == 3 or j == NT - 1:
                    em.end_sched()
        else:
            for j in range(NT):
                dma(out_t[j], qv[:], ["qv0", "qv1"], ["out_dma"], sem="DO")

        with nc.semaphore("sT") as sT, nc.semaphore("sV") as sV, nc.semaphore("sA") as sA, \
             nc.semaphore("sG") as sG, nc.semaphore("sDW") as sDW, nc.semaphore("sDX") as sDX, \
             nc.semaphore("sDQ") as sDQ, nc.semaphore("sDO") as sDO, \
             nc.semaphore("sDY") as sDY, nc.Block() as blk:
            sems = {"T": sT, "V": sV, "A": sA, "G": sG, "DW": sDW, "DX": sDX,
                    "DQ": sDQ, "DO": sDO, "DY": sDY}

            def runner(ename):
                def body(eng):
                    for waits, fn, sem, inc in em.ops[ename]:
                        for s, c in waits:
                            eng.wait_ge(sems[s], c)
                        if fn is not None:
                            fn(eng).then_inc(sems[sem], inc)
                    if ename == "S":
                        eng.wait_ge(sDO, em.cnt["DO"])
                return body

            blk.tensor(runner("T"))
            blk.vector(runner("V"))
            blk.scalar(runner("A"))
            blk.gpsimd(runner("G"))
            blk.sync(runner("S"))

    return nc


def kernel(L0_attn_x, L0_attn_Wq, L0_attn_Wk, L0_attn_Wv, L0_attn_Wp, L0_attn_bp,
           L0_attn_conv_w, L0_attn_conv_b, L0_ffn_w_gate, L0_ffn_w_up, L0_ffn_w_down):
    x = np.ascontiguousarray(np.asarray(L0_attn_x, dtype=np.float32))
    B, SL, _ = x.shape
    per = B // 8
    wqT = np.ascontiguousarray(np.asarray(L0_attn_Wq, np.float32).T)
    wkT = np.ascontiguousarray(np.asarray(L0_attn_Wk, np.float32).T)
    wvT = np.ascontiguousarray(np.asarray(L0_attn_Wv, np.float32).T)
    wpT = np.ascontiguousarray(np.asarray(L0_attn_Wp, np.float32).T)
    bp = np.ascontiguousarray(np.asarray(L0_attn_bp, np.float32).reshape(1, D))
    convw = np.ascontiguousarray(np.asarray(L0_attn_conv_w, np.float32))
    convb = np.ascontiguousarray(np.asarray(L0_attn_conv_b, np.float32).reshape(KC, P).T)
    wg = np.ascontiguousarray(np.asarray(L0_ffn_w_gate, np.float32))
    wu = np.ascontiguousarray(np.asarray(L0_ffn_w_up, np.float32))
    wd = np.ascontiguousarray(np.asarray(L0_ffn_w_down, np.float32))

    nc = build_program(seq_len=SL, n_seq=per)
    in_maps = []
    for c in range(8):
        shard = np.ascontiguousarray(x[c * per:(c + 1) * per].reshape(per * SL, D))
        in_maps.append({"x": shard, "wqT": wqT, "wkT": wkT, "wvT": wvT, "wpT": wpT,
                        "bp": bp, "convw": convw, "convb": convb,
                        "wg": wg, "wu": wu, "wd": wd})
    res = run_bass_kernel_spmd(nc, in_maps, core_ids=list(range(8)))
    out = np.concatenate([res.results[c]["out"].reshape(per, SL, D) for c in range(8)], axis=0)
    return np.ascontiguousarray(out.astype(np.float32))


# version 33
# speedup vs baseline: 1.0160x; 1.0100x over previous
import os
import numpy as np
from contextlib import ExitStack

import concourse.bass as bass
import concourse.mybir as mybir
from concourse.bass_utils import run_bass_kernel_spmd

F32 = mybir.dt.float32
BF16 = mybir.dt.bfloat16
I16 = mybir.dt.int16
AF = mybir.ActivationFunctionType
ALU = mybir.AluOpType
AX = mybir.AxisListType

P = 128
D = 1024
DFF = 2816
KC = D // P
DC = DFF // P
HEADS = 16
HD = 64
NPAIR = 8
EPS = 1e-5
SEGS = [(0, 768), (768, 768), (1536, 768), (2304, 512)]

_SIM = os.environ.get("BASS_SIM_FP64") == "1"
CB = 0.5 if _SIM else 0.0
_TRACE = []
_LAST_EM = None


class Emit:

    DMA_SEMS = ("DW", "DX", "DQ", "DO", "DY")

    def __init__(self):
        self.ops = {e: [] for e in ("T", "V", "A", "G", "S")}
        self.cnt = {s: 0 for s in ("T", "V", "A", "G") + self.DMA_SEMS}
        self.vc = {e: {} for e in self.ops}
        self.buf = {}
        self.sched = None

    def op(self, eng, fn, reads=(), writes=(), dma_sem=None, cost=None):
        if self.sched is not None:
            self.sched.append((eng, fn, tuple(reads), tuple(writes), dma_sem, cost))
            return
        self._op(eng, fn, reads, writes, dma_sem)

    def begin_sched(self):
        assert self.sched is None
        if os.environ.get("KSCHED") == "0":
            return
        self.sched = []

    def end_sched(self):
        if self.sched is None:
            return
        ops, self.sched = self.sched, None
        for k in _list_schedule(ops):
            eng, fn, reads, writes, dma_sem, _ = ops[k]
            self._op(eng, fn, reads, writes, dma_sem)

    def _op(self, eng, fn, reads=(), writes=(), dma_sem=None):
        if os.environ.get("KTRACE") == "1":
            _TRACE.append((eng, dma_sem, tuple(reads), tuple(writes)))
        sem = dma_sem if dma_sem else eng
        inc = 16 if dma_sem else 1
        if dma_sem:
            reads = list(reads) + [f"_tok_{sem}"]
            writes = list(writes) + [f"_tok_{sem}"]
        needed = {}
        for b in reads:
            st = self.buf.get(b)
            if st and st["w"]:
                s, c = st["w"]
                needed[s] = max(needed.get(s, 0), c)
        for b in writes:
            st = self.buf.get(b)
            if st:
                if st["w"]:
                    s, c = st["w"]
                    needed[s] = max(needed.get(s, 0), c)
                for s, c in st["r"].items():
                    needed[s] = max(needed.get(s, 0), c)
        waits = []
        vc = self.vc[eng]
        for s, c in sorted(needed.items()):
            if eng == "T" and s == "T":
                continue
            if vc.get(s, 0) < c:
                waits.append((s, c))
                vc[s] = c
        self.cnt[sem] += inc
        cnt = self.cnt[sem]
        self.ops[eng].append((waits, fn, sem, inc))
        for b in reads:
            st = self.buf.setdefault(b, {"w": None, "r": {}})
            st["r"][sem] = max(st["r"].get(sem, 0), cnt)
        for b in writes:
            self.buf[b] = {"w": (sem, cnt), "r": {}}

    def barrier(self):
        assert self.sched is None
        snapshot = dict(self.cnt)
        for e in self.ops:
            vc = self.vc[e]
            waits = []
            for s, c in sorted(snapshot.items()):
                if e == "T" and s == "T":
                    continue
                if c > 0 and vc.get(s, 0) < c:
                    waits.append((s, c))
                    vc[s] = c
            if waits:
                self.ops[e].append((waits, None, None, 0))


_ENG_COST = {"T": 260, "V": 700, "A": 720, "G": 1900, "S": 150}


def _list_schedule(ops):
    n = len(ops)
    busy = [0.0] * n
    lat = [0.0] * n
    for i, (eng, fn, rd, wr, dsem, cost) in enumerate(ops):
        if dsem:
            busy[i] = 150.0
            lat[i] = float(cost) if cost is not None else 4000.0
        else:
            busy[i] = float(cost) if cost is not None else _ENG_COST[eng]
            lat[i] = busy[i] + 100.0
    last_w = {}
    readers = {}
    preds = [None] * n
    succ = [[] for _ in range(n)]
    indeg = [0] * n
    for i, (eng, fn, rd, wr, dsem, cost) in enumerate(ops):
        r = list(rd)
        w = list(wr)
        if dsem:
            tok = "_tok_" + dsem
            r.append(tok)
            w.append(tok)
        dset = set()
        for b in r:
            j = last_w.get(b)
            if j is not None:
                dset.add(j)
        for b in w:
            j = last_w.get(b)
            if j is not None:
                dset.add(j)
            dset.update(readers.get(b, ()))
        dset.discard(i)
        preds[i] = dset
        for j in dset:
            succ[j].append(i)
        indeg[i] = len(dset)
        for b in r:
            readers.setdefault(b, set()).add(i)
        for b in w:
            last_w[b] = i
            readers[b] = set()
    prio = [0.0] * n
    for i in range(n - 1, -1, -1):
        m = 0.0
        for j in succ[i]:
            if prio[j] > m:
                m = prio[j]
        prio[i] = m + lat[i]
    eng_t = {e: 0.0 for e in _ENG_COST}
    avail = [0.0] * n
    ready = [i for i in range(n) if indeg[i] == 0]
    order = []
    while ready:
        best = -1
        best_key = None
        best_est = 0.0
        m = None
        ests = []
        for i in ready:
            est = eng_t[ops[i][0]]
            for j in preds[i]:
                if avail[j] > est:
                    est = avail[j]
            ests.append(est)
            if m is None or est < m:
                m = est
        for i, est in zip(ready, ests):
            if est <= m + 400.0:
                key = (-prio[i], i)
                if best < 0 or key < best_key:
                    best, best_key, best_est = i, key, est
        i = best
        eng_t[ops[i][0]] = best_est + busy[i]
        avail[i] = best_est + lat[i]
        order.append(i)
        ready.remove(i)
        for j in succ[i]:
            indeg[j] -= 1
            if indeg[j] == 0:
                ready.append(j)
    assert len(order) == n
    return order


def build_program(seq_len=2048, n_seq=2, _phases=("prep", "p1", "p2"), _dbg=False):
    S = seq_len * n_seq
    NT = S // P
    ST = seq_len // P
    assert seq_len % P == 0 and ST >= 2
    _ik = "ExternalOutput" if _dbg else "Internal"

    nc = bass.Bass("TRN2", target_bir_lowering=False, debug=False)

    x_d = nc.dram_tensor("x", [S, D], F32, kind="ExternalInput").ap()
    wqT_d = nc.dram_tensor("wqT", [D, D], F32, kind="ExternalInput").ap()
    wkT_d = nc.dram_tensor("wkT", [D, D], F32, kind="ExternalInput").ap()
    wvT_d = nc.dram_tensor("wvT", [D, D], F32, kind="ExternalInput").ap()
    wpT_d = nc.dram_tensor("wpT", [D, D], F32, kind="ExternalInput").ap()
    bp_d = nc.dram_tensor("bp", [1, D], F32, kind="ExternalInput").ap()
    convw_d = nc.dram_tensor("convw", [D, 3], F32, kind="ExternalInput").ap()
    convb_d = nc.dram_tensor("convb", [P, KC], F32, kind="ExternalInput").ap()
    wg_d = nc.dram_tensor("wg", [DFF, D], F32, kind="ExternalInput").ap()
    wu_d = nc.dram_tensor("wu", [DFF, D], F32, kind="ExternalInput").ap()
    wd_d = nc.dram_tensor("wd", [D, DFF], F32, kind="ExternalInput").ap()
    out_d = nc.dram_tensor("out", [S, D], F32, kind="ExternalOutput").ap()
    kv_dram = nc.dram_tensor("kv_scratch", [n_seq, P, NPAIR * P], BF16, kind=_ik).ap()
    xnfm_d = nc.dram_tensor("xnfm_d", [NT, P, D], BF16, kind=_ik).ap()
    xqfm_d = nc.dram_tensor("xqfm_d", [NT, P, D], BF16, kind=_ik).ap()
    qfm_d = nc.dram_tensor("qfm_d", [NT, P, D], BF16, kind=_ik).ap()
    wgq_d = nc.dram_tensor("wgq_d", [P, KC * DFF], BF16, kind=_ik).ap()
    wuq_d = nc.dram_tensor("wuq_d", [P, KC * DFF], BF16, kind=_ik).ap()
    wdq_d = nc.dram_tensor("wdq_d", [P, DC * D], BF16, kind=_ik).ap()
    wgq_v = wgq_d.rearrange("p (k f) -> p k f", f=DFF)
    wuq_v = wuq_d.rearrange("p (k f) -> p k f", f=DFF)
    wdq_v = wdq_d.rearrange("p (k f) -> p k f", f=D)
    if _dbg:
        xn_dbg = nc.dram_tensor("xn_dbg", [NT, P, D], F32, kind="ExternalOutput").ap()
        u1_dbg = nc.dram_tensor("u1_dbg", [NT, P, D], BF16, kind="ExternalOutput").ap()
        x_dbg = nc.dram_tensor("x_dbg", [NT, P, D], F32, kind="ExternalOutput").ap()
        st_dbg = nc.dram_tensor("st_dbg", [P, 2 * NT], F32, kind="ExternalOutput").ap()

    x_t = x_d.rearrange("(t p) d -> t p d", p=P)
    out_t = out_d.rearrange("(t p) d -> t p d", p=P)

    em = Emit()
    global _LAST_EM
    _LAST_EM = em
    E = em.op

    def dma(dst, src, rb, wb, sem="DW", eng="S", cost=None):
        E(eng, lambda e, d_=dst, s_=src: e.dma_start(out=d_, in_=s_), reads=rb, writes=wb,
          dma_sem=sem, cost=cost)

    with ExitStack() as ctx:
        sb = lambda name, shape, dt=F32: ctx.enter_context(nc.sbuf_tensor(name, shape, dt))
        psa = lambda name, shape, dt=F32: ctx.enter_context(nc.psum_tensor(name, shape, dt))

        ident_b = sb("ident_b", [P, P], BF16)
        ones_col = sb("ones_col", [P, 1], BF16)
        ones_row = sb("ones_row", [1, P], BF16)
        bp_b = sb("bp_b", [1, D], BF16)
        convw_b = sb("convw_b", [P, KC, 3], BF16)
        convb_b = sb("convb_b", [P, KC], BF16)
        qv = sb("qv", [P, D], F32)
        u1 = sb("u1", [P, D], BF16)
        u2am = sb("u2am", [P, D], BF16)
        fA = sb("fA", [P, D], F32)
        fB = sb("fB", [P, D], F32)
        bscale = sb("bscale", [P, 112], F32)
        brs = sb("brs", [P, 128], F32)
        psT = psa("psT", [P, KC, P], BF16)
        rden = brs[:, 112:128]
        kvblk = sb("kvblk", [P, NPAIR, P], BF16)
        KSp = sb("KSp", [P, n_seq, NPAIR, 2], BF16)
        wB = sb("wB", [P, KC, D], BF16)

        def quant_ops(src_ap, src_buf, out_ap, out_buf, width, par, scratch=None,
                      sgn_pre=None, g_off=False, force_wide=False):
            nb = width // 16
            wide = force_wide or width > 512
            if scratch is None:
                scratch = ((qv, "qv0", "qv1"), (u1, "u10", "u11"), (u2am, "u20", "u21"))
            (qvt, qn0, qn1), (u1t, un0, un1), (u2t, mn0, mn1) = scratch
            if wide:
                assert width <= 1024 and par == 0
                o = 0
                ob = 0
                qb, ub1, ub2 = [qn0, qn1], [un0, un1], [mn0, mn1]
            else:
                o = par * 512
                ob = par * 32
                qb, ub1, ub2 = [qn0 if par == 0 else qn1], \
                               [un0 if par == 0 else un1], [mn0 if par == 0 else mn1]
            sv = src_ap.rearrange("p (b k) -> p b k", k=16)
            qvp = qvt[:, o:o + width]
            u1p = u1t[:, o:o + width]
            u1i = u1p.bitcast(I16)
            u2p = u2t[:, o:o + width]
            if wide:
                bsc = bscale[:, 64:64 + nb]
                brp = brs[:, 64:64 + nb]
            else:
                bsc = bscale[:, ob:ob + nb]
                brp = brs[:, ob:ob + nb]
            rs_b = brp.unsqueeze(2).broadcast_to([P, nb, 16])
            sc_b = bsc.unsqueeze(2).broadcast_to([P, nb, 16])
            bb, rb = ("bscW", "brsW") if wide else (f"bsc{par}", f"brs{par}")
            ob_ = list(out_buf) if isinstance(out_buf, (tuple, list)) else [out_buf]
            wv = 160 + int(1.12 * width)
            wa = 230 + width
            wg = (160 + int(2.2 * width)) if g_off else wv
            GV = "G" if g_off else "V"
            if _SIM:
                ops = [
                    ("V", lambda e: e.tensor_reduce(bsc, sv, AX.X, ALU.max,
                                                    apply_absolute_value=True),
                     [src_buf, bb], [bb]),
                    ("A", lambda e: e.activation(qvp, src_ap, AF.Abs), [src_buf] + qb, qb),
                    ("V", lambda e: e.tensor_scalar(bsc, bsc, 1.0 / 6.0, 1e-12,
                                                    ALU.mult, ALU.max), [bb], [bb]),
                    ("V", lambda e: e.reciprocal(brp, bsc), [bb, rb], [rb]),
                    (GV,
                     lambda e: e.tensor_tensor(qvp.rearrange("p (b k) -> p b k", k=16),
                                               qvp.rearrange("p (b k) -> p b k", k=16),
                                               rs_b, ALU.mult), qb + [rb], qb),
                    ("V", lambda e: e.tensor_scalar(out_ap, qvp, 5.0, -2.0,
                                                    ALU.is_le, ALU.mult), qb + ob_, ob_),
                    ("A", lambda e: e.activation(u1i, qvp, AF.Copy, bias=CB, scale=2.0),
                     qb + ub1, ub1),
                    ("V", lambda e: e.tensor_scalar(u2p, u1i, 4.0, 0.5, ALU.min, ALU.mult),
                     ub1, ub2),
                    ("A", lambda e: e.activation(u1i, qvp, AF.Copy, bias=CB, scale=1.0),
                     qb + ub1, ub1),
                    ("V", lambda e: e.tensor_scalar(u1p, u1i, 2.0, 4.0, ALU.max, ALU.min),
                     ub1, ub1),
                    (GV, lambda e: e.tensor_add(u2p, u2p, u1p), ub2 + ub1, ub2),
                    (GV, lambda e: e.tensor_add(u2p, u2p, out_ap), ub2 + ob_, ub2),
                ]
                cst = [wv, wa, 300, 400, wg, wv, wa, wv, wa, wv, wg, wg]
                if sgn_pre is None:
                    ops.append(("A", lambda e: e.activation(out_ap, src_ap, AF.Sign),
                                [src_buf] + ob_, ob_))
                    ops.append((GV, lambda e: e.tensor_tensor(u2p, u2p, out_ap, ALU.mult),
                                ub2 + ob_, ub2))
                    cst += [wa, wg]
                else:
                    sga, sgb = sgn_pre
                    ops.append(("V", lambda e: e.tensor_tensor(u2p, u2p, sga, ALU.mult),
                                ub2 + [sgb], ub2))
                    cst += [wv]
            else:
                C2 = float(1.5 * 2 ** 21)
                C3 = float(1.5 * 2 ** 22)
                ops = [
                    ("V", lambda e: e.tensor_reduce(bsc, sv, AX.X, ALU.max,
                                                    apply_absolute_value=True),
                     [src_buf, bb], [bb]),
                    ("V", lambda e: e.tensor_scalar(
                        qvp.bitcast(mybir.dt.int32), src_ap.bitcast(mybir.dt.int32),
                        0x7FFFFFFF, None, ALU.bitwise_and), [src_buf] + qb, qb),
                    ("V", lambda e: e.tensor_scalar(bsc, bsc, 1.0 / 3.0, 2e-12,
                                                    ALU.mult, ALU.max), [bb], [bb]),
                    ("V", lambda e: e.reciprocal(brp, bsc), [bb, rb], [rb]),
                    (GV,
                     lambda e: e.tensor_tensor(qvp.rearrange("p (b k) -> p b k", k=16),
                                               qvp.rearrange("p (b k) -> p b k", k=16),
                                               rs_b, ALU.mult), qb + [rb], qb),
                    ("V", lambda e: e.tensor_scalar(u1p, qvp, C2, C2,
                                                    ALU.add, ALU.subtract), qb + ub1, ub1),
                    ("V", lambda e: e.tensor_scalar(u2p, qvp, C3, C3 + 1.0,
                                                    ALU.add, ALU.subtract), qb + ub2, ub2),
                    ("V", lambda e: e.tensor_scalar(u2p, u2p, 1.0, 0.0,
                                                    ALU.min, ALU.max), ub2, ub2),
                    ("V", lambda e: e.scalar_tensor_tensor(u2p, u1p, 1.0, u2p,
                                                           ALU.min, ALU.add),
                     ub1 + ub2, ub2),
                    ("V", lambda e: e.scalar_tensor_tensor(u2p, qvp, 2.5, u2p,
                                                           ALU.is_gt, ALU.add),
                     qb + ub2, ub2),
                ]
                cst = [wv, wv, 300, 400, wg, wv, wv, wv, wv, wv]
                if sgn_pre is None:
                    ops.append(("A", lambda e: e.activation(out_ap, src_ap, AF.Sign),
                                [src_buf] + ob_, ob_))
                    ops.append((GV, lambda e: e.tensor_tensor(u2p, u2p, out_ap, ALU.mult),
                                ub2 + ob_, ub2))
                    cst += [wa, wg]
                else:
                    sga, sgb = sgn_pre
                    ops.append((GV, lambda e: e.tensor_tensor(u2p, u2p, sga, ALU.mult),
                                ub2 + [sgb], ub2))
                    cst += [wg]
            ops.append((GV, lambda e: e.tensor_tensor(
                out_ap.rearrange("p (b k) -> p b k", k=16),
                u2p.rearrange("p (b k) -> p b k", k=16), sc_b, ALU.mult),
                ub2 + [bb], ob_))
            cst += [wg]
            return [t + (c,) for t, c in zip(ops, cst)]

        def emit_list(ops):
            for t in ops:
                E(t[0], t[1], reads=t[2], writes=t[3],
                  cost=(t[4] if len(t) > 4 else None))

        def emit_zip(la, lb):
            i = j = 0
            while i < len(la) or j < len(lb):
                if i < len(la):
                    t = la[i]
                    E(t[0], t[1], reads=t[2], writes=t[3],
                      cost=(t[4] if len(t) > 4 else None))
                    i += 1
                if j < len(lb):
                    t = lb[j]
                    E(t[0], t[1], reads=t[2], writes=t[3],
                      cost=(t[4] if len(t) > 4 else None))
                    j += 1

        def quant_wide(src_ap, src_buf, out_ap, out_names, width, scratch=None,
                       g_off=False):
            nA, nB = out_names
            if width <= 512:
                emit_list(quant_ops(src_ap, src_buf, out_ap, nA, width, 0, scratch,
                                    g_off=g_off))
                return
            w0 = 512
            w1 = width - 512
            emit_zip(quant_ops(src_ap[:, 0:w0], src_buf, out_ap[:, 0:w0], nA, w0, 0,
                               scratch, g_off=g_off),
                     quant_ops(src_ap[:, w0:width], src_buf, out_ap[:, w0:width],
                               nB, w1, 1, scratch, g_off=g_off))

        def transpose_group(src_ap_fn, nchunks, dst_t, dst_buf, loc_fuse=None,
                            evac="V", dst_off=0, src_bufs=()):
            rsrc = list(src_bufs)
            step = 4 if os.environ.get("KTS", "0") == "1" else KC
            for j in range(0, nchunks, step):
                m = min(step, nchunks - j)
                hf = (j // 4) % 2 if step == 4 else 0
                pb = f"psT{hf}"
                for c in range(m):
                    E("T", lambda e, jj=j + c, cc=4 * hf + c: e.transpose(
                        psT[:, cc, :], src_ap_fn(jj), ident_b[:]),
                      reads=rsrc + ["ident", pb], writes=[pb])
                if loc_fuse is None:
                    if evac == "A":
                        E("A", lambda e, jj=j + dst_off, mm=m, h_=hf: e.activation(
                            dst_t[:, jj:jj + mm, :].rearrange("p c q -> p (c q)"),
                            psT[:, 4 * h_:4 * h_ + mm, :].rearrange("p c q -> p (c q)"),
                            AF.Copy),
                          reads=[pb, dst_buf], writes=[dst_buf], cost=230 + m * P)
                    else:
                        E("V", lambda e, jj=j + dst_off, mm=m, h_=hf: e.tensor_copy(
                            dst_t[:, jj:jj + mm, :].rearrange("p c q -> p (c q)"),
                            psT[:, 4 * h_:4 * h_ + mm, :].rearrange("p c q -> p (c q)")),
                          reads=[pb], writes=[dst_buf], cost=160 + m * P)
                else:
                    loc_t, loc_buf = loc_fuse
                    E("V", lambda e, jj=j, mm=m, h_=hf: e.scalar_tensor_tensor(
                        dst_t[:, jj:jj + mm, :].rearrange("p c q -> p (c q)"),
                        psT[:, 4 * h_:4 * h_ + mm, :].rearrange("p c q -> p (c q)"), 1.0,
                        loc_t[:, jj:jj + mm, :].rearrange("p c q -> p (c q)"),
                        ALU.mult, ALU.add),
                      reads=[pb, loc_buf], writes=[dst_buf], cost=160 + m * P)

        E("G", lambda e: e.memset(ident_b[:], 0.0), writes=["ident"])
        E("G", lambda e: e.affine_select(out=ident_b[:], in_=ident_b[:],
                                         compare_op=ALU.not_equal, fill=1.0, base=0,
                                         pattern=[[-1, P]], channel_multiplier=1),
          reads=["ident"], writes=["ident"])
        E("G", lambda e: e.memset(ones_col[:], 1.0), writes=["ones"])
        E("G", lambda e: e.memset(ones_row[:], 1.0), writes=["ones_r"])

        with ExitStack() as pctx:
            sbp = lambda name, shape, dt: pctx.enter_context(nc.sbuf_tensor(name, shape, dt))
            bp_f32 = sbp("bp_f32", [1, D], F32)
            convw_sb = sbp("convw_sb", [P, KC, 3], F32)
            convb_sb = sbp("convb_sb", [P, KC], F32)

            dma(bp_f32[:], bp_d[:], [], ["bp_f32"])
            E("V", lambda e: e.tensor_copy(bp_b[:], bp_f32[:]), reads=["bp_f32"], writes=["bp_b"])
            dma(convw_sb[:], convw_d.rearrange("(c p) k -> p c k", p=P), [], ["convw"])
            dma(convb_sb[:], convb_d[:], [], ["convb"])
            E("V", lambda e: e.tensor_copy(convw_b[:], convw_sb[:]),
              reads=["convw"], writes=["convw_b"])
            E("V", lambda e: e.tensor_copy(convb_b[:], convb_sb[:]),
              reads=["convb"], writes=["convb_b"])

        em.barrier()

        with ExitStack() as p1ctx:
            sb1 = lambda name, shape, dt: p1ctx.enter_context(nc.sbuf_tensor(name, shape, dt))
            ps1 = lambda name, shape, dt: p1ctx.enter_context(nc.psum_tensor(name, shape, dt))
            wA = sb1("wA", [P, KC, D], BF16)
            wC = sb1("wC", [P, KC, D], BF16)
            stats = sb1("stats", [P, 2 * NT], F32)
            stscr = sb1("stscr", [P, 8 * NT], F32)
            x_sb = sb1("x_sb", [P, 2, D], F32)
            xn_tm = sb1("xn_tm", [P, 2, D], F32)
            u1p1 = sb1("u1p1", [P, 2, D], BF16)
            qe = sb1("qe", [P, 2, D], BF16)
            xq_sb = sb1("xq_sb", [P, 2, D], BF16)
            qs1 = sb1("qs1", [P, 2, D], BF16)
            eluA = sb1("eluA", [P, 2, D], F32)
            eluB = sb1("eluB", [P, 2, D], F32)
            fC = sb1("fC", [P, D], F32)
            fD = sb1("fD", [P, D], F32)
            xnfm_s = sb1("xnfm_s", [P, 2, KC, P], BF16)
            xqfm_s = sb1("xqfm_s", [P, 2, KC, P], BF16)
            qfm_s = sb1("qfm_s", [P, 2, KC, P], BF16)
            k_tm = sb1("k_tm", [P, 2, D], BF16)
            v_tm = sb1("v_tm", [P, 2, D], BF16)
            ks_sb = sb1("ks_sb", [P, NPAIR], F32)
            wstage = sb1("wstage", [P, 2, D], F32)
            wqchunk = sb1("wqchunk", [P, 2, D], BF16)
            wstg_out = sb1("wstg_out", [P, 2, KC, P], BF16)
            wload = sb1("wload", [P, 2, D], F32)
            psK = ps1("psK", [P, D], F32)
            psV = ps1("psV", [P, D], F32)
            psKV = ps1("psKV", [P, NPAIR, P], F32)
            psKS = ps1("psKS", [P, NPAIR], F32)

            wjobs = [(wkT_d, wA, "wA"), (wvT_d, wB, "wB"), (wqT_d, wC, "wC")]
            for jx, (w_ap, dst, dbuf) in enumerate(wjobs):
                for c in range(KC):
                    sl = (jx * KC + c) % 2
                    sem, eng = ("DW", "S") if sl == 0 else ("DY", "G")
                    dma(wload[:, sl, :], w_ap[c * P:(c + 1) * P, :],
                        [f"wload{sl}"], [f"wload{sl}"], sem=sem, eng=eng, cost=3000)
                    E("V", lambda e, cc=c, dd=dst, s_=sl: e.tensor_copy(
                        dd[:, cc, :], wload[:, s_, :]),
                      reads=[f"wload{sl}"], writes=[dbuf], cost=1000)

            def prep_task(w_dram, c, s0, sw, dst_v, ti):
                pr = ti % 2
                wst = wstage[:, pr, 0:sw]
                wqc = wqchunk[:, pr, :]
                dma(wst, w_dram[c * P:(c + 1) * P, s0:s0 + sw],
                    [f"wstage{pr}"], [f"wstage{pr}"], cost=5000)
                quant_wide(wst, f"wstage{pr}", wqc[:, 0:sw],
                           (f"wqcA{pr}", f"wqcB{pr}"), sw, g_off=True)
                g0 = s0 // P
                for j in range(0, sw // P, KC):
                    m = min(KC, sw // P - j)
                    transpose_group(
                        lambda jj, p_=pr, b_=j: wqchunk[:, p_, (b_ + jj) * P:(b_ + jj + 1) * P],
                        m, wstg_out[:, pr], f"wstg{pr}", evac="A",
                        src_bufs=(f"wqcA{pr}", f"wqcB{pr}"))
                    dma(dst_v[:, g0 + j:g0 + j + m, c * P:(c + 1) * P],
                        wstg_out[:, pr, 0:m, :],
                        [f"wstg{pr}"], [f"wstg{pr}", "wq_dram"], sem="DY", eng="G",
                        cost=3000)

            prep_tasks = []
            if "prep" in _phases:
                ti = 0
                for (w_dram, nrc, width, dst_v) in (
                        (wg_d, DC, D, wgq_v), (wu_d, DC, D, wuq_v),
                        (wd_d, KC, DFF, wdq_v)):
                    for c in range(nrc):
                        for s0 in range(0, width, 1024):
                            sw = min(1024, width - s0)
                            prep_tasks.append(
                                lambda w_=w_dram, c_=c, s_=s0, sw_=sw, d_=dst_v, t_=ti:
                                prep_task(w_, c_, s_, sw_, d_, t_))
                            ti += 1
            ntask = len(prep_tasks)
            tix = 0

            def drain_prep(n):
                nonlocal tix
                for _ in range(n):
                    if tix < ntask:
                        prep_tasks[tix]()
                        tix += 1

            nt1 = NT if "p1" in _phases else 0
            for i in range(nt1):
                seq = i // ST
                first = (i % ST == 0)
                last = (i % ST == ST - 1)
                if i % 16 == 0:
                    em.begin_sched()
                sp = i % 2
                stt = stscr[:, 8 * i:8 * i + 8]
                nmS = lambda s, sp_=sp: s + str(sp_)
                xv = x_sb[:, sp, :]
                xnv = xn_tm[:, sp, :]
                xbuf = nmS("x")
                xnbuf = nmS("xn")
                ksink = kvblk[:].rearrange("p a b -> p (a b)")
                dma(xv, x_t[i], [xbuf], [xbuf], sem="DX", eng="A", cost=5000)
                E("A", lambda e, st_=stt, x_=xv: e.activation(ksink, x_, AF.Square,
                                                       accum_out=st_[:, 0:1]),
                  reads=[xbuf, "kvblk", "brs"], writes=["kvblk", nmS("st_sq")], cost=1400)
                E("A", lambda e, st_=stt, x_=xv: e.activation(ksink, x_, AF.Copy,
                                                       accum_out=st_[:, 1:2]),
                  reads=[xbuf, "kvblk"], writes=["kvblk", nmS("st_sum")], cost=1400)
                E("V", lambda e, st_=stt: e.tensor_scalar(st_[:, 2:3], st_[:, 1:2],
                                                          1.0 / D, None, ALU.mult),
                  reads=[nmS("st_sum")], writes=[nmS("st_m")], cost=250)
                E("V", lambda e, st_=stt: e.tensor_tensor(st_[:, 3:4], st_[:, 2:3],
                                                          st_[:, 2:3], ALU.mult),
                  reads=[nmS("st_m")], writes=[nmS("st_m2")], cost=250)
                E("V", lambda e, st_=stt: e.scalar_tensor_tensor(
                    st_[:, 4:5], st_[:, 0:1], 1.0 / D, st_[:, 3:4],
                    ALU.mult, ALU.subtract),
                  reads=[nmS("st_sq"), nmS("st_m2")], writes=[nmS("st_u")], cost=250)
                E("V", lambda e, st_=stt: e.tensor_scalar(st_[:, 4:5], st_[:, 4:5], EPS,
                                                          None, ALU.add),
                  reads=[nmS("st_u")], writes=[nmS("st_u")], cost=250)
                E("A", lambda e, st_=stt: e.activation(st_[:, 5:6], st_[:, 4:5],
                                                       AF.Sqrt),
                  reads=[nmS("st_u")], writes=[nmS("st_sd")], cost=260)
                E("V", lambda e, ii=i, st_=stt: e.reciprocal(
                    stats[:, 2 * ii:2 * ii + 1], st_[:, 5:6]),
                  reads=[nmS("st_sd"), "stats"], writes=["stats"], cost=250)
                E("V", lambda e, ii=i, st_=stt: e.scalar_tensor_tensor(
                    stats[:, 2 * ii + 1:2 * ii + 2], st_[:, 2:3], -1.0,
                    stats[:, 2 * ii:2 * ii + 1], ALU.mult, ALU.mult),
                  reads=[nmS("st_m"), "stats"], writes=["stats"], cost=250)
                E("A", lambda e, ii=i, xn_=xnv, x_=xv: e.activation(
                    xn_, x_, AF.Identity,
                    bias=stats[:, 2 * ii + 1:2 * ii + 2], scale=stats[:, 2 * ii:2 * ii + 1]),
                  reads=[xbuf, "stats", xnbuf], writes=[xnbuf], cost=1400)
                u1v = u1p1[:, sp, :]
                u1b = nmS("u1p1")
                xnfmv = xnfm_s[:, sp, :, :]
                xnfmb = nmS("xnfm_s")
                ktv = k_tm[:, sp, :]
                ktb = nmS("k_tm")
                vtv = v_tm[:, sp, :]
                vtb = nmS("v_tm")
                qev = qe[:, sp, :]
                qeb = nmS("qe")
                qfmv = qfm_s[:, sp, :, :]
                qfmb = nmS("qfm_s")
                xqv_ = xq_sb[:, sp, :]
                xqfmv = xqfm_s[:, sp, :, :]
                xqfmb = nmS("xqfm_s")
                eAv = eluA[:, sp, :]
                eAb = nmS("eluA")
                eBv = eluB[:, sp, :]
                eBb = nmS("eluB")
                qsc = ((fA, "fA", "fA"), (qs1[:, 0, :], "qs1A0", "qs1B0"),
                       (fB, "fB", "fB")) if sp == 0 else                       ((fC, "fC", "fC"), (qs1[:, 1, :], "qs1A1", "qs1B1"),
                       (fD, "fD", "fD"))
                E("V", lambda e, xn_=xnv, u_=u1v, ub=u1b: e.tensor_copy(u_, xn_),
                  reads=[xnbuf, u1b], writes=[u1b], cost=1000)
                if _dbg:
                    dma(xn_dbg[i], xnv, [xnbuf], ["xdbg"], sem="DO", eng="S")
                    dma(u1_dbg[i], u1v, [u1b], ["udbg"], sem="DO", eng="S")
                    dma(x_dbg[i], xv, [xbuf], ["xdbg2"], sem="DO", eng="S")
                transpose_group(lambda j, u_=u1v: u_[:, j * P:(j + 1) * P], KC, xnfmv,
                                xnfmb, evac="A", src_bufs=(u1b,))
                dma(xnfm_d[i], xnfmv.rearrange("p c q -> p (c q)"),
                    [xnfmb], ["xnfm_dram"], sem="DY", eng="G", cost=3000)
                quant_wide(xnv, xnbuf, xqv_, (nmS("xqA"), nmS("xqB")), D,
                           scratch=qsc, g_off=True)
                transpose_group(lambda j, x_=xqv_: x_[:, j * P:(j + 1) * P], KC, xqfmv,
                                xqfmb, evac="A", src_bufs=(nmS("xqA"), nmS("xqB")))
                dma(xqfm_d[i], xqfmv.rearrange("p c q -> p (c q)"),
                    [xqfmb], ["xqfm_dram"], sem="DY", eng="G", cost=3000)
                for kc in range(KC):
                    for (ps, wsrc, wbuf, pbuf) in ((psK, wA, "wA", "psK"),
                                                   (psV, wB, "wB", "psV")):
                        for oc in range(2):
                            E("T", lambda e, k_=kc, o_=oc, p_=ps, w_=wsrc, x_=xnfmv:
                                e.matmul(
                                    p_[:, o_ * 512:(o_ + 1) * 512], x_[:, k_, :],
                                    w_[:, k_, o_ * 512:(o_ + 1) * 512],
                                    start=(k_ == 0), stop=(k_ == KC - 1)),
                              reads=[xnfmb, wbuf, pbuf], writes=[pbuf])
                E("V", lambda e, v_=vtv: e.tensor_copy(v_, psV[:]),
                  reads=["psV", vtb], writes=[vtb], cost=1100)
                E("A", lambda e, a_=eAv: e.activation(a_, psK[:], AF.Relu, scale=-1.0),
                  reads=["psK", eAb], writes=[eAb], cost=1400)
                E("A", lambda e, a_=eAv, b_=eBv: e.activation(b_, a_, AF.Exp, scale=-1.0),
                  reads=[eAb, eBb], writes=[eBb], cost=1400)
                E("V", lambda e, k_=ktv, b_=eBv: e.scalar_tensor_tensor(
                    k_, psK[:], 1.0, b_, ALU.add, ALU.max),
                  reads=["psK", eBb, ktb], writes=[ktb], cost=1300)
                for oc in range(KC):
                    for kc in range(KC):
                        E("T", lambda e, k_=kc, o_=oc, x_=xnfmv: e.matmul(
                            psV[:, o_ * P:(o_ + 1) * P],
                            wC[:, k_, o_ * P:(o_ + 1) * P], x_[:, k_, :],
                            start=(k_ == 0), stop=(k_ == KC - 1)),
                          reads=[xnfmb, "wC", "psV"], writes=["psV"])
                E("A", lambda e, a_=eAv: e.activation(a_, psV[:], AF.Relu, scale=-1.0),
                  reads=["psV", eAb], writes=[eAb], cost=1400)
                E("A", lambda e, a_=eAv, b_=eBv: e.activation(b_, a_, AF.Exp, scale=-1.0),
                  reads=[eAb, eBb], writes=[eBb], cost=1400)
                E("V", lambda e, q_=qfmv, b_=eBv: e.scalar_tensor_tensor(
                    q_.rearrange("p c q -> p (c q)"), psV[:], 1.0, b_,
                    ALU.add, ALU.max),
                  reads=["psV", eBb, qfmb], writes=[qfmb], cost=1300)
                for pair in range(NPAIR):
                    E("T", lambda e, p_=pair, f_=first, l_=last, k_=ktv, v_=vtv:
                        e.matmul(
                            psKV[:, p_, :], k_[:, p_ * P:(p_ + 1) * P],
                            v_[:, p_ * P:(p_ + 1) * P],
                            start=(f_ and p_ in (0, 4)), stop=(l_ and p_ in (3, 7))),
                      reads=[ktb, vtb, "psKV"], writes=["psKV"])
                    E("T", lambda e, p_=pair, f_=first, l_=last, k_=ktv: e.matmul(
                        psKS[:, p_:p_ + 1], k_[:, p_ * P:(p_ + 1) * P], ones_col[:],
                        start=(f_ and p_ == 0), stop=(l_ and p_ == NPAIR - 1)),
                      reads=[ktb, "ones", "psKS"], writes=["psKS"])
                dma(qfm_d[i], qfmv.rearrange("p c q -> p (c q)"),
                    [qfmb], ["qfm_dram"], sem="DY", eng="G", cost=3000)
                if last:
                    E("A", lambda e: e.activation(ks_sb[:], psKS[:], AF.Copy),
                      reads=["psKS", "ks_sb"], writes=["ks_sb"])
                    for pair in range(NPAIR):
                        E("A", lambda e, p_=pair: e.activation(
                            kvblk[:, p_, :], psKV[:, p_, :], AF.Copy),
                          reads=["psKV", "kvblk"], writes=["kvblk"])
                        E("V", lambda e, p_=pair: e.memset(kvblk[0:64, p_, 64:128], 0.0),
                          reads=["kvblk"], writes=["kvblk"])
                        E("V", lambda e, p_=pair: e.memset(kvblk[64:128, p_, 0:64], 0.0),
                          reads=["kvblk"], writes=["kvblk"])
                    dma(kv_dram[seq], kvblk[:].rearrange("p a b -> p (a b)"),
                        ["kvblk"], ["kv_dram"], sem="DY", eng="G")
                    E("V", lambda e, s_=seq: e.tensor_copy(KSp[:, s_, :, 0], ks_sb[:]),
                      reads=["ks_sb", "KSp"], writes=["KSp"])
                    E("V", lambda e, s_=seq: e.tensor_copy(KSp[:, s_, :, 1], ks_sb[:]),
                      reads=["ks_sb", "KSp"], writes=["KSp"])
                    E("V", lambda e, s_=seq: e.memset(KSp[64:128, s_, :, 0], 0.0),
                      reads=["KSp"], writes=["KSp"])
                    E("V", lambda e, s_=seq: e.memset(KSp[0:64, s_, :, 1], 0.0),
                      reads=["KSp"], writes=["KSp"])
                want = ((i + 1) * ntask) // max(nt1, 1)
                drain_prep(max(0, want - tix))
                if i % 16 == 15 or i == nt1 - 1:
                    em.end_sched()
            em.begin_sched()
            drain_prep(ntask)
            em.end_sched()
            if _dbg and nt1:
                dma(st_dbg[:], stats[:], ["stats"], ["sdbg"], sem="DO", eng="S")

        em.barrier()

        wgT_q = sb("wgT_q", [P, KC, DFF], BF16)
        wuT_q = sb("wuT_q", [P, KC, DFF], BF16)
        wdT_q = sb("wdT_q", [P, DC, D], BF16)
        xnr = sb("xnr", [P, 3, KC, P], BF16)
        xqr = sb("xqr", [P, 3, KC, P], BF16)
        qfr = sb("qfr", [P, 2, KC, P], BF16)
        edge = sb("edge", [P, 3, KC], BF16)
        fcol = sb("fcol", [P, 3, KC], BF16)
        loc = sb("loc", [P, 2, KC, P], BF16)
        xam = sb("xam", [P, KC, P], BF16)
        g_sb = sb("g_sb", [P, 2, D], BF16)
        xhq = sb("xhq", [P, 2, 768], BF16)
        u3 = sb("u3", [P, 2, 768], BF16)
        hB = sb("hB", [P, 768], F32)
        hqloc = sb("hqloc", [P, 6, P], BF16)
        psAD = psa("psAD", [P, 2 * P + HEADS], F32)
        psG1 = psa("psG1", [P, 512], F32)
        psG2 = psa("psG2", [P, 512], F32)
        psQ = psa("psQ", [P, P], F32)
        psG3 = psa("psG3", [P, 512], F32)
        psDown = psa("psDown", [P, D], F32)
        psA_v = psAD[:, 0:P]
        psDn_v = psAD[:, P:P + HEADS]
        psQ_v = psQ[:, :]
        psGs = [(psG1, "psG1"), (psG2, "psG2"), (psG3, "psG3")]
        gctr = [0]

        def p2_reloads():
            for si, (s0, sw) in enumerate(SEGS):
                dma(wgT_q[:, :, s0:s0 + sw], wgq_v[:, :, s0:s0 + sw],
                    ["wq_dram"], [f"wgTs{si}"], sem="DW", eng="S", cost=12000)
            for si, (s0, sw) in enumerate(SEGS):
                dma(wuT_q[:, :, s0:s0 + sw], wuq_v[:, :, s0:s0 + sw],
                    ["wq_dram"], [f"wuTs{si}"], sem="DY", eng="G", cost=12000)
            dma(wdT_q[:, :, 0:512], wdq_v[:, :, 0:512],
                ["wq_dram"], ["wdTh0"], sem="DW", eng="S", cost=16000)
            dma(wdT_q[:, :, 512:1024], wdq_v[:, :, 512:1024],
                ["wq_dram"], ["wdTh1"], sem="DY", eng="G", cost=16000)
            for c in range(KC):
                for h in range(2):
                    dma(qv[:, 0:512], wpT_d[c * P:(c + 1) * P, h * 512:(h + 1) * 512],
                        ["qv0"], ["qv0"], sem="DX", eng="A", cost=2000)
                    E("A", lambda e, cc=c, h_=h: e.activation(
                        wB[:, cc, h_ * 512:(h_ + 1) * 512], qv[:, 0:512], AF.Copy),
                      reads=["qv0", "wB"], writes=["wB"], cost=700)

        def p2_prefetch(j):
            r3 = j % 3
            dma(xnr[:, r3, :, :].rearrange("p c q -> p (c q)"), xnfm_d[j],
                ["xnfm_dram", f"xnr{r3}"], [f"xnr{r3}"],
                sem="DX", eng="A", cost=2500)
            E("V", lambda e, r_=r3: e.tensor_copy(edge[:, r_, :], xnr[:, r_, :, P - 1]),
              reads=[f"xnr{r3}"], writes=[f"edge{r3}"], cost=200)
            E("V", lambda e, r_=r3: e.tensor_copy(fcol[:, r_, :], xnr[:, r_, :, 0]),
              reads=[f"xnr{r3}"], writes=[f"fcol{r3}"], cost=200)
            dma(xqr[:, r3, :, :].rearrange("p c q -> p (c q)"), xqfm_d[j],
                ["xqfm_dram", f"xqr{r3}"], [f"xqr{r3}"], sem="DQ", eng="S", cost=2500)
            r2q = j % 2
            dma(qfr[:, r2q, :, :].rearrange("p c q -> p (c q)"), qfm_d[j],
                ["qfm_dram", f"qfr{r2q}"], [f"qfr{r2q}"], sem="DQ", eng="S", cost=2500)

        def p2_tile(j):
            seq = j // ST
            first = (j % ST == 0)
            last = (j % ST == ST - 1)
            r3 = j % 3
            sp = j % 2
            xqv = xqr[:, r3, :, :]
            xqb = f"xqr{r3}"
            qfv = qfr[:, sp, :, :]
            qfb = f"qfr{sp}"
            locv = loc[:, sp, :, :]
            locb = f"loc{sp}"
            xamv = xam[:, :, :]
            xamb = "xam"
            gv = g_sb[:, sp, :]
            gb = f"g_sb{sp}"
            if first:
                dma(kvblk[:].rearrange("p a b -> p (a b)"), kv_dram[seq],
                    ["kv_dram", "kvblk"], ["kvblk"], sem="DQ", eng="S", cost=3000)
            xv = xnr[:, r3, :, :]
            xb = f"xnr{r3}"
            w0b = convw_b[:, :, 0].unsqueeze(2).broadcast_to([P, KC, P])
            w1b = convw_b[:, :, 1].unsqueeze(2).broadcast_to([P, KC, P])
            w2b = convw_b[:, :, 2].unsqueeze(2).broadcast_to([P, KC, P])
            cbb = convb_b[:, :].unsqueeze(2).broadcast_to([P, KC, P])
            s1v = fA[:, 0:D].rearrange("p (c q) -> p c q", q=P)
            s2v = fB[:, 0:D].rearrange("p (c q) -> p c q", q=P)
            E("G", lambda e: e.tensor_tensor(s1v, xv, w1b, ALU.mult),
              reads=[xb, "convw_b", "fA"], writes=["fA"], cost=2400)
            E("G", lambda e: e.tensor_tensor(s2v[:, :, 1:P], xv[:, :, 0:P - 1],
                                             w0b[:, :, 1:P], ALU.mult),
              reads=[xb, "convw_b", "fB"], writes=["fB"], cost=2400)
            E("G", lambda e: e.tensor_tensor(s1v, s1v, cbb, ALU.add),
              reads=["fA", "convb_b"], writes=["fA"], cost=2400)
            if first:
                E("G", lambda e: e.memset(s2v[:, :, 0:1], 0.0),
                  reads=["fB"], writes=["fB"], cost=300)
            else:
                lsl = (j - 1) % 3
                E("G", lambda e, l_=lsl: e.tensor_tensor(
                    s2v[:, :, 0:1], edge[:, l_, :].unsqueeze(2),
                    convw_b[:, :, 0].unsqueeze(2), ALU.mult),
                  reads=[f"edge{lsl}", "convw_b", "fB"], writes=["fB"], cost=300)
            E("G", lambda e: e.tensor_tensor(fA[:, 0:D], fA[:, 0:D], fB[:, 0:D], ALU.add),
              reads=["fA", "fB"], writes=["fA"], cost=2200)
            E("G", lambda e: e.tensor_tensor(s2v[:, :, 0:P - 1], xv[:, :, 1:P],
                                             w2b[:, :, 0:P - 1], ALU.mult),
              reads=[xb, "convw_b", "fB"], writes=["fB"], cost=2400)
            if last:
                E("G", lambda e: e.memset(s2v[:, :, P - 1:P], 0.0),
                  reads=["fB"], writes=["fB"], cost=300)
            else:
                rsl = (j + 1) % 3
                E("G", lambda e, r_=rsl: e.tensor_tensor(
                    s2v[:, :, P - 1:P], fcol[:, r_, :].unsqueeze(2),
                    convw_b[:, :, 2].unsqueeze(2), ALU.mult),
                  reads=[f"fcol{rsl}", "convw_b", "fB"], writes=["fB"], cost=300)
            E("G", lambda e, lv=locv, lb=locb: e.tensor_tensor(lv, s1v, s2v, ALU.add),
              reads=["fA", "fB", locb], writes=[locb], cost=2200)
            if j + 2 < NT:
                p2_prefetch(j + 2)

            def seg_list(s):
                s0, sw = SEGS[s]
                ops = []
                for b0 in range(0, sw, 512):
                    bw = min(512, sw - b0)
                    pg, ng = psGs[gctr[0] % 3]
                    pu, nu = psGs[(gctr[0] + 1) % 3]
                    gctr[0] += 2
                    for kc in range(KC):
                        ops.append(("T", (lambda e, k_=kc, a_=s0 + b0, w_=bw, p_=pg:
                            e.matmul(
                                p_[:, 0:w_], xqv[:, k_, :], wgT_q[:, k_, a_:a_ + w_],
                                start=(k_ == 0), stop=(k_ == KC - 1))),
                            [xqb, f"wgTs{s}", ng], [ng]))
                        ops.append(("T", (lambda e, k_=kc, a_=s0 + b0, w_=bw, p_=pu:
                            e.matmul(
                                p_[:, 0:w_], xqv[:, k_, :], wuT_q[:, k_, a_:a_ + w_],
                                start=(k_ == 0), stop=(k_ == KC - 1))),
                            [xqb, f"wuTs{s}", nu], [nu]))
                    hv, hb = (fB, "fB") if s % 2 == 0 else (hB, "hB")
                    u3v = u3[:, s % 2, :]
                    u3b = f"u3{s % 2}"
                    ops += [
                        ("A", (lambda e, w_=bw, p_=pg: e.activation(
                            fA[:, 0:w_], p_[:, 0:w_], AF.Tanh, scale=0.5)),
                         [ng, "fA"], ["fA"]),
                        ("V", (lambda e, w_=bw, p_=pg: e.scalar_tensor_tensor(
                            fA[:, 0:w_], fA[:, 0:w_], 1.0, p_[:, 0:w_],
                            ALU.add, ALU.mult)), ["fA", ng], ["fA"]),
                        ("V", (lambda e, b_=b0, w_=bw, p_=pu, h_=hv: e.scalar_tensor_tensor(
                            h_[:, b_:b_ + w_], fA[:, 0:w_], 0.5, p_[:, 0:w_],
                            ALU.mult, ALU.mult)), ["fA", nu, hb], [hb]),
                        ("A", (lambda e, b_=b0, w_=bw, h_=hv, u_=u3v: e.activation(
                            u_[:, b_:b_ + w_], h_[:, b_:b_ + w_], AF.Sign)),
                         [hb, u3b], [u3b]),
                    ]
                return ops

            def qnt_list(s):
                s0, sw = SEGS[s]
                hv, hb = (fB, "fB") if s % 2 == 0 else (hB, "hB")
                return quant_ops(hv[:, 0:sw], hb, xhq[:, s % 2, 0:sw],
                                 (f"hqA{s % 2}", f"hqB{s % 2}"),
                                 sw, 0, sgn_pre=(u3[:, s % 2, 0:sw], f"u3{s % 2}"),
                                 force_wide=True, g_off=True)

            def tdown(s):
                s0, sw = SEGS[s]
                nch = sw // P
                sp2 = s % 2
                hql = hqloc
                hqb = "hqloc"
                transpose_group(lambda jj, s_=sp2: xhq[:, s_, jj * P:(jj + 1) * P], nch,
                                hql, hqb, evac="A",
                                src_bufs=(f"hqA{sp2}", f"hqB{sp2}"))
                for lc in range(nch):
                    gk = s0 // P + lc
                    for half in range(2):
                        E("T", lambda e, l_=lc, g_=gk, h_=half, q_=hql: e.matmul(
                            psDown[:, h_ * 512:(h_ + 1) * 512], q_[:, l_, :],
                            wdT_q[:, g_, h_ * 512:(h_ + 1) * 512],
                            start=(g_ == 0), stop=(g_ == DC - 1)),
                          reads=[hqb, f"wdTh{half}", f"psDn{half}"],
                          writes=[f"psDn{half}"])

            nseg = len(SEGS)
            pend = None
            for s in range(nseg):
                if s == 0:
                    emit_list(seg_list(0))
                else:
                    emit_zip(pend, seg_list(s))
                    tdown(s - 1)
                pend = qnt_list(s)
            emit_list(pend)
            tdown(nseg - 1)
            E("A", lambda e: e.activation(fA[:, 0:D], psDown[:], AF.Tanh, scale=0.5),
              reads=["psDn0", "psDn1", "fA"], writes=["fA"], cost=1400)
            E("V", lambda e: e.scalar_tensor_tensor(fA[:, 0:D], fA[:, 0:D], 1.0,
                                                    psDown[:], ALU.add, ALU.mult),
              reads=["fA", "psDn0", "psDn1"], writes=["fA"], cost=1300)
            E("A", lambda e, g_=gv, gb_=gb: e.activation(g_, fA[:, 0:D], AF.Copy, scale=0.5),
              reads=["fA", gb], writes=[gb], cost=1400)

            dma(qv[:], x_t[j], ["qv0", "qv1"], ["qv0", "qv1"], sem="DX", eng="A",
                cost=5000)

            for pair in range(NPAIR):
                E("T", lambda e, p_=pair, q_=seq: e.matmul(
                    psDn_v[:, 2 * p_:2 * p_ + 2], qfv[:, p_, :], KSp[:, q_, p_, :],
                    start=True, stop=True),
                  reads=[qfb, "KSp", "psAD"], writes=["psAD"])
            E("V", lambda e: e.tensor_scalar(rden[:, :], psDn_v, 1e-6, None, ALU.max),
              reads=["psAD", "rden"], writes=["rden"], cost=250)
            E("V", lambda e: e.reciprocal(rden[:, :], rden[:, :]),
              reads=["rden"], writes=["rden"], cost=300)
            for pair in range(NPAIR):
                pv = psA_v if pair % 2 == 0 else psQ_v
                pb = "psAD" if pair % 2 == 0 else "psQ"
                E("T", lambda e, p_=pair, v_=pv, b_=pb: e.matmul(
                    v_, qfv[:, p_, :], kvblk[:, p_, :], start=True, stop=True),
                  reads=[qfb, "kvblk", pb], writes=[pb])
                rb = rden[:, 2 * pair:2 * pair + 2].unsqueeze(2).broadcast_to([P, 2, HD])
                E("V", lambda e, p_=pair, rb_=rb, v_=pv, b_=pb: e.tensor_tensor(
                    u2am[:, p_ * P:(p_ + 1) * P].rearrange("p (h q) -> p h q", q=HD),
                    v_.rearrange("p (h q) -> p h q", q=HD), rb_, ALU.mult),
                  reads=[pb, "rden", "u20", "u21"], writes=["u20", "u21"], cost=400)
            transpose_group(lambda jj: u2am[:, jj * P:(jj + 1) * P], KC, xamv,
                            xamb, loc_fuse=(locv, locb), src_bufs=("u20", "u21"))
            for half in range(2):
                for kc in range(KC):
                    E("T", lambda e, k_=kc, h_=half: e.matmul(
                        psDown[:, h_ * 512:(h_ + 1) * 512], xamv[:, k_, :],
                        wB[:, k_, h_ * 512:(h_ + 1) * 512], start=(k_ == 0), stop=False),
                      reads=[xamb, "wB", f"psDn{half}"], writes=[f"psDn{half}"])
                E("T", lambda e, h_=half: e.matmul(
                    psDown[:, h_ * 512:(h_ + 1) * 512], ones_row[:, :],
                    bp_b[:, h_ * 512:(h_ + 1) * 512], start=False, stop=True),
                  reads=["ones_r", "bp_b", f"psDn{half}"], writes=[f"psDn{half}"])
                E("V", lambda e, h_=half: e.tensor_tensor(
                    u2am[:, h_ * 512:(h_ + 1) * 512], psDown[:, h_ * 512:(h_ + 1) * 512],
                    gv[:, h_ * 512:(h_ + 1) * 512], ALU.mult),
                  reads=[f"psDn{half}", gb, "u20" if half == 0 else "u21"],
                  writes=["u20" if half == 0 else "u21"])
                E("V", lambda e, h_=half: e.tensor_tensor(
                    qv[:, h_ * 512:(h_ + 1) * 512], qv[:, h_ * 512:(h_ + 1) * 512],
                    u2am[:, h_ * 512:(h_ + 1) * 512], ALU.add),
                  reads=[f"qv{half}", "u20" if half == 0 else "u21"],
                  writes=[f"qv{half}"])
                dma(out_t[j][:, half * 512:(half + 1) * 512], qv[:, half * 512:(half + 1) * 512],
                    [f"qv{half}"], ["out_dma"], sem="DO")

        if "p2" in _phases:
            em.begin_sched()
            p2_reloads()
            for jj in range(min(2, NT)):
                p2_prefetch(jj)
            em.end_sched()
            for j in range(NT):
                if j % 8 == 0:
                    em.begin_sched()
                p2_tile(j)
                if j % 8 == 7 or j == NT - 1:
                    em.end_sched()
        else:
            for j in range(NT):
                dma(out_t[j], qv[:], ["qv0", "qv1"], ["out_dma"], sem="DO")

        with nc.semaphore("sT") as sT, nc.semaphore("sV") as sV, nc.semaphore("sA") as sA, \
             nc.semaphore("sG") as sG, nc.semaphore("sDW") as sDW, nc.semaphore("sDX") as sDX, \
             nc.semaphore("sDQ") as sDQ, nc.semaphore("sDO") as sDO, \
             nc.semaphore("sDY") as sDY, nc.Block() as blk:
            sems = {"T": sT, "V": sV, "A": sA, "G": sG, "DW": sDW, "DX": sDX,
                    "DQ": sDQ, "DO": sDO, "DY": sDY}

            def runner(ename):
                def body(eng):
                    for waits, fn, sem, inc in em.ops[ename]:
                        for s, c in waits:
                            eng.wait_ge(sems[s], c)
                        if fn is not None:
                            fn(eng).then_inc(sems[sem], inc)
                    if ename == "S":
                        eng.wait_ge(sDO, em.cnt["DO"])
                return body

            blk.tensor(runner("T"))
            blk.vector(runner("V"))
            blk.scalar(runner("A"))
            blk.gpsimd(runner("G"))
            blk.sync(runner("S"))

    return nc


def kernel(L0_attn_x, L0_attn_Wq, L0_attn_Wk, L0_attn_Wv, L0_attn_Wp, L0_attn_bp,
           L0_attn_conv_w, L0_attn_conv_b, L0_ffn_w_gate, L0_ffn_w_up, L0_ffn_w_down):
    x = np.ascontiguousarray(np.asarray(L0_attn_x, dtype=np.float32))
    B, SL, _ = x.shape
    per = B // 8
    wqT = np.ascontiguousarray(np.asarray(L0_attn_Wq, np.float32).T)
    wkT = np.ascontiguousarray(np.asarray(L0_attn_Wk, np.float32).T)
    wvT = np.ascontiguousarray(np.asarray(L0_attn_Wv, np.float32).T)
    wpT = np.ascontiguousarray(np.asarray(L0_attn_Wp, np.float32).T)
    bp = np.ascontiguousarray(np.asarray(L0_attn_bp, np.float32).reshape(1, D))
    convw = np.ascontiguousarray(np.asarray(L0_attn_conv_w, np.float32))
    convb = np.ascontiguousarray(np.asarray(L0_attn_conv_b, np.float32).reshape(KC, P).T)
    wg = np.ascontiguousarray(np.asarray(L0_ffn_w_gate, np.float32))
    wu = np.ascontiguousarray(np.asarray(L0_ffn_w_up, np.float32))
    wd = np.ascontiguousarray(np.asarray(L0_ffn_w_down, np.float32))

    nc = build_program(seq_len=SL, n_seq=per)
    in_maps = []
    for c in range(8):
        shard = np.ascontiguousarray(x[c * per:(c + 1) * per].reshape(per * SL, D))
        in_maps.append({"x": shard, "wqT": wqT, "wkT": wkT, "wvT": wvT, "wpT": wpT,
                        "bp": bp, "convw": convw, "convb": convb,
                        "wg": wg, "wu": wu, "wd": wd})
    res = run_bass_kernel_spmd(nc, in_maps, core_ids=list(range(8)))
    out = np.concatenate([res.results[c]["out"].reshape(per, SL, D) for c in range(8)], axis=0)
    return np.ascontiguousarray(out.astype(np.float32))


# version 34
# speedup vs baseline: 1.0169x; 1.0009x over previous
import os
import numpy as np
from contextlib import ExitStack

import concourse.bass as bass
import concourse.mybir as mybir
from concourse.bass_utils import run_bass_kernel_spmd

F32 = mybir.dt.float32
BF16 = mybir.dt.bfloat16
I16 = mybir.dt.int16
AF = mybir.ActivationFunctionType
ALU = mybir.AluOpType
AX = mybir.AxisListType

P = 128
D = 1024
DFF = 2816
KC = D // P
DC = DFF // P
HEADS = 16
HD = 64
NPAIR = 8
EPS = 1e-5
SEGS = [(0, 768), (768, 768), (1536, 768), (2304, 512)]

_SIM = os.environ.get("BASS_SIM_FP64") == "1"
CB = 0.5 if _SIM else 0.0
_TRACE = []
_LAST_EM = None


class Emit:

    DMA_SEMS = ("DW", "DX", "DQ", "DO", "DY")

    def __init__(self):
        self.ops = {e: [] for e in ("T", "V", "A", "G", "S")}
        self.cnt = {s: 0 for s in ("T", "V", "A", "G") + self.DMA_SEMS}
        self.vc = {e: {} for e in self.ops}
        self.buf = {}
        self.sched = None

    def op(self, eng, fn, reads=(), writes=(), dma_sem=None, cost=None):
        if self.sched is not None:
            self.sched.append((eng, fn, tuple(reads), tuple(writes), dma_sem, cost))
            return
        self._op(eng, fn, reads, writes, dma_sem)

    def begin_sched(self):
        assert self.sched is None
        if os.environ.get("KSCHED") == "0":
            return
        self.sched = []

    def end_sched(self):
        if self.sched is None:
            return
        ops, self.sched = self.sched, None
        for k in _list_schedule(ops):
            eng, fn, reads, writes, dma_sem, _ = ops[k]
            self._op(eng, fn, reads, writes, dma_sem)

    def _op(self, eng, fn, reads=(), writes=(), dma_sem=None):
        if os.environ.get("KTRACE") == "1":
            _TRACE.append((eng, dma_sem, tuple(reads), tuple(writes)))
        sem = dma_sem if dma_sem else eng
        inc = 16 if dma_sem else 1
        if dma_sem:
            reads = list(reads) + [f"_tok_{sem}"]
            writes = list(writes) + [f"_tok_{sem}"]
        needed = {}
        for b in reads:
            st = self.buf.get(b)
            if st and st["w"]:
                s, c = st["w"]
                needed[s] = max(needed.get(s, 0), c)
        for b in writes:
            st = self.buf.get(b)
            if st:
                if st["w"]:
                    s, c = st["w"]
                    needed[s] = max(needed.get(s, 0), c)
                for s, c in st["r"].items():
                    needed[s] = max(needed.get(s, 0), c)
        waits = []
        vc = self.vc[eng]
        for s, c in sorted(needed.items()):
            if eng == "T" and s == "T":
                continue
            if vc.get(s, 0) < c:
                waits.append((s, c))
                vc[s] = c
        self.cnt[sem] += inc
        cnt = self.cnt[sem]
        self.ops[eng].append((waits, fn, sem, inc))
        for b in reads:
            st = self.buf.setdefault(b, {"w": None, "r": {}})
            st["r"][sem] = max(st["r"].get(sem, 0), cnt)
        for b in writes:
            self.buf[b] = {"w": (sem, cnt), "r": {}}

    def barrier(self):
        assert self.sched is None
        snapshot = dict(self.cnt)
        for e in self.ops:
            vc = self.vc[e]
            waits = []
            for s, c in sorted(snapshot.items()):
                if e == "T" and s == "T":
                    continue
                if c > 0 and vc.get(s, 0) < c:
                    waits.append((s, c))
                    vc[s] = c
            if waits:
                self.ops[e].append((waits, None, None, 0))


_ENG_COST = {"T": 260, "V": 700, "A": 720, "G": 1900, "S": 150}


def _list_schedule(ops):
    n = len(ops)
    busy = [0.0] * n
    lat = [0.0] * n
    for i, (eng, fn, rd, wr, dsem, cost) in enumerate(ops):
        if dsem:
            busy[i] = 150.0
            lat[i] = float(cost) if cost is not None else 4000.0
        else:
            busy[i] = float(cost) if cost is not None else _ENG_COST[eng]
            lat[i] = busy[i] + 100.0
    last_w = {}
    readers = {}
    preds = [None] * n
    succ = [[] for _ in range(n)]
    indeg = [0] * n
    for i, (eng, fn, rd, wr, dsem, cost) in enumerate(ops):
        r = list(rd)
        w = list(wr)
        if dsem:
            tok = "_tok_" + dsem
            r.append(tok)
            w.append(tok)
        dset = set()
        for b in r:
            j = last_w.get(b)
            if j is not None:
                dset.add(j)
        for b in w:
            j = last_w.get(b)
            if j is not None:
                dset.add(j)
            dset.update(readers.get(b, ()))
        dset.discard(i)
        preds[i] = dset
        for j in dset:
            succ[j].append(i)
        indeg[i] = len(dset)
        for b in r:
            readers.setdefault(b, set()).add(i)
        for b in w:
            last_w[b] = i
            readers[b] = set()
    prio = [0.0] * n
    for i in range(n - 1, -1, -1):
        m = 0.0
        for j in succ[i]:
            if prio[j] > m:
                m = prio[j]
        prio[i] = m + lat[i]
    eng_t = {e: 0.0 for e in _ENG_COST}
    avail = [0.0] * n
    ready = [i for i in range(n) if indeg[i] == 0]
    order = []
    while ready:
        best = -1
        best_key = None
        best_est = 0.0
        m = None
        ests = []
        for i in ready:
            est = eng_t[ops[i][0]]
            for j in preds[i]:
                if avail[j] > est:
                    est = avail[j]
            ests.append(est)
            if m is None or est < m:
                m = est
        for i, est in zip(ready, ests):
            if est <= m + 400.0:
                key = (-prio[i], i)
                if best < 0 or key < best_key:
                    best, best_key, best_est = i, key, est
        i = best
        eng_t[ops[i][0]] = best_est + busy[i]
        avail[i] = best_est + lat[i]
        order.append(i)
        ready.remove(i)
        for j in succ[i]:
            indeg[j] -= 1
            if indeg[j] == 0:
                ready.append(j)
    assert len(order) == n
    return order


def build_program(seq_len=2048, n_seq=2, _phases=("prep", "p1", "p2"), _dbg=False):
    S = seq_len * n_seq
    NT = S // P
    ST = seq_len // P
    assert seq_len % P == 0 and ST >= 2
    _ik = "ExternalOutput" if _dbg else "Internal"

    nc = bass.Bass("TRN2", target_bir_lowering=False, debug=False)

    x_d = nc.dram_tensor("x", [S, D], F32, kind="ExternalInput").ap()
    wqT_d = nc.dram_tensor("wqT", [D, D], F32, kind="ExternalInput").ap()
    wkT_d = nc.dram_tensor("wkT", [D, D], F32, kind="ExternalInput").ap()
    wvT_d = nc.dram_tensor("wvT", [D, D], F32, kind="ExternalInput").ap()
    wpT_d = nc.dram_tensor("wpT", [D, D], F32, kind="ExternalInput").ap()
    bp_d = nc.dram_tensor("bp", [1, D], F32, kind="ExternalInput").ap()
    convw_d = nc.dram_tensor("convw", [D, 3], F32, kind="ExternalInput").ap()
    convb_d = nc.dram_tensor("convb", [P, KC], F32, kind="ExternalInput").ap()
    wg_d = nc.dram_tensor("wg", [DFF, D], F32, kind="ExternalInput").ap()
    wu_d = nc.dram_tensor("wu", [DFF, D], F32, kind="ExternalInput").ap()
    wd_d = nc.dram_tensor("wd", [D, DFF], F32, kind="ExternalInput").ap()
    out_d = nc.dram_tensor("out", [S, D], F32, kind="ExternalOutput").ap()
    kv_dram = nc.dram_tensor("kv_scratch", [n_seq, P, NPAIR * P], BF16, kind=_ik).ap()
    xnfm_d = nc.dram_tensor("xnfm_d", [NT, P, D], BF16, kind=_ik).ap()
    xqfm_d = nc.dram_tensor("xqfm_d", [NT, P, D], BF16, kind=_ik).ap()
    qfm_d = nc.dram_tensor("qfm_d", [NT, P, D], BF16, kind=_ik).ap()
    wgq_d = nc.dram_tensor("wgq_d", [P, KC * DFF], BF16, kind=_ik).ap()
    wuq_d = nc.dram_tensor("wuq_d", [P, KC * DFF], BF16, kind=_ik).ap()
    wdq_d = nc.dram_tensor("wdq_d", [P, DC * D], BF16, kind=_ik).ap()
    wgq_v = wgq_d.rearrange("p (k f) -> p k f", f=DFF)
    wuq_v = wuq_d.rearrange("p (k f) -> p k f", f=DFF)
    wdq_v = wdq_d.rearrange("p (k f) -> p k f", f=D)
    if _dbg:
        xn_dbg = nc.dram_tensor("xn_dbg", [NT, P, D], F32, kind="ExternalOutput").ap()
        u1_dbg = nc.dram_tensor("u1_dbg", [NT, P, D], BF16, kind="ExternalOutput").ap()
        x_dbg = nc.dram_tensor("x_dbg", [NT, P, D], F32, kind="ExternalOutput").ap()
        st_dbg = nc.dram_tensor("st_dbg", [P, 2 * NT], F32, kind="ExternalOutput").ap()

    x_t = x_d.rearrange("(t p) d -> t p d", p=P)
    out_t = out_d.rearrange("(t p) d -> t p d", p=P)

    em = Emit()
    global _LAST_EM
    _LAST_EM = em
    E = em.op

    def dma(dst, src, rb, wb, sem="DW", eng="S", cost=None):
        E(eng, lambda e, d_=dst, s_=src: e.dma_start(out=d_, in_=s_), reads=rb, writes=wb,
          dma_sem=sem, cost=cost)

    with ExitStack() as ctx:
        sb = lambda name, shape, dt=F32: ctx.enter_context(nc.sbuf_tensor(name, shape, dt))
        psa = lambda name, shape, dt=F32: ctx.enter_context(nc.psum_tensor(name, shape, dt))

        ident_b = sb("ident_b", [P, P], BF16)
        ones_col = sb("ones_col", [P, 1], BF16)
        ones_row = sb("ones_row", [1, P], BF16)
        bp_b = sb("bp_b", [1, D], BF16)
        convw_b = sb("convw_b", [P, KC, 3], BF16)
        convb_b = sb("convb_b", [P, KC], BF16)
        qv = sb("qv", [P, D], F32)
        u1 = sb("u1", [P, D], BF16)
        u2am = sb("u2am", [P, D], BF16)
        fA = sb("fA", [P, D], F32)
        fB = sb("fB", [P, D], F32)
        bscale = sb("bscale", [P, 112], F32)
        brs = sb("brs", [P, 128], F32)
        psT = psa("psT", [P, KC, P], BF16)
        rden = brs[:, 112:128]
        kvblk = sb("kvblk", [P, NPAIR, P], BF16)
        KSp = sb("KSp", [P, n_seq, NPAIR, 2], BF16)
        wB = sb("wB", [P, KC, D], BF16)

        def quant_ops(src_ap, src_buf, out_ap, out_buf, width, par, scratch=None,
                      sgn_pre=None, g_off=False, force_wide=False):
            nb = width // 16
            wide = force_wide or width > 512
            if scratch is None:
                scratch = ((qv, "qv0", "qv1"), (u1, "u10", "u11"), (u2am, "u20", "u21"))
            (qvt, qn0, qn1), (u1t, un0, un1), (u2t, mn0, mn1) = scratch
            if wide:
                assert width <= 1024 and par == 0
                o = 0
                ob = 0
                qb, ub1, ub2 = [qn0, qn1], [un0, un1], [mn0, mn1]
            else:
                o = par * 512
                ob = par * 32
                qb, ub1, ub2 = [qn0 if par == 0 else qn1], \
                               [un0 if par == 0 else un1], [mn0 if par == 0 else mn1]
            sv = src_ap.rearrange("p (b k) -> p b k", k=16)
            qvp = qvt[:, o:o + width]
            u1p = u1t[:, o:o + width]
            u1i = u1p.bitcast(I16)
            u2p = u2t[:, o:o + width]
            if wide:
                bsc = bscale[:, 64:64 + nb]
                brp = brs[:, 64:64 + nb]
            else:
                bsc = bscale[:, ob:ob + nb]
                brp = brs[:, ob:ob + nb]
            rs_b = brp.unsqueeze(2).broadcast_to([P, nb, 16])
            sc_b = bsc.unsqueeze(2).broadcast_to([P, nb, 16])
            bb, rb = ("bscW", "brsW") if wide else (f"bsc{par}", f"brs{par}")
            ob_ = list(out_buf) if isinstance(out_buf, (tuple, list)) else [out_buf]
            wv = 160 + int(1.12 * width)
            wa = 230 + width
            wg = (160 + int(2.2 * width)) if g_off else wv
            GV = "G" if g_off else "V"
            if _SIM:
                ops = [
                    ("V", lambda e: e.tensor_reduce(bsc, sv, AX.X, ALU.max,
                                                    apply_absolute_value=True),
                     [src_buf, bb], [bb]),
                    ("A", lambda e: e.activation(qvp, src_ap, AF.Abs), [src_buf] + qb, qb),
                    ("V", lambda e: e.tensor_scalar(bsc, bsc, 1.0 / 6.0, 1e-12,
                                                    ALU.mult, ALU.max), [bb], [bb]),
                    ("V", lambda e: e.reciprocal(brp, bsc), [bb, rb], [rb]),
                    (GV,
                     lambda e: e.tensor_tensor(qvp.rearrange("p (b k) -> p b k", k=16),
                                               qvp.rearrange("p (b k) -> p b k", k=16),
                                               rs_b, ALU.mult), qb + [rb], qb),
                    ("V", lambda e: e.tensor_scalar(out_ap, qvp, 5.0, -2.0,
                                                    ALU.is_le, ALU.mult), qb + ob_, ob_),
                    ("A", lambda e: e.activation(u1i, qvp, AF.Copy, bias=CB, scale=2.0),
                     qb + ub1, ub1),
                    ("V", lambda e: e.tensor_scalar(u2p, u1i, 4.0, 0.5, ALU.min, ALU.mult),
                     ub1, ub2),
                    ("A", lambda e: e.activation(u1i, qvp, AF.Copy, bias=CB, scale=1.0),
                     qb + ub1, ub1),
                    ("V", lambda e: e.tensor_scalar(u1p, u1i, 2.0, 4.0, ALU.max, ALU.min),
                     ub1, ub1),
                    (GV, lambda e: e.tensor_add(u2p, u2p, u1p), ub2 + ub1, ub2),
                    (GV, lambda e: e.tensor_add(u2p, u2p, out_ap), ub2 + ob_, ub2),
                ]
                cst = [wv, wa, 300, 400, wg, wv, wa, wv, wa, wv, wg, wg]
                if sgn_pre is None:
                    ops.append(("A", lambda e: e.activation(out_ap, src_ap, AF.Sign),
                                [src_buf] + ob_, ob_))
                    ops.append((GV, lambda e: e.tensor_tensor(u2p, u2p, out_ap, ALU.mult),
                                ub2 + ob_, ub2))
                    cst += [wa, wg]
                else:
                    sga, sgb = sgn_pre
                    ops.append(("V", lambda e: e.tensor_tensor(u2p, u2p, sga, ALU.mult),
                                ub2 + [sgb], ub2))
                    cst += [wv]
            else:
                C2 = float(1.5 * 2 ** 21)
                C3 = float(1.5 * 2 ** 22)
                ops = [
                    ("V", lambda e: e.tensor_reduce(bsc, sv, AX.X, ALU.max,
                                                    apply_absolute_value=True),
                     [src_buf, bb], [bb]),
                    ("V", lambda e: e.tensor_scalar(
                        qvp.bitcast(mybir.dt.int32), src_ap.bitcast(mybir.dt.int32),
                        0x7FFFFFFF, None, ALU.bitwise_and), [src_buf] + qb, qb),
                    ("V", lambda e: e.tensor_scalar(bsc, bsc, 1.0 / 3.0, 2e-12,
                                                    ALU.mult, ALU.max), [bb], [bb]),
                    ("V", lambda e: e.reciprocal(brp, bsc), [bb, rb], [rb]),
                    (GV,
                     lambda e: e.tensor_tensor(qvp.rearrange("p (b k) -> p b k", k=16),
                                               qvp.rearrange("p (b k) -> p b k", k=16),
                                               rs_b, ALU.mult), qb + [rb], qb),
                    ("V", lambda e: e.tensor_scalar(u1p, qvp, C2, C2,
                                                    ALU.add, ALU.subtract), qb + ub1, ub1),
                    ("V", lambda e: e.tensor_scalar(u2p, qvp, C3, C3 + 1.0,
                                                    ALU.add, ALU.subtract), qb + ub2, ub2),
                    ("V", lambda e: e.tensor_scalar(u2p, u2p, 1.0, 0.0,
                                                    ALU.min, ALU.max), ub2, ub2),
                    ("V", lambda e: e.scalar_tensor_tensor(u2p, u1p, 1.0, u2p,
                                                           ALU.min, ALU.add),
                     ub1 + ub2, ub2),
                    ("V", lambda e: e.scalar_tensor_tensor(u2p, qvp, 2.5, u2p,
                                                           ALU.is_gt, ALU.add),
                     qb + ub2, ub2),
                ]
                cst = [wv, wv, 300, 400, wg, wv, wv, wv, wv, wv]
                if sgn_pre is None:
                    ops.append(("A", lambda e: e.activation(out_ap, src_ap, AF.Sign),
                                [src_buf] + ob_, ob_))
                    ops.append((GV, lambda e: e.tensor_tensor(u2p, u2p, out_ap, ALU.mult),
                                ub2 + ob_, ub2))
                    cst += [wa, wg]
                else:
                    sga, sgb = sgn_pre
                    ops.append((GV, lambda e: e.tensor_tensor(u2p, u2p, sga, ALU.mult),
                                ub2 + [sgb], ub2))
                    cst += [wg]
            ops.append((GV, lambda e: e.tensor_tensor(
                out_ap.rearrange("p (b k) -> p b k", k=16),
                u2p.rearrange("p (b k) -> p b k", k=16), sc_b, ALU.mult),
                ub2 + [bb], ob_))
            cst += [wg]
            return [t + (c,) for t, c in zip(ops, cst)]

        def emit_list(ops):
            for t in ops:
                E(t[0], t[1], reads=t[2], writes=t[3],
                  cost=(t[4] if len(t) > 4 else None))

        def emit_zip(la, lb):
            i = j = 0
            while i < len(la) or j < len(lb):
                if i < len(la):
                    t = la[i]
                    E(t[0], t[1], reads=t[2], writes=t[3],
                      cost=(t[4] if len(t) > 4 else None))
                    i += 1
                if j < len(lb):
                    t = lb[j]
                    E(t[0], t[1], reads=t[2], writes=t[3],
                      cost=(t[4] if len(t) > 4 else None))
                    j += 1

        def quant_wide(src_ap, src_buf, out_ap, out_names, width, scratch=None,
                       g_off=False):
            nA, nB = out_names
            if width <= 512:
                emit_list(quant_ops(src_ap, src_buf, out_ap, nA, width, 0, scratch,
                                    g_off=g_off))
                return
            w0 = 512
            w1 = width - 512
            emit_zip(quant_ops(src_ap[:, 0:w0], src_buf, out_ap[:, 0:w0], nA, w0, 0,
                               scratch, g_off=g_off),
                     quant_ops(src_ap[:, w0:width], src_buf, out_ap[:, w0:width],
                               nB, w1, 1, scratch, g_off=g_off))

        def transpose_group(src_ap_fn, nchunks, dst_t, dst_buf, loc_fuse=None,
                            evac="V", dst_off=0, src_bufs=()):
            rsrc = list(src_bufs)
            step = 4 if os.environ.get("KTS", "0") == "1" else KC
            for j in range(0, nchunks, step):
                m = min(step, nchunks - j)
                hf = (j // 4) % 2 if step == 4 else 0
                pb = f"psT{hf}"
                for c in range(m):
                    E("T", lambda e, jj=j + c, cc=4 * hf + c: e.transpose(
                        psT[:, cc, :], src_ap_fn(jj), ident_b[:]),
                      reads=rsrc + ["ident", pb], writes=[pb])
                if loc_fuse is None:
                    if evac == "A":
                        E("A", lambda e, jj=j + dst_off, mm=m, h_=hf: e.activation(
                            dst_t[:, jj:jj + mm, :].rearrange("p c q -> p (c q)"),
                            psT[:, 4 * h_:4 * h_ + mm, :].rearrange("p c q -> p (c q)"),
                            AF.Copy),
                          reads=[pb, dst_buf], writes=[dst_buf], cost=230 + m * P)
                    else:
                        E("V", lambda e, jj=j + dst_off, mm=m, h_=hf: e.tensor_copy(
                            dst_t[:, jj:jj + mm, :].rearrange("p c q -> p (c q)"),
                            psT[:, 4 * h_:4 * h_ + mm, :].rearrange("p c q -> p (c q)")),
                          reads=[pb], writes=[dst_buf], cost=160 + m * P)
                else:
                    loc_t, loc_buf = loc_fuse
                    E("V", lambda e, jj=j, mm=m, h_=hf: e.scalar_tensor_tensor(
                        dst_t[:, jj:jj + mm, :].rearrange("p c q -> p (c q)"),
                        psT[:, 4 * h_:4 * h_ + mm, :].rearrange("p c q -> p (c q)"), 1.0,
                        loc_t[:, jj:jj + mm, :].rearrange("p c q -> p (c q)"),
                        ALU.mult, ALU.add),
                      reads=[pb, loc_buf], writes=[dst_buf], cost=160 + m * P)

        E("G", lambda e: e.memset(ident_b[:], 0.0), writes=["ident"])
        E("G", lambda e: e.affine_select(out=ident_b[:], in_=ident_b[:],
                                         compare_op=ALU.not_equal, fill=1.0, base=0,
                                         pattern=[[-1, P]], channel_multiplier=1),
          reads=["ident"], writes=["ident"])
        E("G", lambda e: e.memset(ones_col[:], 1.0), writes=["ones"])
        E("G", lambda e: e.memset(ones_row[:], 1.0), writes=["ones_r"])

        with ExitStack() as pctx:
            sbp = lambda name, shape, dt: pctx.enter_context(nc.sbuf_tensor(name, shape, dt))
            bp_f32 = sbp("bp_f32", [1, D], F32)
            convw_sb = sbp("convw_sb", [P, KC, 3], F32)
            convb_sb = sbp("convb_sb", [P, KC], F32)

            dma(bp_f32[:], bp_d[:], [], ["bp_f32"])
            E("V", lambda e: e.tensor_copy(bp_b[:], bp_f32[:]), reads=["bp_f32"], writes=["bp_b"])
            dma(convw_sb[:], convw_d.rearrange("(c p) k -> p c k", p=P), [], ["convw"])
            dma(convb_sb[:], convb_d[:], [], ["convb"])
            E("V", lambda e: e.tensor_copy(convw_b[:], convw_sb[:]),
              reads=["convw"], writes=["convw_b"])
            E("V", lambda e: e.tensor_copy(convb_b[:], convb_sb[:]),
              reads=["convb"], writes=["convb_b"])

        em.barrier()

        with ExitStack() as p1ctx:
            sb1 = lambda name, shape, dt: p1ctx.enter_context(nc.sbuf_tensor(name, shape, dt))
            ps1 = lambda name, shape, dt: p1ctx.enter_context(nc.psum_tensor(name, shape, dt))
            wA = sb1("wA", [P, KC, D], BF16)
            wC = sb1("wC", [P, KC, D], BF16)
            stats = sb1("stats", [P, 2 * NT], F32)
            stscr = sb1("stscr", [P, 8 * NT], F32)
            x_sb = sb1("x_sb", [P, 2, D], F32)
            xn_tm = sb1("xn_tm", [P, 2, D], F32)
            u1p1 = sb1("u1p1", [P, 2, D], BF16)
            qe = sb1("qe", [P, 2, D], BF16)
            xq_sb = sb1("xq_sb", [P, 2, D], BF16)
            qs1 = sb1("qs1", [P, 2, D], BF16)
            eluA = sb1("eluA", [P, 2, D], F32)
            eluB = sb1("eluB", [P, 2, D], F32)
            fC = sb1("fC", [P, D], F32)
            fD = sb1("fD", [P, D], F32)
            xnfm_s = sb1("xnfm_s", [P, 2, KC, P], BF16)
            xqfm_s = sb1("xqfm_s", [P, 2, KC, P], BF16)
            qfm_s = sb1("qfm_s", [P, 2, KC, P], BF16)
            k_tm = sb1("k_tm", [P, 2, D], BF16)
            v_tm = sb1("v_tm", [P, 2, D], BF16)
            ks_sb = sb1("ks_sb", [P, NPAIR], F32)
            wstage = sb1("wstage", [P, 2, D], F32)
            wqchunk = sb1("wqchunk", [P, 2, D], BF16)
            wstg_out = sb1("wstg_out", [P, 2, KC, P], BF16)
            wload = sb1("wload", [P, 2, D], F32)
            psK = ps1("psK", [P, D], F32)
            psV = ps1("psV", [P, D], F32)
            psKV = ps1("psKV", [P, NPAIR, P], F32)
            psKS = ps1("psKS", [P, NPAIR], F32)

            wjobs = [(wkT_d, wA, "wA"), (wvT_d, wB, "wB"), (wqT_d, wC, "wC")]
            for jx, (w_ap, dst, dbuf) in enumerate(wjobs):
                for c in range(KC):
                    sl = (jx * KC + c) % 2
                    sem, eng = ("DW", "S") if sl == 0 else ("DY", "G")
                    dma(wload[:, sl, :], w_ap[c * P:(c + 1) * P, :],
                        [f"wload{sl}"], [f"wload{sl}"], sem=sem, eng=eng, cost=3000)
                    E("V", lambda e, cc=c, dd=dst, s_=sl: e.tensor_copy(
                        dd[:, cc, :], wload[:, s_, :]),
                      reads=[f"wload{sl}"], writes=[dbuf], cost=1000)

            def prep_task(w_dram, c, s0, sw, dst_v, ti):
                pr = ti % 2
                wst = wstage[:, pr, 0:sw]
                wqc = wqchunk[:, pr, :]
                dma(wst, w_dram[c * P:(c + 1) * P, s0:s0 + sw],
                    [f"wstage{pr}"], [f"wstage{pr}"], cost=5000)
                quant_wide(wst, f"wstage{pr}", wqc[:, 0:sw],
                           (f"wqcA{pr}", f"wqcB{pr}"), sw, g_off=True)
                g0 = s0 // P
                for j in range(0, sw // P, KC):
                    m = min(KC, sw // P - j)
                    transpose_group(
                        lambda jj, p_=pr, b_=j: wqchunk[:, p_, (b_ + jj) * P:(b_ + jj + 1) * P],
                        m, wstg_out[:, pr], f"wstg{pr}", evac="A",
                        src_bufs=(f"wqcA{pr}", f"wqcB{pr}"))
                    dma(dst_v[:, g0 + j:g0 + j + m, c * P:(c + 1) * P],
                        wstg_out[:, pr, 0:m, :],
                        [f"wstg{pr}"], [f"wstg{pr}", "wq_dram"], sem="DY", eng="G",
                        cost=3000)

            prep_tasks = []
            if "prep" in _phases:
                ti = 0
                for (w_dram, nrc, width, dst_v) in (
                        (wg_d, DC, D, wgq_v), (wu_d, DC, D, wuq_v),
                        (wd_d, KC, DFF, wdq_v)):
                    for c in range(nrc):
                        for s0 in range(0, width, 1024):
                            sw = min(1024, width - s0)
                            prep_tasks.append(
                                lambda w_=w_dram, c_=c, s_=s0, sw_=sw, d_=dst_v, t_=ti:
                                prep_task(w_, c_, s_, sw_, d_, t_))
                            ti += 1
            ntask = len(prep_tasks)
            tix = 0

            def drain_prep(n):
                nonlocal tix
                for _ in range(n):
                    if tix < ntask:
                        prep_tasks[tix]()
                        tix += 1

            nt1 = NT if "p1" in _phases else 0
            for i in range(nt1):
                seq = i // ST
                first = (i % ST == 0)
                last = (i % ST == ST - 1)
                if i % 32 == 0:
                    em.begin_sched()
                sp = i % 2
                stt = stscr[:, 8 * i:8 * i + 8]
                nmS = lambda s, sp_=sp: s + str(sp_)
                xv = x_sb[:, sp, :]
                xnv = xn_tm[:, sp, :]
                xbuf = nmS("x")
                xnbuf = nmS("xn")
                ksink = kvblk[:].rearrange("p a b -> p (a b)")
                dma(xv, x_t[i], [xbuf], [xbuf], sem="DX", eng="A", cost=5000)
                E("A", lambda e, st_=stt, x_=xv: e.activation(ksink, x_, AF.Square,
                                                       accum_out=st_[:, 0:1]),
                  reads=[xbuf, "kvblk", "brs"], writes=["kvblk", nmS("st_sq")], cost=1400)
                E("A", lambda e, st_=stt, x_=xv: e.activation(ksink, x_, AF.Copy,
                                                       accum_out=st_[:, 1:2]),
                  reads=[xbuf, "kvblk"], writes=["kvblk", nmS("st_sum")], cost=1400)
                E("V", lambda e, st_=stt: e.tensor_scalar(st_[:, 2:3], st_[:, 1:2],
                                                          1.0 / D, None, ALU.mult),
                  reads=[nmS("st_sum")], writes=[nmS("st_m")], cost=250)
                E("V", lambda e, st_=stt: e.tensor_tensor(st_[:, 3:4], st_[:, 2:3],
                                                          st_[:, 2:3], ALU.mult),
                  reads=[nmS("st_m")], writes=[nmS("st_m2")], cost=250)
                E("V", lambda e, st_=stt: e.scalar_tensor_tensor(
                    st_[:, 4:5], st_[:, 0:1], 1.0 / D, st_[:, 3:4],
                    ALU.mult, ALU.subtract),
                  reads=[nmS("st_sq"), nmS("st_m2")], writes=[nmS("st_u")], cost=250)
                E("V", lambda e, st_=stt: e.tensor_scalar(st_[:, 4:5], st_[:, 4:5], EPS,
                                                          None, ALU.add),
                  reads=[nmS("st_u")], writes=[nmS("st_u")], cost=250)
                E("A", lambda e, st_=stt: e.activation(st_[:, 5:6], st_[:, 4:5],
                                                       AF.Sqrt),
                  reads=[nmS("st_u")], writes=[nmS("st_sd")], cost=260)
                E("V", lambda e, ii=i, st_=stt: e.reciprocal(
                    stats[:, 2 * ii:2 * ii + 1], st_[:, 5:6]),
                  reads=[nmS("st_sd"), "stats"], writes=["stats"], cost=250)
                E("V", lambda e, ii=i, st_=stt: e.scalar_tensor_tensor(
                    stats[:, 2 * ii + 1:2 * ii + 2], st_[:, 2:3], -1.0,
                    stats[:, 2 * ii:2 * ii + 1], ALU.mult, ALU.mult),
                  reads=[nmS("st_m"), "stats"], writes=["stats"], cost=250)
                E("A", lambda e, ii=i, xn_=xnv, x_=xv: e.activation(
                    xn_, x_, AF.Identity,
                    bias=stats[:, 2 * ii + 1:2 * ii + 2], scale=stats[:, 2 * ii:2 * ii + 1]),
                  reads=[xbuf, "stats", xnbuf], writes=[xnbuf], cost=1400)
                u1v = u1p1[:, sp, :]
                u1b = nmS("u1p1")
                xnfmv = xnfm_s[:, sp, :, :]
                xnfmb = nmS("xnfm_s")
                ktv = k_tm[:, sp, :]
                ktb = nmS("k_tm")
                vtv = v_tm[:, sp, :]
                vtb = nmS("v_tm")
                qev = qe[:, sp, :]
                qeb = nmS("qe")
                qfmv = qfm_s[:, sp, :, :]
                qfmb = nmS("qfm_s")
                xqv_ = xq_sb[:, sp, :]
                xqfmv = xqfm_s[:, sp, :, :]
                xqfmb = nmS("xqfm_s")
                eAv = eluA[:, sp, :]
                eAb = nmS("eluA")
                eBv = eluB[:, sp, :]
                eBb = nmS("eluB")
                qsc = ((fA, "fA", "fA"), (qs1[:, 0, :], "qs1A0", "qs1B0"),
                       (fB, "fB", "fB")) if sp == 0 else                       ((fC, "fC", "fC"), (qs1[:, 1, :], "qs1A1", "qs1B1"),
                       (fD, "fD", "fD"))
                E("V", lambda e, xn_=xnv, u_=u1v, ub=u1b: e.tensor_copy(u_, xn_),
                  reads=[xnbuf, u1b], writes=[u1b], cost=1000)
                if _dbg:
                    dma(xn_dbg[i], xnv, [xnbuf], ["xdbg"], sem="DO", eng="S")
                    dma(u1_dbg[i], u1v, [u1b], ["udbg"], sem="DO", eng="S")
                    dma(x_dbg[i], xv, [xbuf], ["xdbg2"], sem="DO", eng="S")
                transpose_group(lambda j, u_=u1v: u_[:, j * P:(j + 1) * P], KC, xnfmv,
                                xnfmb, evac="A", src_bufs=(u1b,))
                dma(xnfm_d[i], xnfmv.rearrange("p c q -> p (c q)"),
                    [xnfmb], ["xnfm_dram"], sem="DY", eng="G", cost=3000)
                quant_wide(xnv, xnbuf, xqv_, (nmS("xqA"), nmS("xqB")), D,
                           scratch=qsc, g_off=True)
                transpose_group(lambda j, x_=xqv_: x_[:, j * P:(j + 1) * P], KC, xqfmv,
                                xqfmb, evac="A", src_bufs=(nmS("xqA"), nmS("xqB")))
                dma(xqfm_d[i], xqfmv.rearrange("p c q -> p (c q)"),
                    [xqfmb], ["xqfm_dram"], sem="DY", eng="G", cost=3000)
                for kc in range(KC):
                    for (ps, wsrc, wbuf, pbuf) in ((psK, wA, "wA", "psK"),
                                                   (psV, wB, "wB", "psV")):
                        for oc in range(2):
                            E("T", lambda e, k_=kc, o_=oc, p_=ps, w_=wsrc, x_=xnfmv:
                                e.matmul(
                                    p_[:, o_ * 512:(o_ + 1) * 512], x_[:, k_, :],
                                    w_[:, k_, o_ * 512:(o_ + 1) * 512],
                                    start=(k_ == 0), stop=(k_ == KC - 1)),
                              reads=[xnfmb, wbuf, pbuf], writes=[pbuf])
                E("V", lambda e, v_=vtv: e.tensor_copy(v_, psV[:]),
                  reads=["psV", vtb], writes=[vtb], cost=1100)
                E("A", lambda e, a_=eAv: e.activation(a_, psK[:], AF.Relu, scale=-1.0),
                  reads=["psK", eAb], writes=[eAb], cost=1400)
                E("A", lambda e, a_=eAv, b_=eBv: e.activation(b_, a_, AF.Exp, scale=-1.0),
                  reads=[eAb, eBb], writes=[eBb], cost=1400)
                E("V", lambda e, k_=ktv, b_=eBv: e.scalar_tensor_tensor(
                    k_, psK[:], 1.0, b_, ALU.add, ALU.max),
                  reads=["psK", eBb, ktb], writes=[ktb], cost=1300)
                for oc in range(KC):
                    for kc in range(KC):
                        E("T", lambda e, k_=kc, o_=oc, x_=xnfmv: e.matmul(
                            psV[:, o_ * P:(o_ + 1) * P],
                            wC[:, k_, o_ * P:(o_ + 1) * P], x_[:, k_, :],
                            start=(k_ == 0), stop=(k_ == KC - 1)),
                          reads=[xnfmb, "wC", "psV"], writes=["psV"])
                E("A", lambda e, a_=eAv: e.activation(a_, psV[:], AF.Relu, scale=-1.0),
                  reads=["psV", eAb], writes=[eAb], cost=1400)
                E("A", lambda e, a_=eAv, b_=eBv: e.activation(b_, a_, AF.Exp, scale=-1.0),
                  reads=[eAb, eBb], writes=[eBb], cost=1400)
                E("V", lambda e, q_=qfmv, b_=eBv: e.scalar_tensor_tensor(
                    q_.rearrange("p c q -> p (c q)"), psV[:], 1.0, b_,
                    ALU.add, ALU.max),
                  reads=["psV", eBb, qfmb], writes=[qfmb], cost=1300)
                for pair in range(NPAIR):
                    E("T", lambda e, p_=pair, f_=first, l_=last, k_=ktv, v_=vtv:
                        e.matmul(
                            psKV[:, p_, :], k_[:, p_ * P:(p_ + 1) * P],
                            v_[:, p_ * P:(p_ + 1) * P],
                            start=(f_ and p_ in (0, 4)), stop=(l_ and p_ in (3, 7))),
                      reads=[ktb, vtb, "psKV"], writes=["psKV"])
                    E("T", lambda e, p_=pair, f_=first, l_=last, k_=ktv: e.matmul(
                        psKS[:, p_:p_ + 1], k_[:, p_ * P:(p_ + 1) * P], ones_col[:],
                        start=(f_ and p_ == 0), stop=(l_ and p_ == NPAIR - 1)),
                      reads=[ktb, "ones", "psKS"], writes=["psKS"])
                dma(qfm_d[i], qfmv.rearrange("p c q -> p (c q)"),
                    [qfmb], ["qfm_dram"], sem="DY", eng="G", cost=3000)
                if last:
                    E("A", lambda e: e.activation(ks_sb[:], psKS[:], AF.Copy),
                      reads=["psKS", "ks_sb"], writes=["ks_sb"])
                    for pair in range(NPAIR):
                        E("A", lambda e, p_=pair: e.activation(
                            kvblk[:, p_, :], psKV[:, p_, :], AF.Copy),
                          reads=["psKV", "kvblk"], writes=["kvblk"])
                        E("V", lambda e, p_=pair: e.memset(kvblk[0:64, p_, 64:128], 0.0),
                          reads=["kvblk"], writes=["kvblk"])
                        E("V", lambda e, p_=pair: e.memset(kvblk[64:128, p_, 0:64], 0.0),
                          reads=["kvblk"], writes=["kvblk"])
                    dma(kv_dram[seq], kvblk[:].rearrange("p a b -> p (a b)"),
                        ["kvblk"], ["kv_dram"], sem="DY", eng="G")
                    E("V", lambda e, s_=seq: e.tensor_copy(KSp[:, s_, :, 0], ks_sb[:]),
                      reads=["ks_sb", "KSp"], writes=["KSp"])
                    E("V", lambda e, s_=seq: e.tensor_copy(KSp[:, s_, :, 1], ks_sb[:]),
                      reads=["ks_sb", "KSp"], writes=["KSp"])
                    E("V", lambda e, s_=seq: e.memset(KSp[64:128, s_, :, 0], 0.0),
                      reads=["KSp"], writes=["KSp"])
                    E("V", lambda e, s_=seq: e.memset(KSp[0:64, s_, :, 1], 0.0),
                      reads=["KSp"], writes=["KSp"])
                want = ((i + 1) * ntask) // max(nt1, 1)
                drain_prep(max(0, want - tix))
                if i % 32 == 31 or i == nt1 - 1:
                    em.end_sched()
            em.begin_sched()
            drain_prep(ntask)
            em.end_sched()
            if _dbg and nt1:
                dma(st_dbg[:], stats[:], ["stats"], ["sdbg"], sem="DO", eng="S")

        em.barrier()

        wgT_q = sb("wgT_q", [P, KC, DFF], BF16)
        wuT_q = sb("wuT_q", [P, KC, DFF], BF16)
        wdT_q = sb("wdT_q", [P, DC, D], BF16)
        xnr = sb("xnr", [P, 3, KC, P], BF16)
        xqr = sb("xqr", [P, 3, KC, P], BF16)
        qfr = sb("qfr", [P, 2, KC, P], BF16)
        edge = sb("edge", [P, 3, KC], BF16)
        fcol = sb("fcol", [P, 3, KC], BF16)
        loc = sb("loc", [P, 2, KC, P], BF16)
        xam = sb("xam", [P, KC, P], BF16)
        g_sb = sb("g_sb", [P, 2, D], BF16)
        xhq = sb("xhq", [P, 2, 768], BF16)
        u3 = sb("u3", [P, 2, 768], BF16)
        hB = sb("hB", [P, 768], F32)
        hqloc = sb("hqloc", [P, 6, P], BF16)
        psAD = psa("psAD", [P, 2 * P + HEADS], F32)
        psG1 = psa("psG1", [P, 512], F32)
        psG2 = psa("psG2", [P, 512], F32)
        psQ = psa("psQ", [P, P], F32)
        psG3 = psa("psG3", [P, 512], F32)
        psDown = psa("psDown", [P, D], F32)
        psA_v = psAD[:, 0:P]
        psDn_v = psAD[:, P:P + HEADS]
        psQ_v = psQ[:, :]
        psGs = [(psG1, "psG1"), (psG2, "psG2"), (psG3, "psG3")]
        gctr = [0]

        def p2_reloads():
            for si, (s0, sw) in enumerate(SEGS):
                dma(wgT_q[:, :, s0:s0 + sw], wgq_v[:, :, s0:s0 + sw],
                    ["wq_dram"], [f"wgTs{si}"], sem="DW", eng="S", cost=12000)
            for si, (s0, sw) in enumerate(SEGS):
                dma(wuT_q[:, :, s0:s0 + sw], wuq_v[:, :, s0:s0 + sw],
                    ["wq_dram"], [f"wuTs{si}"], sem="DY", eng="G", cost=12000)
            dma(wdT_q[:, :, 0:512], wdq_v[:, :, 0:512],
                ["wq_dram"], ["wdTh0"], sem="DW", eng="S", cost=16000)
            dma(wdT_q[:, :, 512:1024], wdq_v[:, :, 512:1024],
                ["wq_dram"], ["wdTh1"], sem="DY", eng="G", cost=16000)
            for c in range(KC):
                for h in range(2):
                    dma(qv[:, 0:512], wpT_d[c * P:(c + 1) * P, h * 512:(h + 1) * 512],
                        ["qv0"], ["qv0"], sem="DX", eng="A", cost=2000)
                    E("A", lambda e, cc=c, h_=h: e.activation(
                        wB[:, cc, h_ * 512:(h_ + 1) * 512], qv[:, 0:512], AF.Copy),
                      reads=["qv0", "wB"], writes=["wB"], cost=700)

        def p2_prefetch(j):
            r3 = j % 3
            dma(xnr[:, r3, :, :].rearrange("p c q -> p (c q)"), xnfm_d[j],
                ["xnfm_dram", f"xnr{r3}"], [f"xnr{r3}"],
                sem="DX", eng="A", cost=2500)
            E("V", lambda e, r_=r3: e.tensor_copy(edge[:, r_, :], xnr[:, r_, :, P - 1]),
              reads=[f"xnr{r3}"], writes=[f"edge{r3}"], cost=200)
            E("V", lambda e, r_=r3: e.tensor_copy(fcol[:, r_, :], xnr[:, r_, :, 0]),
              reads=[f"xnr{r3}"], writes=[f"fcol{r3}"], cost=200)
            dma(xqr[:, r3, :, :].rearrange("p c q -> p (c q)"), xqfm_d[j],
                ["xqfm_dram", f"xqr{r3}"], [f"xqr{r3}"], sem="DQ", eng="S", cost=2500)
            r2q = j % 2
            dma(qfr[:, r2q, :, :].rearrange("p c q -> p (c q)"), qfm_d[j],
                ["qfm_dram", f"qfr{r2q}"], [f"qfr{r2q}"], sem="DQ", eng="S", cost=2500)

        def p2_tile(j):
            seq = j // ST
            first = (j % ST == 0)
            last = (j % ST == ST - 1)
            r3 = j % 3
            sp = j % 2
            xqv = xqr[:, r3, :, :]
            xqb = f"xqr{r3}"
            qfv = qfr[:, sp, :, :]
            qfb = f"qfr{sp}"
            locv = loc[:, sp, :, :]
            locb = f"loc{sp}"
            xamv = xam[:, :, :]
            xamb = "xam"
            gv = g_sb[:, sp, :]
            gb = f"g_sb{sp}"
            if first:
                dma(kvblk[:].rearrange("p a b -> p (a b)"), kv_dram[seq],
                    ["kv_dram", "kvblk"], ["kvblk"], sem="DQ", eng="S", cost=3000)
            xv = xnr[:, r3, :, :]
            xb = f"xnr{r3}"
            w0b = convw_b[:, :, 0].unsqueeze(2).broadcast_to([P, KC, P])
            w1b = convw_b[:, :, 1].unsqueeze(2).broadcast_to([P, KC, P])
            w2b = convw_b[:, :, 2].unsqueeze(2).broadcast_to([P, KC, P])
            cbb = convb_b[:, :].unsqueeze(2).broadcast_to([P, KC, P])
            s1v = fA[:, 0:D].rearrange("p (c q) -> p c q", q=P)
            s2v = fB[:, 0:D].rearrange("p (c q) -> p c q", q=P)
            E("G", lambda e: e.tensor_tensor(s1v, xv, w1b, ALU.mult),
              reads=[xb, "convw_b", "fA"], writes=["fA"], cost=2400)
            E("G", lambda e: e.tensor_tensor(s2v[:, :, 1:P], xv[:, :, 0:P - 1],
                                             w0b[:, :, 1:P], ALU.mult),
              reads=[xb, "convw_b", "fB"], writes=["fB"], cost=2400)
            E("G", lambda e: e.tensor_tensor(s1v, s1v, cbb, ALU.add),
              reads=["fA", "convb_b"], writes=["fA"], cost=2400)
            if first:
                E("G", lambda e: e.memset(s2v[:, :, 0:1], 0.0),
                  reads=["fB"], writes=["fB"], cost=300)
            else:
                lsl = (j - 1) % 3
                E("G", lambda e, l_=lsl: e.tensor_tensor(
                    s2v[:, :, 0:1], edge[:, l_, :].unsqueeze(2),
                    convw_b[:, :, 0].unsqueeze(2), ALU.mult),
                  reads=[f"edge{lsl}", "convw_b", "fB"], writes=["fB"], cost=300)
            E("G", lambda e: e.tensor_tensor(fA[:, 0:D], fA[:, 0:D], fB[:, 0:D], ALU.add),
              reads=["fA", "fB"], writes=["fA"], cost=2200)
            E("G", lambda e: e.tensor_tensor(s2v[:, :, 0:P - 1], xv[:, :, 1:P],
                                             w2b[:, :, 0:P - 1], ALU.mult),
              reads=[xb, "convw_b", "fB"], writes=["fB"], cost=2400)
            if last:
                E("G", lambda e: e.memset(s2v[:, :, P - 1:P], 0.0),
                  reads=["fB"], writes=["fB"], cost=300)
            else:
                rsl = (j + 1) % 3
                E("G", lambda e, r_=rsl: e.tensor_tensor(
                    s2v[:, :, P - 1:P], fcol[:, r_, :].unsqueeze(2),
                    convw_b[:, :, 2].unsqueeze(2), ALU.mult),
                  reads=[f"fcol{rsl}", "convw_b", "fB"], writes=["fB"], cost=300)
            E("G", lambda e, lv=locv, lb=locb: e.tensor_tensor(lv, s1v, s2v, ALU.add),
              reads=["fA", "fB", locb], writes=[locb], cost=2200)
            if j + 2 < NT:
                p2_prefetch(j + 2)

            def seg_list(s):
                s0, sw = SEGS[s]
                ops = []
                for b0 in range(0, sw, 512):
                    bw = min(512, sw - b0)
                    pg, ng = psGs[gctr[0] % 3]
                    pu, nu = psGs[(gctr[0] + 1) % 3]
                    gctr[0] += 2
                    for kc in range(KC):
                        ops.append(("T", (lambda e, k_=kc, a_=s0 + b0, w_=bw, p_=pg:
                            e.matmul(
                                p_[:, 0:w_], xqv[:, k_, :], wgT_q[:, k_, a_:a_ + w_],
                                start=(k_ == 0), stop=(k_ == KC - 1))),
                            [xqb, f"wgTs{s}", ng], [ng]))
                        ops.append(("T", (lambda e, k_=kc, a_=s0 + b0, w_=bw, p_=pu:
                            e.matmul(
                                p_[:, 0:w_], xqv[:, k_, :], wuT_q[:, k_, a_:a_ + w_],
                                start=(k_ == 0), stop=(k_ == KC - 1))),
                            [xqb, f"wuTs{s}", nu], [nu]))
                    hv, hb = (fB, "fB") if s % 2 == 0 else (hB, "hB")
                    u3v = u3[:, s % 2, :]
                    u3b = f"u3{s % 2}"
                    ops += [
                        ("A", (lambda e, w_=bw, p_=pg: e.activation(
                            fA[:, 0:w_], p_[:, 0:w_], AF.Tanh, scale=0.5)),
                         [ng, "fA"], ["fA"]),
                        ("V", (lambda e, w_=bw, p_=pg: e.scalar_tensor_tensor(
                            fA[:, 0:w_], fA[:, 0:w_], 1.0, p_[:, 0:w_],
                            ALU.add, ALU.mult)), ["fA", ng], ["fA"]),
                        ("V", (lambda e, b_=b0, w_=bw, p_=pu, h_=hv: e.scalar_tensor_tensor(
                            h_[:, b_:b_ + w_], fA[:, 0:w_], 0.5, p_[:, 0:w_],
                            ALU.mult, ALU.mult)), ["fA", nu, hb], [hb]),
                        ("A", (lambda e, b_=b0, w_=bw, h_=hv, u_=u3v: e.activation(
                            u_[:, b_:b_ + w_], h_[:, b_:b_ + w_], AF.Sign)),
                         [hb, u3b], [u3b]),
                    ]
                return ops

            def qnt_list(s):
                s0, sw = SEGS[s]
                hv, hb = (fB, "fB") if s % 2 == 0 else (hB, "hB")
                return quant_ops(hv[:, 0:sw], hb, xhq[:, s % 2, 0:sw],
                                 (f"hqA{s % 2}", f"hqB{s % 2}"),
                                 sw, 0, sgn_pre=(u3[:, s % 2, 0:sw], f"u3{s % 2}"),
                                 force_wide=True, g_off=True)

            def tdown(s):
                s0, sw = SEGS[s]
                nch = sw // P
                sp2 = s % 2
                hql = hqloc
                hqb = "hqloc"
                transpose_group(lambda jj, s_=sp2: xhq[:, s_, jj * P:(jj + 1) * P], nch,
                                hql, hqb, evac="A",
                                src_bufs=(f"hqA{sp2}", f"hqB{sp2}"))
                for lc in range(nch):
                    gk = s0 // P + lc
                    for half in range(2):
                        E("T", lambda e, l_=lc, g_=gk, h_=half, q_=hql: e.matmul(
                            psDown[:, h_ * 512:(h_ + 1) * 512], q_[:, l_, :],
                            wdT_q[:, g_, h_ * 512:(h_ + 1) * 512],
                            start=(g_ == 0), stop=(g_ == DC - 1)),
                          reads=[hqb, f"wdTh{half}", f"psDn{half}"],
                          writes=[f"psDn{half}"])

            nseg = len(SEGS)
            pend = None
            for s in range(nseg):
                if s == 0:
                    emit_list(seg_list(0))
                else:
                    emit_zip(pend, seg_list(s))
                    tdown(s - 1)
                pend = qnt_list(s)
            emit_list(pend)
            tdown(nseg - 1)
            E("A", lambda e: e.activation(fA[:, 0:D], psDown[:], AF.Tanh, scale=0.5),
              reads=["psDn0", "psDn1", "fA"], writes=["fA"], cost=1400)
            E("V", lambda e: e.scalar_tensor_tensor(fA[:, 0:D], fA[:, 0:D], 1.0,
                                                    psDown[:], ALU.add, ALU.mult),
              reads=["fA", "psDn0", "psDn1"], writes=["fA"], cost=1300)
            E("A", lambda e, g_=gv, gb_=gb: e.activation(g_, fA[:, 0:D], AF.Copy, scale=0.5),
              reads=["fA", gb], writes=[gb], cost=1400)

            dma(qv[:], x_t[j], ["qv0", "qv1"], ["qv0", "qv1"], sem="DX", eng="A",
                cost=5000)

            for pair in range(NPAIR):
                E("T", lambda e, p_=pair, q_=seq: e.matmul(
                    psDn_v[:, 2 * p_:2 * p_ + 2], qfv[:, p_, :], KSp[:, q_, p_, :],
                    start=True, stop=True),
                  reads=[qfb, "KSp", "psAD"], writes=["psAD"])
            E("V", lambda e: e.tensor_scalar(rden[:, :], psDn_v, 1e-6, None, ALU.max),
              reads=["psAD", "rden"], writes=["rden"], cost=250)
            E("V", lambda e: e.reciprocal(rden[:, :], rden[:, :]),
              reads=["rden"], writes=["rden"], cost=300)
            for pair in range(NPAIR):
                pv = psA_v if pair % 2 == 0 else psQ_v
                pb = "psAD" if pair % 2 == 0 else "psQ"
                E("T", lambda e, p_=pair, v_=pv, b_=pb: e.matmul(
                    v_, qfv[:, p_, :], kvblk[:, p_, :], start=True, stop=True),
                  reads=[qfb, "kvblk", pb], writes=[pb])
                rb = rden[:, 2 * pair:2 * pair + 2].unsqueeze(2).broadcast_to([P, 2, HD])
                E("V", lambda e, p_=pair, rb_=rb, v_=pv, b_=pb: e.tensor_tensor(
                    u2am[:, p_ * P:(p_ + 1) * P].rearrange("p (h q) -> p h q", q=HD),
                    v_.rearrange("p (h q) -> p h q", q=HD), rb_, ALU.mult),
                  reads=[pb, "rden", "u20", "u21"], writes=["u20", "u21"], cost=400)
            transpose_group(lambda jj: u2am[:, jj * P:(jj + 1) * P], KC, xamv,
                            xamb, loc_fuse=(locv, locb), src_bufs=("u20", "u21"))
            for half in range(2):
                for kc in range(KC):
                    E("T", lambda e, k_=kc, h_=half: e.matmul(
                        psDown[:, h_ * 512:(h_ + 1) * 512], xamv[:, k_, :],
                        wB[:, k_, h_ * 512:(h_ + 1) * 512], start=(k_ == 0), stop=False),
                      reads=[xamb, "wB", f"psDn{half}"], writes=[f"psDn{half}"])
                E("T", lambda e, h_=half: e.matmul(
                    psDown[:, h_ * 512:(h_ + 1) * 512], ones_row[:, :],
                    bp_b[:, h_ * 512:(h_ + 1) * 512], start=False, stop=True),
                  reads=["ones_r", "bp_b", f"psDn{half}"], writes=[f"psDn{half}"])
                E("V", lambda e, h_=half: e.tensor_tensor(
                    u2am[:, h_ * 512:(h_ + 1) * 512], psDown[:, h_ * 512:(h_ + 1) * 512],
                    gv[:, h_ * 512:(h_ + 1) * 512], ALU.mult),
                  reads=[f"psDn{half}", gb, "u20" if half == 0 else "u21"],
                  writes=["u20" if half == 0 else "u21"])
                E("V", lambda e, h_=half: e.tensor_tensor(
                    qv[:, h_ * 512:(h_ + 1) * 512], qv[:, h_ * 512:(h_ + 1) * 512],
                    u2am[:, h_ * 512:(h_ + 1) * 512], ALU.add),
                  reads=[f"qv{half}", "u20" if half == 0 else "u21"],
                  writes=[f"qv{half}"])
                dma(out_t[j][:, half * 512:(half + 1) * 512], qv[:, half * 512:(half + 1) * 512],
                    [f"qv{half}"], ["out_dma"], sem="DO")

        if "p2" in _phases:
            em.begin_sched()
            p2_reloads()
            for jj in range(min(2, NT)):
                p2_prefetch(jj)
            em.end_sched()
            for j in range(NT):
                if j % 16 == 0:
                    em.begin_sched()
                p2_tile(j)
                if j % 16 == 15 or j == NT - 1:
                    em.end_sched()
        else:
            for j in range(NT):
                dma(out_t[j], qv[:], ["qv0", "qv1"], ["out_dma"], sem="DO")

        with nc.semaphore("sT") as sT, nc.semaphore("sV") as sV, nc.semaphore("sA") as sA, \
             nc.semaphore("sG") as sG, nc.semaphore("sDW") as sDW, nc.semaphore("sDX") as sDX, \
             nc.semaphore("sDQ") as sDQ, nc.semaphore("sDO") as sDO, \
             nc.semaphore("sDY") as sDY, nc.Block() as blk:
            sems = {"T": sT, "V": sV, "A": sA, "G": sG, "DW": sDW, "DX": sDX,
                    "DQ": sDQ, "DO": sDO, "DY": sDY}

            def runner(ename):
                def body(eng):
                    for waits, fn, sem, inc in em.ops[ename]:
                        for s, c in waits:
                            eng.wait_ge(sems[s], c)
                        if fn is not None:
                            fn(eng).then_inc(sems[sem], inc)
                    if ename == "S":
                        eng.wait_ge(sDO, em.cnt["DO"])
                return body

            blk.tensor(runner("T"))
            blk.vector(runner("V"))
            blk.scalar(runner("A"))
            blk.gpsimd(runner("G"))
            blk.sync(runner("S"))

    return nc


def kernel(L0_attn_x, L0_attn_Wq, L0_attn_Wk, L0_attn_Wv, L0_attn_Wp, L0_attn_bp,
           L0_attn_conv_w, L0_attn_conv_b, L0_ffn_w_gate, L0_ffn_w_up, L0_ffn_w_down):
    x = np.ascontiguousarray(np.asarray(L0_attn_x, dtype=np.float32))
    B, SL, _ = x.shape
    per = B // 8
    wqT = np.ascontiguousarray(np.asarray(L0_attn_Wq, np.float32).T)
    wkT = np.ascontiguousarray(np.asarray(L0_attn_Wk, np.float32).T)
    wvT = np.ascontiguousarray(np.asarray(L0_attn_Wv, np.float32).T)
    wpT = np.ascontiguousarray(np.asarray(L0_attn_Wp, np.float32).T)
    bp = np.ascontiguousarray(np.asarray(L0_attn_bp, np.float32).reshape(1, D))
    convw = np.ascontiguousarray(np.asarray(L0_attn_conv_w, np.float32))
    convb = np.ascontiguousarray(np.asarray(L0_attn_conv_b, np.float32).reshape(KC, P).T)
    wg = np.ascontiguousarray(np.asarray(L0_ffn_w_gate, np.float32))
    wu = np.ascontiguousarray(np.asarray(L0_ffn_w_up, np.float32))
    wd = np.ascontiguousarray(np.asarray(L0_ffn_w_down, np.float32))

    nc = build_program(seq_len=SL, n_seq=per)
    in_maps = []
    for c in range(8):
        shard = np.ascontiguousarray(x[c * per:(c + 1) * per].reshape(per * SL, D))
        in_maps.append({"x": shard, "wqT": wqT, "wkT": wkT, "wvT": wvT, "wpT": wpT,
                        "bp": bp, "convw": convw, "convb": convb,
                        "wg": wg, "wu": wu, "wd": wd})
    res = run_bass_kernel_spmd(nc, in_maps, core_ids=list(range(8)))
    out = np.concatenate([res.results[c]["out"].reshape(per, SL, D) for c in range(8)], axis=0)
    return np.ascontiguousarray(out.astype(np.float32))


# version 35
# speedup vs baseline: 1.0225x; 1.0055x over previous
import os
import numpy as np
from contextlib import ExitStack

import concourse.bass as bass
import concourse.mybir as mybir
from concourse.bass_utils import run_bass_kernel_spmd

F32 = mybir.dt.float32
BF16 = mybir.dt.bfloat16
I16 = mybir.dt.int16
AF = mybir.ActivationFunctionType
ALU = mybir.AluOpType
AX = mybir.AxisListType

P = 128
D = 1024
DFF = 2816
KC = D // P
DC = DFF // P
HEADS = 16
HD = 64
NPAIR = 8
EPS = 1e-5
SEGS = [(0, 768), (768, 768), (1536, 768), (2304, 512)]

_SIM = os.environ.get("BASS_SIM_FP64") == "1"
CB = 0.5 if _SIM else 0.0
_TRACE = []
_LAST_EM = None


class Emit:

    DMA_SEMS = ("DW", "DX", "DQ", "DO", "DY")

    def __init__(self):
        self.ops = {e: [] for e in ("T", "V", "A", "G", "S")}
        self.cnt = {s: 0 for s in ("T", "V", "A", "G") + self.DMA_SEMS}
        self.vc = {e: {} for e in self.ops}
        self.buf = {}
        self.sched = None

    def op(self, eng, fn, reads=(), writes=(), dma_sem=None, cost=None):
        if self.sched is not None:
            self.sched.append((eng, fn, tuple(reads), tuple(writes), dma_sem, cost))
            return
        self._op(eng, fn, reads, writes, dma_sem)

    def begin_sched(self):
        assert self.sched is None
        if os.environ.get("KSCHED") == "0":
            return
        self.sched = []

    def end_sched(self):
        if self.sched is None:
            return
        ops, self.sched = self.sched, None
        for k in _list_schedule(ops):
            eng, fn, reads, writes, dma_sem, _ = ops[k]
            self._op(eng, fn, reads, writes, dma_sem)

    def _op(self, eng, fn, reads=(), writes=(), dma_sem=None):
        if os.environ.get("KTRACE") == "1":
            _TRACE.append((eng, dma_sem, tuple(reads), tuple(writes)))
        sem = dma_sem if dma_sem else eng
        inc = 16 if dma_sem else 1
        if dma_sem:
            reads = list(reads) + [f"_tok_{sem}"]
            writes = list(writes) + [f"_tok_{sem}"]
        needed = {}
        for b in reads:
            st = self.buf.get(b)
            if st and st["w"]:
                s, c = st["w"]
                needed[s] = max(needed.get(s, 0), c)
        for b in writes:
            st = self.buf.get(b)
            if st:
                if st["w"]:
                    s, c = st["w"]
                    needed[s] = max(needed.get(s, 0), c)
                for s, c in st["r"].items():
                    needed[s] = max(needed.get(s, 0), c)
        waits = []
        vc = self.vc[eng]
        for s, c in sorted(needed.items()):
            if eng == "T" and s == "T":
                continue
            if vc.get(s, 0) < c:
                waits.append((s, c))
                vc[s] = c
        self.cnt[sem] += inc
        cnt = self.cnt[sem]
        self.ops[eng].append((waits, fn, sem, inc))
        for b in reads:
            st = self.buf.setdefault(b, {"w": None, "r": {}})
            st["r"][sem] = max(st["r"].get(sem, 0), cnt)
        for b in writes:
            self.buf[b] = {"w": (sem, cnt), "r": {}}

    def barrier(self):
        assert self.sched is None
        snapshot = dict(self.cnt)
        for e in self.ops:
            vc = self.vc[e]
            waits = []
            for s, c in sorted(snapshot.items()):
                if e == "T" and s == "T":
                    continue
                if c > 0 and vc.get(s, 0) < c:
                    waits.append((s, c))
                    vc[s] = c
            if waits:
                self.ops[e].append((waits, None, None, 0))


_ENG_COST = {"T": 260, "V": 700, "A": 720, "G": 1900, "S": 150}


def _list_schedule(ops):
    n = len(ops)
    busy = [0.0] * n
    lat = [0.0] * n
    for i, (eng, fn, rd, wr, dsem, cost) in enumerate(ops):
        if dsem:
            busy[i] = 150.0
            lat[i] = float(cost) if cost is not None else 4000.0
        else:
            busy[i] = float(cost) if cost is not None else _ENG_COST[eng]
            lat[i] = busy[i] + 100.0
    last_w = {}
    readers = {}
    preds = [None] * n
    succ = [[] for _ in range(n)]
    indeg = [0] * n
    for i, (eng, fn, rd, wr, dsem, cost) in enumerate(ops):
        r = list(rd)
        w = list(wr)
        if dsem:
            tok = "_tok_" + dsem
            r.append(tok)
            w.append(tok)
        dset = set()
        for b in r:
            j = last_w.get(b)
            if j is not None:
                dset.add(j)
        for b in w:
            j = last_w.get(b)
            if j is not None:
                dset.add(j)
            dset.update(readers.get(b, ()))
        dset.discard(i)
        preds[i] = dset
        for j in dset:
            succ[j].append(i)
        indeg[i] = len(dset)
        for b in r:
            readers.setdefault(b, set()).add(i)
        for b in w:
            last_w[b] = i
            readers[b] = set()
    prio = [0.0] * n
    for i in range(n - 1, -1, -1):
        m = 0.0
        for j in succ[i]:
            if prio[j] > m:
                m = prio[j]
        prio[i] = m + lat[i]
    eng_t = {e: 0.0 for e in _ENG_COST}
    avail = [0.0] * n
    ready = [i for i in range(n) if indeg[i] == 0]
    order = []
    while ready:
        best = -1
        best_key = None
        best_est = 0.0
        m = None
        ests = []
        for i in ready:
            est = eng_t[ops[i][0]]
            for j in preds[i]:
                if avail[j] > est:
                    est = avail[j]
            ests.append(est)
            if m is None or est < m:
                m = est
        for i, est in zip(ready, ests):
            if est <= m + 400.0:
                key = (-prio[i], i)
                if best < 0 or key < best_key:
                    best, best_key, best_est = i, key, est
        i = best
        eng_t[ops[i][0]] = best_est + busy[i]
        avail[i] = best_est + lat[i]
        order.append(i)
        ready.remove(i)
        for j in succ[i]:
            indeg[j] -= 1
            if indeg[j] == 0:
                ready.append(j)
    assert len(order) == n
    return order


def build_program(seq_len=2048, n_seq=2, _phases=("prep", "p1", "p2"), _dbg=False):
    S = seq_len * n_seq
    NT = S // P
    ST = seq_len // P
    assert seq_len % P == 0 and ST >= 2
    _ik = "ExternalOutput" if _dbg else "Internal"

    nc = bass.Bass("TRN2", target_bir_lowering=False, debug=False)

    x_d = nc.dram_tensor("x", [S, D], F32, kind="ExternalInput").ap()
    wqT_d = nc.dram_tensor("wqT", [D, D], F32, kind="ExternalInput").ap()
    wkT_d = nc.dram_tensor("wkT", [D, D], F32, kind="ExternalInput").ap()
    wvT_d = nc.dram_tensor("wvT", [D, D], F32, kind="ExternalInput").ap()
    wpT_d = nc.dram_tensor("wpT", [D, D], F32, kind="ExternalInput").ap()
    bp_d = nc.dram_tensor("bp", [1, D], F32, kind="ExternalInput").ap()
    convw_d = nc.dram_tensor("convw", [D, 3], F32, kind="ExternalInput").ap()
    convb_d = nc.dram_tensor("convb", [P, KC], F32, kind="ExternalInput").ap()
    wg_d = nc.dram_tensor("wg", [DFF, D], F32, kind="ExternalInput").ap()
    wu_d = nc.dram_tensor("wu", [DFF, D], F32, kind="ExternalInput").ap()
    wd_d = nc.dram_tensor("wd", [D, DFF], F32, kind="ExternalInput").ap()
    out_d = nc.dram_tensor("out", [S, D], F32, kind="ExternalOutput").ap()
    kv_dram = nc.dram_tensor("kv_scratch", [n_seq, P, NPAIR * P], BF16, kind=_ik).ap()
    xnfm_d = nc.dram_tensor("xnfm_d", [NT, P, D], BF16, kind=_ik).ap()
    xqfm_d = nc.dram_tensor("xqfm_d", [NT, P, D], BF16, kind=_ik).ap()
    qfm_d = nc.dram_tensor("qfm_d", [NT, P, D], BF16, kind=_ik).ap()
    wgq_d = nc.dram_tensor("wgq_d", [P, KC * DFF], BF16, kind=_ik).ap()
    wuq_d = nc.dram_tensor("wuq_d", [P, KC * DFF], BF16, kind=_ik).ap()
    wdq_d = nc.dram_tensor("wdq_d", [P, DC * D], BF16, kind=_ik).ap()
    wgq_v = wgq_d.rearrange("p (k f) -> p k f", f=DFF)
    wuq_v = wuq_d.rearrange("p (k f) -> p k f", f=DFF)
    wdq_v = wdq_d.rearrange("p (k f) -> p k f", f=D)
    if _dbg:
        xn_dbg = nc.dram_tensor("xn_dbg", [NT, P, D], F32, kind="ExternalOutput").ap()
        u1_dbg = nc.dram_tensor("u1_dbg", [NT, P, D], BF16, kind="ExternalOutput").ap()
        x_dbg = nc.dram_tensor("x_dbg", [NT, P, D], F32, kind="ExternalOutput").ap()
        st_dbg = nc.dram_tensor("st_dbg", [P, 2 * NT], F32, kind="ExternalOutput").ap()

    x_t = x_d.rearrange("(t p) d -> t p d", p=P)
    out_t = out_d.rearrange("(t p) d -> t p d", p=P)

    em = Emit()
    global _LAST_EM
    _LAST_EM = em
    E = em.op

    def dma(dst, src, rb, wb, sem="DW", eng="S", cost=None):
        E(eng, lambda e, d_=dst, s_=src: e.dma_start(out=d_, in_=s_), reads=rb, writes=wb,
          dma_sem=sem, cost=cost)

    with ExitStack() as ctx:
        sb = lambda name, shape, dt=F32: ctx.enter_context(nc.sbuf_tensor(name, shape, dt))
        psa = lambda name, shape, dt=F32: ctx.enter_context(nc.psum_tensor(name, shape, dt))

        ident_b = sb("ident_b", [P, P], BF16)
        ones_col = sb("ones_col", [P, 1], BF16)
        ones_row = sb("ones_row", [1, P], BF16)
        bp_b = sb("bp_b", [1, D], BF16)
        convw_b = sb("convw_b", [P, KC, 3], BF16)
        convb_b = sb("convb_b", [P, KC], BF16)
        qv = sb("qv", [P, D], F32)
        u1 = sb("u1", [P, D], BF16)
        u2am = sb("u2am", [P, D], BF16)
        fA = sb("fA", [P, D], F32)
        fB = sb("fB", [P, D], F32)
        bscale = sb("bscale", [P, 112], F32)
        brs = sb("brs", [P, 128], F32)
        psT = psa("psT", [P, KC, P], BF16)
        rden = brs[:, 112:128]
        kvblk = sb("kvblk", [P, NPAIR, P], BF16)
        KSp = sb("KSp", [P, n_seq, NPAIR, 2], BF16)
        wB = sb("wB", [P, KC, D], BF16)

        def quant_ops(src_ap, src_buf, out_ap, out_buf, width, par, scratch=None,
                      sgn_pre=None, g_off=False, force_wide=False):
            nb = width // 16
            wide = force_wide or width > 512
            if scratch is None:
                scratch = ((qv, "qv0", "qv1"), (u1, "u10", "u11"), (u2am, "u20", "u21"))
            (qvt, qn0, qn1), (u1t, un0, un1), (u2t, mn0, mn1) = scratch
            if wide:
                assert width <= 1024 and par == 0
                o = 0
                ob = 0
                qb, ub1, ub2 = [qn0, qn1], [un0, un1], [mn0, mn1]
            else:
                o = par * 512
                ob = par * 32
                qb, ub1, ub2 = [qn0 if par == 0 else qn1], \
                               [un0 if par == 0 else un1], [mn0 if par == 0 else mn1]
            sv = src_ap.rearrange("p (b k) -> p b k", k=16)
            qvp = qvt[:, o:o + width]
            u1p = u1t[:, o:o + width]
            u1i = u1p.bitcast(I16)
            u2p = u2t[:, o:o + width]
            if wide:
                bsc = bscale[:, 64:64 + nb]
                brp = brs[:, 64:64 + nb]
            else:
                bsc = bscale[:, ob:ob + nb]
                brp = brs[:, ob:ob + nb]
            rs_b = brp.unsqueeze(2).broadcast_to([P, nb, 16])
            sc_b = bsc.unsqueeze(2).broadcast_to([P, nb, 16])
            bb, rb = ("bscW", "brsW") if wide else (f"bsc{par}", f"brs{par}")
            ob_ = list(out_buf) if isinstance(out_buf, (tuple, list)) else [out_buf]
            wv = 160 + int(1.12 * width)
            wa = 230 + width
            wg = (160 + int(2.2 * width)) if g_off else wv
            GV = "G" if g_off else "V"
            if _SIM:
                ops = [
                    ("V", lambda e: e.tensor_reduce(bsc, sv, AX.X, ALU.max,
                                                    apply_absolute_value=True),
                     [src_buf, bb], [bb]),
                    ("A", lambda e: e.activation(qvp, src_ap, AF.Abs), [src_buf] + qb, qb),
                    ("V", lambda e: e.tensor_scalar(bsc, bsc, 1.0 / 6.0, 1e-12,
                                                    ALU.mult, ALU.max), [bb], [bb]),
                    ("V", lambda e: e.reciprocal(brp, bsc), [bb, rb], [rb]),
                    (GV,
                     lambda e: e.tensor_tensor(qvp.rearrange("p (b k) -> p b k", k=16),
                                               qvp.rearrange("p (b k) -> p b k", k=16),
                                               rs_b, ALU.mult), qb + [rb], qb),
                    ("V", lambda e: e.tensor_scalar(out_ap, qvp, 5.0, -2.0,
                                                    ALU.is_le, ALU.mult), qb + ob_, ob_),
                    ("A", lambda e: e.activation(u1i, qvp, AF.Copy, bias=CB, scale=2.0),
                     qb + ub1, ub1),
                    ("V", lambda e: e.tensor_scalar(u2p, u1i, 4.0, 0.5, ALU.min, ALU.mult),
                     ub1, ub2),
                    ("A", lambda e: e.activation(u1i, qvp, AF.Copy, bias=CB, scale=1.0),
                     qb + ub1, ub1),
                    ("V", lambda e: e.tensor_scalar(u1p, u1i, 2.0, 4.0, ALU.max, ALU.min),
                     ub1, ub1),
                    (GV, lambda e: e.tensor_add(u2p, u2p, u1p), ub2 + ub1, ub2),
                    (GV, lambda e: e.tensor_add(u2p, u2p, out_ap), ub2 + ob_, ub2),
                ]
                cst = [wv, wa, 300, 400, wg, wv, wa, wv, wa, wv, wg, wg]
                if sgn_pre is None:
                    ops.append(("A", lambda e: e.activation(out_ap, src_ap, AF.Sign),
                                [src_buf] + ob_, ob_))
                    ops.append((GV, lambda e: e.tensor_tensor(u2p, u2p, out_ap, ALU.mult),
                                ub2 + ob_, ub2))
                    cst += [wa, wg]
                else:
                    sga, sgb = sgn_pre
                    ops.append(("V", lambda e: e.tensor_tensor(u2p, u2p, sga, ALU.mult),
                                ub2 + [sgb], ub2))
                    cst += [wv]
            else:
                C2 = float(1.5 * 2 ** 21)
                C3 = float(1.5 * 2 ** 22)
                ops = [
                    ("V", lambda e: e.tensor_reduce(bsc, sv, AX.X, ALU.max,
                                                    apply_absolute_value=True),
                     [src_buf, bb], [bb]),
                    ("V", lambda e: e.tensor_scalar(
                        qvp.bitcast(mybir.dt.int32), src_ap.bitcast(mybir.dt.int32),
                        0x7FFFFFFF, None, ALU.bitwise_and), [src_buf] + qb, qb),
                    ("V", lambda e: e.tensor_scalar(bsc, bsc, 1.0 / 3.0, 2e-12,
                                                    ALU.mult, ALU.max), [bb], [bb]),
                    ("V", lambda e: e.reciprocal(brp, bsc), [bb, rb], [rb]),
                    (GV,
                     lambda e: e.tensor_tensor(qvp.rearrange("p (b k) -> p b k", k=16),
                                               qvp.rearrange("p (b k) -> p b k", k=16),
                                               rs_b, ALU.mult), qb + [rb], qb),
                    ("V", lambda e: e.tensor_scalar(u1p, qvp, C2, C2,
                                                    ALU.add, ALU.subtract), qb + ub1, ub1),
                    ("V", lambda e: e.tensor_scalar(u2p, qvp, C3, C3 + 1.0,
                                                    ALU.add, ALU.subtract), qb + ub2, ub2),
                    ("V", lambda e: e.tensor_scalar(u2p, u2p, 1.0, 0.0,
                                                    ALU.min, ALU.max), ub2, ub2),
                    ("V", lambda e: e.scalar_tensor_tensor(u2p, u1p, 1.0, u2p,
                                                           ALU.min, ALU.add),
                     ub1 + ub2, ub2),
                    ("V", lambda e: e.scalar_tensor_tensor(u2p, qvp, 2.5, u2p,
                                                           ALU.is_gt, ALU.add),
                     qb + ub2, ub2),
                ]
                cst = [wv, wv, 300, 400, wg, wv, wv, wv, wv, wv]
                if sgn_pre is None:
                    ops.append(("A", lambda e: e.activation(out_ap, src_ap, AF.Sign),
                                [src_buf] + ob_, ob_))
                    ops.append((GV, lambda e: e.tensor_tensor(u2p, u2p, out_ap, ALU.mult),
                                ub2 + ob_, ub2))
                    cst += [wa, wg]
                else:
                    sga, sgb = sgn_pre
                    ops.append((GV, lambda e: e.tensor_tensor(u2p, u2p, sga, ALU.mult),
                                ub2 + [sgb], ub2))
                    cst += [wg]
            ops.append((GV, lambda e: e.tensor_tensor(
                out_ap.rearrange("p (b k) -> p b k", k=16),
                u2p.rearrange("p (b k) -> p b k", k=16), sc_b, ALU.mult),
                ub2 + [bb], ob_))
            cst += [wg]
            return [t + (c,) for t, c in zip(ops, cst)]

        def emit_list(ops):
            for t in ops:
                E(t[0], t[1], reads=t[2], writes=t[3],
                  cost=(t[4] if len(t) > 4 else None))

        def emit_zip(la, lb):
            i = j = 0
            while i < len(la) or j < len(lb):
                if i < len(la):
                    t = la[i]
                    E(t[0], t[1], reads=t[2], writes=t[3],
                      cost=(t[4] if len(t) > 4 else None))
                    i += 1
                if j < len(lb):
                    t = lb[j]
                    E(t[0], t[1], reads=t[2], writes=t[3],
                      cost=(t[4] if len(t) > 4 else None))
                    j += 1

        def quant_wide(src_ap, src_buf, out_ap, out_names, width, scratch=None,
                       g_off=False):
            nA, nB = out_names
            if width <= 512:
                emit_list(quant_ops(src_ap, src_buf, out_ap, nA, width, 0, scratch,
                                    g_off=g_off))
                return
            w0 = 512
            w1 = width - 512
            emit_zip(quant_ops(src_ap[:, 0:w0], src_buf, out_ap[:, 0:w0], nA, w0, 0,
                               scratch, g_off=g_off),
                     quant_ops(src_ap[:, w0:width], src_buf, out_ap[:, w0:width],
                               nB, w1, 1, scratch, g_off=g_off))

        def transpose_group(src_ap_fn, nchunks, dst_t, dst_buf, loc_fuse=None,
                            evac="V", dst_off=0, src_bufs=()):
            rsrc = list(src_bufs)
            step = 4 if os.environ.get("KTS", "0") == "1" else KC
            for j in range(0, nchunks, step):
                m = min(step, nchunks - j)
                hf = (j // 4) % 2 if step == 4 else 0
                pb = f"psT{hf}"
                for c in range(m):
                    E("T", lambda e, jj=j + c, cc=4 * hf + c: e.transpose(
                        psT[:, cc, :], src_ap_fn(jj), ident_b[:]),
                      reads=rsrc + ["ident", pb], writes=[pb])
                if loc_fuse is None:
                    if evac == "A":
                        E("A", lambda e, jj=j + dst_off, mm=m, h_=hf: e.activation(
                            dst_t[:, jj:jj + mm, :].rearrange("p c q -> p (c q)"),
                            psT[:, 4 * h_:4 * h_ + mm, :].rearrange("p c q -> p (c q)"),
                            AF.Copy),
                          reads=[pb, dst_buf], writes=[dst_buf], cost=230 + m * P)
                    else:
                        E("V", lambda e, jj=j + dst_off, mm=m, h_=hf: e.tensor_copy(
                            dst_t[:, jj:jj + mm, :].rearrange("p c q -> p (c q)"),
                            psT[:, 4 * h_:4 * h_ + mm, :].rearrange("p c q -> p (c q)")),
                          reads=[pb], writes=[dst_buf], cost=160 + m * P)
                else:
                    loc_t, loc_buf = loc_fuse
                    E("V", lambda e, jj=j, mm=m, h_=hf: e.scalar_tensor_tensor(
                        dst_t[:, jj:jj + mm, :].rearrange("p c q -> p (c q)"),
                        psT[:, 4 * h_:4 * h_ + mm, :].rearrange("p c q -> p (c q)"), 1.0,
                        loc_t[:, jj:jj + mm, :].rearrange("p c q -> p (c q)"),
                        ALU.mult, ALU.add),
                      reads=[pb, loc_buf], writes=[dst_buf], cost=160 + m * P)

        E("G", lambda e: e.memset(ident_b[:], 0.0), writes=["ident"])
        E("G", lambda e: e.affine_select(out=ident_b[:], in_=ident_b[:],
                                         compare_op=ALU.not_equal, fill=1.0, base=0,
                                         pattern=[[-1, P]], channel_multiplier=1),
          reads=["ident"], writes=["ident"])
        E("G", lambda e: e.memset(ones_col[:], 1.0), writes=["ones"])
        E("G", lambda e: e.memset(ones_row[:], 1.0), writes=["ones_r"])

        with ExitStack() as pctx:
            sbp = lambda name, shape, dt: pctx.enter_context(nc.sbuf_tensor(name, shape, dt))
            bp_f32 = sbp("bp_f32", [1, D], F32)
            convw_sb = sbp("convw_sb", [P, KC, 3], F32)
            convb_sb = sbp("convb_sb", [P, KC], F32)

            dma(bp_f32[:], bp_d[:], [], ["bp_f32"])
            E("V", lambda e: e.tensor_copy(bp_b[:], bp_f32[:]), reads=["bp_f32"], writes=["bp_b"])
            dma(convw_sb[:], convw_d.rearrange("(c p) k -> p c k", p=P), [], ["convw"])
            dma(convb_sb[:], convb_d[:], [], ["convb"])
            E("V", lambda e: e.tensor_copy(convw_b[:], convw_sb[:]),
              reads=["convw"], writes=["convw_b"])
            E("V", lambda e: e.tensor_copy(convb_b[:], convb_sb[:]),
              reads=["convb"], writes=["convb_b"])

        em.barrier()

        with ExitStack() as p1ctx:
            sb1 = lambda name, shape, dt: p1ctx.enter_context(nc.sbuf_tensor(name, shape, dt))
            ps1 = lambda name, shape, dt: p1ctx.enter_context(nc.psum_tensor(name, shape, dt))
            wA = sb1("wA", [P, KC, D], BF16)
            wC = sb1("wC", [P, KC, D], BF16)
            stats = sb1("stats", [P, 2 * NT], F32)
            stscr = sb1("stscr", [P, 8 * NT], F32)
            x_sb = sb1("x_sb", [P, 2, D], F32)
            xn_tm = sb1("xn_tm", [P, 2, D], F32)
            u1p1 = sb1("u1p1", [P, 2, D], BF16)
            qe = sb1("qe", [P, 2, D], BF16)
            xq_sb = sb1("xq_sb", [P, 2, D], BF16)
            qs1 = sb1("qs1", [P, 2, D], BF16)
            eluA = sb1("eluA", [P, 2, D], F32)
            eluB = sb1("eluB", [P, 2, D], F32)
            fC = sb1("fC", [P, D], F32)
            fD = sb1("fD", [P, D], F32)
            xnfm_s = sb1("xnfm_s", [P, 2, KC, P], BF16)
            xqfm_s = sb1("xqfm_s", [P, 2, KC, P], BF16)
            qfm_s = sb1("qfm_s", [P, 2, KC, P], BF16)
            k_tm = sb1("k_tm", [P, 2, D], BF16)
            v_tm = sb1("v_tm", [P, 2, D], BF16)
            ks_sb = sb1("ks_sb", [P, NPAIR], F32)
            wstage = sb1("wstage", [P, 2, D], F32)
            wqchunk = sb1("wqchunk", [P, 2, D], BF16)
            wstg_out = sb1("wstg_out", [P, 2, KC, P], BF16)
            wload = sb1("wload", [P, 2, D], F32)
            psK = ps1("psK", [P, D], F32)
            psV = ps1("psV", [P, D], F32)
            psKV = ps1("psKV", [P, NPAIR, P], F32)
            psKS = ps1("psKS", [P, NPAIR], F32)

            wjobs = [(wkT_d, wA, "wA"), (wvT_d, wB, "wB"), (wqT_d, wC, "wC")]
            for jx, (w_ap, dst, dbuf) in enumerate(wjobs):
                for c in range(KC):
                    sl = (jx * KC + c) % 2
                    sem, eng = ("DW", "S") if sl == 0 else ("DY", "G")
                    dma(wload[:, sl, :], w_ap[c * P:(c + 1) * P, :],
                        [f"wload{sl}"], [f"wload{sl}"], sem=sem, eng=eng, cost=3000)
                    E("V", lambda e, cc=c, dd=dst, s_=sl: e.tensor_copy(
                        dd[:, cc, :], wload[:, s_, :]),
                      reads=[f"wload{sl}"], writes=[dbuf], cost=1000)

            def prep_task(w_dram, c, s0, sw, dst_v, ti):
                pr = ti % 2
                wst = wstage[:, pr, 0:sw]
                wqc = wqchunk[:, pr, :]
                dma(wst, w_dram[c * P:(c + 1) * P, s0:s0 + sw],
                    [f"wstage{pr}"], [f"wstage{pr}"], cost=5000)
                quant_wide(wst, f"wstage{pr}", wqc[:, 0:sw],
                           (f"wqcA{pr}", f"wqcB{pr}"), sw, g_off=True)
                g0 = s0 // P
                for j in range(0, sw // P, KC):
                    m = min(KC, sw // P - j)
                    transpose_group(
                        lambda jj, p_=pr, b_=j: wqchunk[:, p_, (b_ + jj) * P:(b_ + jj + 1) * P],
                        m, wstg_out[:, pr], f"wstg{pr}", evac="A",
                        src_bufs=(f"wqcA{pr}", f"wqcB{pr}"))
                    dma(dst_v[:, g0 + j:g0 + j + m, c * P:(c + 1) * P],
                        wstg_out[:, pr, 0:m, :],
                        [f"wstg{pr}"], [f"wstg{pr}", "wq_dram"], sem="DY", eng="G",
                        cost=3000)

            prep_tasks = []
            if "prep" in _phases:
                ti = 0
                for (w_dram, nrc, width, dst_v) in (
                        (wg_d, DC, D, wgq_v), (wu_d, DC, D, wuq_v),
                        (wd_d, KC, DFF, wdq_v)):
                    for c in range(nrc):
                        for s0 in range(0, width, 1024):
                            sw = min(1024, width - s0)
                            prep_tasks.append(
                                lambda w_=w_dram, c_=c, s_=s0, sw_=sw, d_=dst_v, t_=ti:
                                prep_task(w_, c_, s_, sw_, d_, t_))
                            ti += 1
            ntask = len(prep_tasks)
            tix = 0

            def drain_prep(n):
                nonlocal tix
                for _ in range(n):
                    if tix < ntask:
                        prep_tasks[tix]()
                        tix += 1

            nt1 = NT if "p1" in _phases else 0
            for i in range(nt1):
                seq = i // ST
                first = (i % ST == 0)
                last = (i % ST == ST - 1)
                if i % 16 == 0:
                    em.begin_sched()
                sp = i % 2
                stt = stscr[:, 8 * i:8 * i + 8]
                nmS = lambda s, sp_=sp: s + str(sp_)
                xv = x_sb[:, sp, :]
                xnv = xn_tm[:, sp, :]
                xbuf = nmS("x")
                xnbuf = nmS("xn")
                ksink = kvblk[:].rearrange("p a b -> p (a b)")
                dma(xv, x_t[i], [xbuf], [xbuf], sem="DX", eng="A", cost=5000)
                E("A", lambda e, st_=stt, x_=xv: e.activation(ksink, x_, AF.Square,
                                                       accum_out=st_[:, 0:1]),
                  reads=[xbuf, "kvblk", "brs"], writes=["kvblk", nmS("st_sq")], cost=1400)
                E("A", lambda e, st_=stt, x_=xv: e.activation(ksink, x_, AF.Copy,
                                                       accum_out=st_[:, 1:2]),
                  reads=[xbuf, "kvblk"], writes=["kvblk", nmS("st_sum")], cost=1400)
                E("V", lambda e, st_=stt: e.tensor_scalar(st_[:, 2:3], st_[:, 1:2],
                                                          1.0 / D, None, ALU.mult),
                  reads=[nmS("st_sum")], writes=[nmS("st_m")], cost=250)
                E("V", lambda e, st_=stt: e.tensor_tensor(st_[:, 3:4], st_[:, 2:3],
                                                          st_[:, 2:3], ALU.mult),
                  reads=[nmS("st_m")], writes=[nmS("st_m2")], cost=250)
                E("V", lambda e, st_=stt: e.scalar_tensor_tensor(
                    st_[:, 4:5], st_[:, 0:1], 1.0 / D, st_[:, 3:4],
                    ALU.mult, ALU.subtract),
                  reads=[nmS("st_sq"), nmS("st_m2")], writes=[nmS("st_u")], cost=250)
                E("V", lambda e, st_=stt: e.tensor_scalar(st_[:, 4:5], st_[:, 4:5], EPS,
                                                          None, ALU.add),
                  reads=[nmS("st_u")], writes=[nmS("st_u")], cost=250)
                E("A", lambda e, st_=stt: e.activation(st_[:, 5:6], st_[:, 4:5],
                                                       AF.Sqrt),
                  reads=[nmS("st_u")], writes=[nmS("st_sd")], cost=260)
                E("V", lambda e, ii=i, st_=stt: e.reciprocal(
                    stats[:, 2 * ii:2 * ii + 1], st_[:, 5:6]),
                  reads=[nmS("st_sd"), "stats"], writes=["stats"], cost=250)
                E("V", lambda e, ii=i, st_=stt: e.scalar_tensor_tensor(
                    stats[:, 2 * ii + 1:2 * ii + 2], st_[:, 2:3], -1.0,
                    stats[:, 2 * ii:2 * ii + 1], ALU.mult, ALU.mult),
                  reads=[nmS("st_m"), "stats"], writes=["stats"], cost=250)
                E("A", lambda e, ii=i, xn_=xnv, x_=xv: e.activation(
                    xn_, x_, AF.Identity,
                    bias=stats[:, 2 * ii + 1:2 * ii + 2], scale=stats[:, 2 * ii:2 * ii + 1]),
                  reads=[xbuf, "stats", xnbuf], writes=[xnbuf], cost=1400)
                u1v = u1p1[:, sp, :]
                u1b = nmS("u1p1")
                xnfmv = xnfm_s[:, sp, :, :]
                xnfmb = nmS("xnfm_s")
                ktv = k_tm[:, sp, :]
                ktb = nmS("k_tm")
                vtv = v_tm[:, sp, :]
                vtb = nmS("v_tm")
                qev = qe[:, sp, :]
                qeb = nmS("qe")
                qfmv = qfm_s[:, sp, :, :]
                qfmb = nmS("qfm_s")
                xqv_ = xq_sb[:, sp, :]
                xqfmv = xqfm_s[:, sp, :, :]
                xqfmb = nmS("xqfm_s")
                eAv = eluA[:, sp, :]
                eAb = nmS("eluA")
                eBv = eluB[:, sp, :]
                eBb = nmS("eluB")
                qsc = ((fA, "fA", "fA"), (qs1[:, 0, :], "qs1A0", "qs1B0"),
                       (fB, "fB", "fB")) if sp == 0 else                       ((fC, "fC", "fC"), (qs1[:, 1, :], "qs1A1", "qs1B1"),
                       (fD, "fD", "fD"))
                E("V", lambda e, xn_=xnv, u_=u1v, ub=u1b: e.tensor_copy(u_, xn_),
                  reads=[xnbuf, u1b], writes=[u1b], cost=1000)
                if _dbg:
                    dma(xn_dbg[i], xnv, [xnbuf], ["xdbg"], sem="DO", eng="S")
                    dma(u1_dbg[i], u1v, [u1b], ["udbg"], sem="DO", eng="S")
                    dma(x_dbg[i], xv, [xbuf], ["xdbg2"], sem="DO", eng="S")
                transpose_group(lambda j, u_=u1v: u_[:, j * P:(j + 1) * P], KC, xnfmv,
                                xnfmb, evac="A", src_bufs=(u1b,))
                dma(xnfm_d[i], xnfmv.rearrange("p c q -> p (c q)"),
                    [xnfmb], ["xnfm_dram"], sem="DY", eng="G", cost=3000)
                quant_wide(xnv, xnbuf, xqv_, (nmS("xqA"), nmS("xqB")), D,
                           scratch=qsc, g_off=True)
                transpose_group(lambda j, x_=xqv_: x_[:, j * P:(j + 1) * P], KC, xqfmv,
                                xqfmb, evac="A", src_bufs=(nmS("xqA"), nmS("xqB")))
                dma(xqfm_d[i], xqfmv.rearrange("p c q -> p (c q)"),
                    [xqfmb], ["xqfm_dram"], sem="DY", eng="G", cost=3000)
                for kc in range(KC):
                    for (ps, wsrc, wbuf, pbuf) in ((psK, wA, "wA", "psK"),
                                                   (psV, wB, "wB", "psV")):
                        for oc in range(2):
                            E("T", lambda e, k_=kc, o_=oc, p_=ps, w_=wsrc, x_=xnfmv:
                                e.matmul(
                                    p_[:, o_ * 512:(o_ + 1) * 512], x_[:, k_, :],
                                    w_[:, k_, o_ * 512:(o_ + 1) * 512],
                                    start=(k_ == 0), stop=(k_ == KC - 1)),
                              reads=[xnfmb, wbuf, pbuf], writes=[pbuf])
                E("V", lambda e, v_=vtv: e.tensor_copy(v_, psV[:]),
                  reads=["psV", vtb], writes=[vtb], cost=1100)
                E("A", lambda e, a_=eAv: e.activation(a_, psK[:], AF.Relu, scale=-1.0),
                  reads=["psK", eAb], writes=[eAb], cost=1400)
                E("A", lambda e, a_=eAv, b_=eBv: e.activation(b_, a_, AF.Exp, scale=-1.0),
                  reads=[eAb, eBb], writes=[eBb], cost=1400)
                E("V", lambda e, k_=ktv, b_=eBv: e.scalar_tensor_tensor(
                    k_, psK[:], 1.0, b_, ALU.add, ALU.max),
                  reads=["psK", eBb, ktb], writes=[ktb], cost=1300)
                for oc in range(KC):
                    for kc in range(KC):
                        E("T", lambda e, k_=kc, o_=oc, x_=xnfmv: e.matmul(
                            psV[:, o_ * P:(o_ + 1) * P],
                            wC[:, k_, o_ * P:(o_ + 1) * P], x_[:, k_, :],
                            start=(k_ == 0), stop=(k_ == KC - 1)),
                          reads=[xnfmb, "wC", "psV"], writes=["psV"])
                E("A", lambda e, a_=eAv: e.activation(a_, psV[:], AF.Relu, scale=-1.0),
                  reads=["psV", eAb], writes=[eAb], cost=1400)
                E("A", lambda e, a_=eAv, b_=eBv: e.activation(b_, a_, AF.Exp, scale=-1.0),
                  reads=[eAb, eBb], writes=[eBb], cost=1400)
                E("V", lambda e, q_=qfmv, b_=eBv: e.scalar_tensor_tensor(
                    q_.rearrange("p c q -> p (c q)"), psV[:], 1.0, b_,
                    ALU.add, ALU.max),
                  reads=["psV", eBb, qfmb], writes=[qfmb], cost=1300)
                for pair in range(NPAIR):
                    E("T", lambda e, p_=pair, f_=first, l_=last, k_=ktv, v_=vtv:
                        e.matmul(
                            psKV[:, p_, :], k_[:, p_ * P:(p_ + 1) * P],
                            v_[:, p_ * P:(p_ + 1) * P],
                            start=(f_ and p_ in (0, 4)), stop=(l_ and p_ in (3, 7))),
                      reads=[ktb, vtb, "psKV"], writes=["psKV"])
                    E("T", lambda e, p_=pair, f_=first, l_=last, k_=ktv: e.matmul(
                        psKS[:, p_:p_ + 1], k_[:, p_ * P:(p_ + 1) * P], ones_col[:],
                        start=(f_ and p_ == 0), stop=(l_ and p_ == NPAIR - 1)),
                      reads=[ktb, "ones", "psKS"], writes=["psKS"])
                dma(qfm_d[i], qfmv.rearrange("p c q -> p (c q)"),
                    [qfmb], ["qfm_dram"], sem="DY", eng="G", cost=3000)
                if last:
                    E("A", lambda e: e.activation(ks_sb[:], psKS[:], AF.Copy),
                      reads=["psKS", "ks_sb"], writes=["ks_sb"])
                    for pair in range(NPAIR):
                        E("A", lambda e, p_=pair: e.activation(
                            kvblk[:, p_, :], psKV[:, p_, :], AF.Copy),
                          reads=["psKV", "kvblk"], writes=["kvblk"])
                        E("V", lambda e, p_=pair: e.memset(kvblk[0:64, p_, 64:128], 0.0),
                          reads=["kvblk"], writes=["kvblk"])
                        E("V", lambda e, p_=pair: e.memset(kvblk[64:128, p_, 0:64], 0.0),
                          reads=["kvblk"], writes=["kvblk"])
                    dma(kv_dram[seq], kvblk[:].rearrange("p a b -> p (a b)"),
                        ["kvblk"], ["kv_dram"], sem="DY", eng="G")
                    E("V", lambda e, s_=seq: e.tensor_copy(KSp[:, s_, :, 0], ks_sb[:]),
                      reads=["ks_sb", "KSp"], writes=["KSp"])
                    E("V", lambda e, s_=seq: e.tensor_copy(KSp[:, s_, :, 1], ks_sb[:]),
                      reads=["ks_sb", "KSp"], writes=["KSp"])
                    E("V", lambda e, s_=seq: e.memset(KSp[64:128, s_, :, 0], 0.0),
                      reads=["KSp"], writes=["KSp"])
                    E("V", lambda e, s_=seq: e.memset(KSp[0:64, s_, :, 1], 0.0),
                      reads=["KSp"], writes=["KSp"])
                want = ((i + 1) * ntask) // max(nt1, 1)
                drain_prep(max(0, want - tix))
                if i % 16 == 15 or i == nt1 - 1:
                    em.end_sched()
            em.begin_sched()
            drain_prep(ntask)
            em.end_sched()
            if _dbg and nt1:
                dma(st_dbg[:], stats[:], ["stats"], ["sdbg"], sem="DO", eng="S")

        em.barrier()

        wgT_q = sb("wgT_q", [P, KC, DFF], BF16)
        wuT_q = sb("wuT_q", [P, KC, DFF], BF16)
        wdT_q = sb("wdT_q", [P, DC, D], BF16)
        xnr = sb("xnr", [P, 3, KC, P], BF16)
        xqr = sb("xqr", [P, 3, KC, P], BF16)
        qfr = sb("qfr", [P, 2, KC, P], BF16)
        edge = sb("edge", [P, 3, KC], BF16)
        fcol = sb("fcol", [P, 3, KC], BF16)
        loc = sb("loc", [P, 2, KC, P], BF16)
        xam = sb("xam", [P, KC, P], BF16)
        g_sb = sb("g_sb", [P, 2, D], BF16)
        xhq = sb("xhq", [P, 2, 768], BF16)
        u3 = sb("u3", [P, 2, 768], BF16)
        hB = sb("hB", [P, 768], F32)
        hqloc = sb("hqloc", [P, 6, P], BF16)
        psAD = psa("psAD", [P, 2 * P + HEADS], F32)
        psG1 = psa("psG1", [P, 512], F32)
        psG2 = psa("psG2", [P, 512], F32)
        psQ = psa("psQ", [P, P], F32)
        psG3 = psa("psG3", [P, 512], F32)
        psDown = psa("psDown", [P, D], F32)
        psA_v = psAD[:, 0:P]
        psDn_v = psAD[:, P:P + HEADS]
        psQ_v = psQ[:, :]
        psGs = [(psG1, "psG1"), (psG2, "psG2"), (psG3, "psG3")]
        gctr = [0]

        def p2_reloads():
            for si, (s0, sw) in enumerate(SEGS):
                dma(wgT_q[:, :, s0:s0 + sw], wgq_v[:, :, s0:s0 + sw],
                    ["wq_dram"], [f"wgTs{si}"], sem="DW", eng="S", cost=12000)
            for si, (s0, sw) in enumerate(SEGS):
                dma(wuT_q[:, :, s0:s0 + sw], wuq_v[:, :, s0:s0 + sw],
                    ["wq_dram"], [f"wuTs{si}"], sem="DY", eng="G", cost=12000)
            dma(wdT_q[:, :, 0:512], wdq_v[:, :, 0:512],
                ["wq_dram"], ["wdTh0"], sem="DW", eng="S", cost=16000)
            dma(wdT_q[:, :, 512:1024], wdq_v[:, :, 512:1024],
                ["wq_dram"], ["wdTh1"], sem="DY", eng="G", cost=16000)
            for c in range(KC):
                for h in range(2):
                    dma(qv[:, 0:512], wpT_d[c * P:(c + 1) * P, h * 512:(h + 1) * 512],
                        ["qv0"], ["qv0"], sem="DX", eng="A", cost=2000)
                    E("A", lambda e, cc=c, h_=h: e.activation(
                        wB[:, cc, h_ * 512:(h_ + 1) * 512], qv[:, 0:512], AF.Copy),
                      reads=["qv0", "wB"], writes=["wB"], cost=700)

        def p2_prefetch(j):
            r3 = j % 3
            dma(xnr[:, r3, :, :].rearrange("p c q -> p (c q)"), xnfm_d[j],
                ["xnfm_dram", f"xnr{r3}"], [f"xnr{r3}"],
                sem="DX", eng="A", cost=2500)
            E("V", lambda e, r_=r3: e.tensor_copy(edge[:, r_, :], xnr[:, r_, :, P - 1]),
              reads=[f"xnr{r3}"], writes=[f"edge{r3}"], cost=200)
            E("V", lambda e, r_=r3: e.tensor_copy(fcol[:, r_, :], xnr[:, r_, :, 0]),
              reads=[f"xnr{r3}"], writes=[f"fcol{r3}"], cost=200)
            dma(xqr[:, r3, :, :].rearrange("p c q -> p (c q)"), xqfm_d[j],
                ["xqfm_dram", f"xqr{r3}"], [f"xqr{r3}"], sem="DQ", eng="S", cost=2500)
            r2q = j % 2
            dma(qfr[:, r2q, :, :].rearrange("p c q -> p (c q)"), qfm_d[j],
                ["qfm_dram", f"qfr{r2q}"], [f"qfr{r2q}"], sem="DQ", eng="S", cost=2500)

        def p2_tile(j):
            seq = j // ST
            first = (j % ST == 0)
            last = (j % ST == ST - 1)
            r3 = j % 3
            sp = j % 2
            xqv = xqr[:, r3, :, :]
            xqb = f"xqr{r3}"
            qfv = qfr[:, sp, :, :]
            qfb = f"qfr{sp}"
            locv = loc[:, sp, :, :]
            locb = f"loc{sp}"
            xamv = xam[:, :, :]
            xamb = "xam"
            gv = g_sb[:, sp, :]
            gb = f"g_sb{sp}"
            if first:
                dma(kvblk[:].rearrange("p a b -> p (a b)"), kv_dram[seq],
                    ["kv_dram", "kvblk"], ["kvblk"], sem="DQ", eng="S", cost=3000)
            xv = xnr[:, r3, :, :]
            xb = f"xnr{r3}"
            w0b = convw_b[:, :, 0].unsqueeze(2).broadcast_to([P, KC, P])
            w1b = convw_b[:, :, 1].unsqueeze(2).broadcast_to([P, KC, P])
            w2b = convw_b[:, :, 2].unsqueeze(2).broadcast_to([P, KC, P])
            cbb = convb_b[:, :].unsqueeze(2).broadcast_to([P, KC, P])
            s1v = fA[:, 0:D].rearrange("p (c q) -> p c q", q=P)
            s2v = fB[:, 0:D].rearrange("p (c q) -> p c q", q=P)
            E("G", lambda e: e.tensor_tensor(s1v, xv, w1b, ALU.mult),
              reads=[xb, "convw_b", "fA"], writes=["fA"], cost=2400)
            E("G", lambda e: e.tensor_tensor(s2v[:, :, 1:P], xv[:, :, 0:P - 1],
                                             w0b[:, :, 1:P], ALU.mult),
              reads=[xb, "convw_b", "fB"], writes=["fB"], cost=2400)
            E("G", lambda e: e.tensor_tensor(s1v, s1v, cbb, ALU.add),
              reads=["fA", "convb_b"], writes=["fA"], cost=2400)
            if first:
                E("G", lambda e: e.memset(s2v[:, :, 0:1], 0.0),
                  reads=["fB"], writes=["fB"], cost=300)
            else:
                lsl = (j - 1) % 3
                E("G", lambda e, l_=lsl: e.tensor_tensor(
                    s2v[:, :, 0:1], edge[:, l_, :].unsqueeze(2),
                    convw_b[:, :, 0].unsqueeze(2), ALU.mult),
                  reads=[f"edge{lsl}", "convw_b", "fB"], writes=["fB"], cost=300)
            E("G", lambda e: e.tensor_tensor(fA[:, 0:D], fA[:, 0:D], fB[:, 0:D], ALU.add),
              reads=["fA", "fB"], writes=["fA"], cost=2200)
            E("G", lambda e: e.tensor_tensor(s2v[:, :, 0:P - 1], xv[:, :, 1:P],
                                             w2b[:, :, 0:P - 1], ALU.mult),
              reads=[xb, "convw_b", "fB"], writes=["fB"], cost=2400)
            if last:
                E("G", lambda e: e.memset(s2v[:, :, P - 1:P], 0.0),
                  reads=["fB"], writes=["fB"], cost=300)
            else:
                rsl = (j + 1) % 3
                E("G", lambda e, r_=rsl: e.tensor_tensor(
                    s2v[:, :, P - 1:P], fcol[:, r_, :].unsqueeze(2),
                    convw_b[:, :, 2].unsqueeze(2), ALU.mult),
                  reads=[f"fcol{rsl}", "convw_b", "fB"], writes=["fB"], cost=300)
            E("G", lambda e, lv=locv, lb=locb: e.tensor_tensor(lv, s1v, s2v, ALU.add),
              reads=["fA", "fB", locb], writes=[locb], cost=2200)
            if j + 2 < NT:
                p2_prefetch(j + 2)

            def seg_list(s):
                s0, sw = SEGS[s]
                ops = []
                for b0 in range(0, sw, 512):
                    bw = min(512, sw - b0)
                    pg, ng = psGs[gctr[0] % 3]
                    pu, nu = psGs[(gctr[0] + 1) % 3]
                    gctr[0] += 2
                    for kc in range(KC):
                        ops.append(("T", (lambda e, k_=kc, a_=s0 + b0, w_=bw, p_=pg:
                            e.matmul(
                                p_[:, 0:w_], xqv[:, k_, :], wgT_q[:, k_, a_:a_ + w_],
                                start=(k_ == 0), stop=(k_ == KC - 1))),
                            [xqb, f"wgTs{s}", ng], [ng]))
                        ops.append(("T", (lambda e, k_=kc, a_=s0 + b0, w_=bw, p_=pu:
                            e.matmul(
                                p_[:, 0:w_], xqv[:, k_, :], wuT_q[:, k_, a_:a_ + w_],
                                start=(k_ == 0), stop=(k_ == KC - 1))),
                            [xqb, f"wuTs{s}", nu], [nu]))
                    hv, hb = (fB, "fB") if s % 2 == 0 else (hB, "hB")
                    u3v = u3[:, s % 2, :]
                    u3b = f"u3{s % 2}"
                    ops += [
                        ("A", (lambda e, w_=bw, p_=pg: e.activation(
                            fA[:, 0:w_], p_[:, 0:w_], AF.Tanh, scale=0.5)),
                         [ng, "fA"], ["fA"]),
                        ("V", (lambda e, w_=bw, p_=pg: e.scalar_tensor_tensor(
                            fA[:, 0:w_], fA[:, 0:w_], 1.0, p_[:, 0:w_],
                            ALU.add, ALU.mult)), ["fA", ng], ["fA"]),
                        ("V", (lambda e, b_=b0, w_=bw, p_=pu, h_=hv: e.scalar_tensor_tensor(
                            h_[:, b_:b_ + w_], fA[:, 0:w_], 0.5, p_[:, 0:w_],
                            ALU.mult, ALU.mult)), ["fA", nu, hb], [hb]),
                        ("A", (lambda e, b_=b0, w_=bw, h_=hv, u_=u3v: e.activation(
                            u_[:, b_:b_ + w_], h_[:, b_:b_ + w_], AF.Sign)),
                         [hb, u3b], [u3b]),
                    ]
                return ops

            def qnt_list(s):
                s0, sw = SEGS[s]
                hv, hb = (fB, "fB") if s % 2 == 0 else (hB, "hB")
                return quant_ops(hv[:, 0:sw], hb, xhq[:, s % 2, 0:sw],
                                 (f"hqA{s % 2}", f"hqB{s % 2}"),
                                 sw, 0, sgn_pre=(u3[:, s % 2, 0:sw], f"u3{s % 2}"),
                                 force_wide=True, g_off=True)

            def tdown(s):
                s0, sw = SEGS[s]
                nch = sw // P
                sp2 = s % 2
                hql = hqloc
                hqb = "hqloc"
                transpose_group(lambda jj, s_=sp2: xhq[:, s_, jj * P:(jj + 1) * P], nch,
                                hql, hqb, evac="A",
                                src_bufs=(f"hqA{sp2}", f"hqB{sp2}"))
                for lc in range(nch):
                    gk = s0 // P + lc
                    for half in range(2):
                        E("T", lambda e, l_=lc, g_=gk, h_=half, q_=hql: e.matmul(
                            psDown[:, h_ * 512:(h_ + 1) * 512], q_[:, l_, :],
                            wdT_q[:, g_, h_ * 512:(h_ + 1) * 512],
                            start=(g_ == 0), stop=(g_ == DC - 1)),
                          reads=[hqb, f"wdTh{half}", f"psDn{half}"],
                          writes=[f"psDn{half}"])

            nseg = len(SEGS)
            pend = None
            for s in range(nseg):
                if s == 0:
                    emit_list(seg_list(0))
                else:
                    emit_zip(pend, seg_list(s))
                    tdown(s - 1)
                pend = qnt_list(s)
            emit_list(pend)
            tdown(nseg - 1)
            E("A", lambda e: e.activation(fA[:, 0:D], psDown[:], AF.Tanh, scale=0.5),
              reads=["psDn0", "psDn1", "fA"], writes=["fA"], cost=1400)
            E("V", lambda e: e.scalar_tensor_tensor(fA[:, 0:D], fA[:, 0:D], 1.0,
                                                    psDown[:], ALU.add, ALU.mult),
              reads=["fA", "psDn0", "psDn1"], writes=["fA"], cost=1300)
            E("A", lambda e, g_=gv, gb_=gb: e.activation(g_, fA[:, 0:D], AF.Copy, scale=0.5),
              reads=["fA", gb], writes=[gb], cost=1400)

            dma(qv[:], x_t[j], ["qv0", "qv1"], ["qv0", "qv1"], sem="DX", eng="A",
                cost=5000)

            for pair in range(NPAIR):
                E("T", lambda e, p_=pair, q_=seq: e.matmul(
                    psDn_v[:, 2 * p_:2 * p_ + 2], qfv[:, p_, :], KSp[:, q_, p_, :],
                    start=True, stop=True),
                  reads=[qfb, "KSp", "psAD"], writes=["psAD"])
            E("V", lambda e: e.tensor_scalar(rden[:, :], psDn_v, 1e-6, None, ALU.max),
              reads=["psAD", "rden"], writes=["rden"], cost=250)
            E("V", lambda e: e.reciprocal(rden[:, :], rden[:, :]),
              reads=["rden"], writes=["rden"], cost=300)
            for pair in range(NPAIR):
                pv = psA_v if pair % 2 == 0 else psQ_v
                pb = "psAD" if pair % 2 == 0 else "psQ"
                E("T", lambda e, p_=pair, v_=pv, b_=pb: e.matmul(
                    v_, qfv[:, p_, :], kvblk[:, p_, :], start=True, stop=True),
                  reads=[qfb, "kvblk", pb], writes=[pb])
                rb = rden[:, 2 * pair:2 * pair + 2].unsqueeze(2).broadcast_to([P, 2, HD])
                E("V", lambda e, p_=pair, rb_=rb, v_=pv, b_=pb: e.tensor_tensor(
                    u2am[:, p_ * P:(p_ + 1) * P].rearrange("p (h q) -> p h q", q=HD),
                    v_.rearrange("p (h q) -> p h q", q=HD), rb_, ALU.mult),
                  reads=[pb, "rden", "u20", "u21"], writes=["u20", "u21"], cost=400)
            transpose_group(lambda jj: u2am[:, jj * P:(jj + 1) * P], KC, xamv,
                            xamb, loc_fuse=(locv, locb), src_bufs=("u20", "u21"))
            for half in range(2):
                for kc in range(KC):
                    E("T", lambda e, k_=kc, h_=half: e.matmul(
                        psDown[:, h_ * 512:(h_ + 1) * 512], xamv[:, k_, :],
                        wB[:, k_, h_ * 512:(h_ + 1) * 512], start=(k_ == 0), stop=False),
                      reads=[xamb, "wB", f"psDn{half}"], writes=[f"psDn{half}"])
                E("T", lambda e, h_=half: e.matmul(
                    psDown[:, h_ * 512:(h_ + 1) * 512], ones_row[:, :],
                    bp_b[:, h_ * 512:(h_ + 1) * 512], start=False, stop=True),
                  reads=["ones_r", "bp_b", f"psDn{half}"], writes=[f"psDn{half}"])
                E("V", lambda e, h_=half: e.tensor_tensor(
                    u2am[:, h_ * 512:(h_ + 1) * 512], psDown[:, h_ * 512:(h_ + 1) * 512],
                    gv[:, h_ * 512:(h_ + 1) * 512], ALU.mult),
                  reads=[f"psDn{half}", gb, "u20" if half == 0 else "u21"],
                  writes=["u20" if half == 0 else "u21"])
                E("V", lambda e, h_=half: e.tensor_tensor(
                    qv[:, h_ * 512:(h_ + 1) * 512], qv[:, h_ * 512:(h_ + 1) * 512],
                    u2am[:, h_ * 512:(h_ + 1) * 512], ALU.add),
                  reads=[f"qv{half}", "u20" if half == 0 else "u21"],
                  writes=[f"qv{half}"])
                dma(out_t[j][:, half * 512:(half + 1) * 512], qv[:, half * 512:(half + 1) * 512],
                    [f"qv{half}"], ["out_dma"], sem="DO")

        if "p2" in _phases:
            em.begin_sched()
            p2_reloads()
            for jj in range(min(2, NT)):
                p2_prefetch(jj)
            em.end_sched()
            for j in range(NT):
                if j % 16 == 0:
                    em.begin_sched()
                p2_tile(j)
                if j % 16 == 15 or j == NT - 1:
                    em.end_sched()
        else:
            for j in range(NT):
                dma(out_t[j], qv[:], ["qv0", "qv1"], ["out_dma"], sem="DO")

        with nc.semaphore("sT") as sT, nc.semaphore("sV") as sV, nc.semaphore("sA") as sA, \
             nc.semaphore("sG") as sG, nc.semaphore("sDW") as sDW, nc.semaphore("sDX") as sDX, \
             nc.semaphore("sDQ") as sDQ, nc.semaphore("sDO") as sDO, \
             nc.semaphore("sDY") as sDY, nc.Block() as blk:
            sems = {"T": sT, "V": sV, "A": sA, "G": sG, "DW": sDW, "DX": sDX,
                    "DQ": sDQ, "DO": sDO, "DY": sDY}

            def runner(ename):
                def body(eng):
                    for waits, fn, sem, inc in em.ops[ename]:
                        for s, c in waits:
                            eng.wait_ge(sems[s], c)
                        if fn is not None:
                            fn(eng).then_inc(sems[sem], inc)
                    if ename == "S":
                        eng.wait_ge(sDO, em.cnt["DO"])
                return body

            blk.tensor(runner("T"))
            blk.vector(runner("V"))
            blk.scalar(runner("A"))
            blk.gpsimd(runner("G"))
            blk.sync(runner("S"))

    return nc


def kernel(L0_attn_x, L0_attn_Wq, L0_attn_Wk, L0_attn_Wv, L0_attn_Wp, L0_attn_bp,
           L0_attn_conv_w, L0_attn_conv_b, L0_ffn_w_gate, L0_ffn_w_up, L0_ffn_w_down):
    x = np.ascontiguousarray(np.asarray(L0_attn_x, dtype=np.float32))
    B, SL, _ = x.shape
    per = B // 8
    wqT = np.ascontiguousarray(np.asarray(L0_attn_Wq, np.float32).T)
    wkT = np.ascontiguousarray(np.asarray(L0_attn_Wk, np.float32).T)
    wvT = np.ascontiguousarray(np.asarray(L0_attn_Wv, np.float32).T)
    wpT = np.ascontiguousarray(np.asarray(L0_attn_Wp, np.float32).T)
    bp = np.ascontiguousarray(np.asarray(L0_attn_bp, np.float32).reshape(1, D))
    convw = np.ascontiguousarray(np.asarray(L0_attn_conv_w, np.float32))
    convb = np.ascontiguousarray(np.asarray(L0_attn_conv_b, np.float32).reshape(KC, P).T)
    wg = np.ascontiguousarray(np.asarray(L0_ffn_w_gate, np.float32))
    wu = np.ascontiguousarray(np.asarray(L0_ffn_w_up, np.float32))
    wd = np.ascontiguousarray(np.asarray(L0_ffn_w_down, np.float32))

    nc = build_program(seq_len=SL, n_seq=per)
    in_maps = []
    for c in range(8):
        shard = np.ascontiguousarray(x[c * per:(c + 1) * per].reshape(per * SL, D))
        in_maps.append({"x": shard, "wqT": wqT, "wkT": wkT, "wvT": wvT, "wpT": wpT,
                        "bp": bp, "convw": convw, "convb": convb,
                        "wg": wg, "wu": wu, "wd": wd})
    res = run_bass_kernel_spmd(nc, in_maps, core_ids=list(range(8)))
    out = np.concatenate([res.results[c]["out"].reshape(per, SL, D) for c in range(8)], axis=0)
    return np.ascontiguousarray(out.astype(np.float32))
